# version 18
# baseline (speedup 1.0000x reference)
import contextlib
import numpy as np
import ml_dtypes
import concourse.bass as bass
import concourse.mybir as mybir
from concourse.bass_utils import run_bass_kernel_spmd

F32 = mybir.dt.float32
BF16 = mybir.dt.bfloat16
AF = mybir.ActivationFunctionType
ALU = mybir.AluOpType
AX = mybir.AxisListType

NCORE = 8
SEQ = 2
SLEN = 2048
D = 1024
NCH = 8
TOK = SEQ * SLEN
NT_SEQ = SLEN // 128
LN_EPS = 1e-5
ATT_EPS = 1e-6
MOE_EPS = 1e-9
LR = 0.01
LAM2 = float(np.float32(np.float32(2.0 / D) * np.float32(LR)))
INV_D = 1.0 / D

ALL_PHASES = ("p0", "a0", "f0", "b0", "f1", "t1")


class S:

    def __init__(self, h):
        self.h = h
        self.v = 0

    def add(self, n=1):
        self.v += n
        return self.v


class Ctx:
    pass


class DPool:

    def __init__(self, nc, stack, name, n):
        self.sems = [S(stack.enter_context(nc.semaphore(f"{name}{i}")))
                     for i in range(n)]
        self.n = n
        self.issued = 0

    def issue(self, ins):
        s = self.sems[self.issued % self.n]
        ins.then_inc(s.h, 16)
        s.add(16)
        self.issued += 1
        return self.issued - 1

    def slot(self, j):
        return self.sems[j % self.n], 16 * (j // self.n + 1)

    def wait(self, eng, j):
        s, v = self.slot(j)
        eng.wait_ge(s.h, v)

    def wait_all(self, eng):
        for s in self.sems:
            if s.v:
                eng.wait_ge(s.h, s.v)


def pool(sems, nc, name, n):
    key = "pool_" + name
    if key not in sems:
        sems[key] = DPool(nc, sems["_stack"], name, n)
    p = sems[key]
    assert p.n == n, (name, p.n, n)
    return p


class Chain:

    def __init__(self, h):
        self.s = S(h)

    def inc(self, ins):
        ins.then_inc(self.s.h, 1)
        self.s.add(1)

    def wait(self, eng):
        if self.s.v:
            eng.wait_ge(self.s.h, self.s.v)


def host_consts():
    i = np.arange(128)
    r, c = i[:, None], i[None, :]
    return {
        "id_bf": np.eye(128, dtype=ml_dtypes.bfloat16),
        "id_f32": np.eye(128, dtype=np.float32),
        "ones_row": np.ones((1, 128), dtype=ml_dtypes.bfloat16),
        "ones_col": np.ones((128, 1), dtype=ml_dtypes.bfloat16),
        "mask_sl": (c < r).astype(np.float32),
        "mask_su": (c > r).astype(np.float32),
        "mask_iun": -(r <= c).astype(np.float32),
    }


def declare_io(nc, dbg):
    c = Ctx()
    dt_ = lambda name, shape, dt, kind: nc.dram_tensor(name, shape, dt, kind=kind).ap()
    c.x = dt_("x", [SEQ, SLEN, D], F32, "ExternalInput")
    c.w_qkv = dt_("w_qkv", [3 * D, D], F32, "ExternalInput")
    c.w_out = dt_("w_out", [D, D], F32, "ExternalInput")
    c.b_out = dt_("b_out", [1, D], F32, "ExternalInput")
    c.gate_w = dt_("gate_w", [D, 8], F32, "ExternalInput")
    c.m0w1 = dt_("m0w1", [8, D, 512], F32, "ExternalInput")
    c.m0b1 = dt_("m0b1", [8, 512], F32, "ExternalInput")
    c.m0w2 = dt_("m0w2", [8, 512, D], F32, "ExternalInput")
    c.m0b2 = dt_("m0b2", [8, D], F32, "ExternalInput")
    c.thk = dt_("thk", [D, D], F32, "ExternalInput")
    c.thv = dt_("thv", [D, D], F32, "ExternalInput")
    c.thq = dt_("thq", [D, D], F32, "ExternalInput")
    c.w0 = dt_("w0", [D, D], F32, "ExternalInput")
    c.b0 = dt_("b0", [1, D], F32, "ExternalInput")
    c.wg = dt_("wg", [D, 8], F32, "ExternalInput")
    c.m1w1 = dt_("m1w1", [8, D, 4096], F32, "ExternalInput")
    c.m1b1 = dt_("m1b1", [8, 4096], F32, "ExternalInput")
    c.m1w2 = dt_("m1w2", [8, 4096, D], F32, "ExternalInput")
    c.m1b2 = dt_("m1b2", [8, D], F32, "ExternalInput")
    c.id_bf = dt_("id_bf", [128, 128], BF16, "ExternalInput")
    c.id_f32 = dt_("id_f32", [128, 128], F32, "ExternalInput")
    c.ones_row = dt_("ones_row", [1, 128], BF16, "ExternalInput")
    c.ones_col = dt_("ones_col", [128, 1], BF16, "ExternalInput")
    c.mask_sl = dt_("mask_sl", [128, 128], F32, "ExternalInput")
    c.mask_su = dt_("mask_su", [128, 128], F32, "ExternalInput")
    c.mask_iun = dt_("mask_iun", [128, 128], F32, "ExternalInput")
    c.out = dt_("out", [SEQ, SLEN, D], F32, "ExternalOutput")

    def scr(name, shape, dt):
        kind = "ExternalOutput" if (name in dbg) else "Internal"
        return dt_(name, shape, dt, kind)

    c.wq_t = scr("wq_t", [128, NCH * D], BF16)
    c.wk_t = scr("wk_t", [128, NCH * D], BF16)
    c.wv_t = scr("wv_t", [128, NCH * D], BF16)
    c.wout_t = scr("wout_t", [128, NCH * D], BF16)
    c.thk_t = scr("thk_t", [128, NCH * D], BF16)
    c.thq_t = scr("thq_t", [128, NCH * D], BF16)
    c.thv_tn = scr("thv_tn", [128, NCH * D], BF16)
    c.w0_t = scr("w0_t", [128, NCH * D], F32)
    c.m0w1_b = scr("m0w1_b", [8, 128, 8 * 512], BF16)
    c.m0w2_b = scr("m0w2_b", [8, 128, 4 * 1024], BF16)
    c.m1w1_b = scr("m1w1_b", [8, 128, 32 * 1024], BF16)
    c.m1w2_b = scr("m1w2_b", [8, 128, 32 * 1024], BF16)
    c.xn_d = scr("xn_d", [128, NCH * TOK], BF16)
    c.xn1_d = scr("xn1_d", [128, NCH * TOK], BF16)
    c.x1_d = scr("x1_d", [TOK, D], F32)
    c.sg0_d = scr("sg0_d", [TOK, D], F32)
    c.sg1_d = scr("sg1_d", [TOK, D], F32)
    c.kv_d = scr("kv_d", [SEQ, 128, NCH * D], BF16)
    return c


def barrier(nc, block, ph):
    engines = ("sync", "tensor", "vector", "scalar")
    ph.add(len(engines))
    target = ph.v
    for name in engines:
        sec = getattr(block, name)

        @sec
        def _(eng, target=target):
            eng.nop().then_inc(ph.h, 1)
            eng.wait_ge(ph.h, target)


def emit_p0(nc, block, c, sems, stack):
    cast, tr, ev, st = (sems[k] for k in
                        ("p0_cast", "p0_tr", "p0_ev", "p0_st"))
    ldp = DPool(nc, sems["_stack"], "p0ld", 2)

    wcast0, wcast1 = sems["wcast0"], sems["wcast1"]

    @block.gpsimd
    def _(g):
        for e in range(8):
            g.dma_start(out=c.m0w1_b[e].rearrange("p (c f) -> p c f", c=8),
                        in_=c.m0w1[e].rearrange("(c p) f -> p c f", p=128)).then_inc(wcast0.h, 16)
            g.dma_start(out=c.m0w2_b[e].rearrange("p (c d) -> p c d", c=4),
                        in_=c.m0w2[e].rearrange("(c p) d -> p c d", p=128)).then_inc(wcast0.h, 16)
            wcast0.add(32)
        for e in range(8):
            for ch in range(8):
                g.dma_start(
                    out=c.m1w1_b[e].rearrange("p (fs blk) -> p fs blk", fs=32)
                        [:, :, ch * 128:(ch + 1) * 128],
                    in_=c.m1w1[e][ch * 128:(ch + 1) * 128, :]
                        .rearrange("p (fs fi) -> p fs fi", fi=128),
                ).then_inc(wcast1.h, 16)
            g.dma_start(out=c.m1w2_b[e].rearrange("p (c d) -> p c d", c=32),
                        in_=c.m1w2[e].rearrange("(c p) d -> p c d", p=128)).then_inc(wcast1.h, 16)
            wcast1.add(16 * 9)

    nat_f = stack.enter_context(nc.sbuf_tensor("p0_nat_f", [128, 2 * D], F32))
    nat_b = stack.enter_context(nc.sbuf_tensor("p0_nat_b", [128, 2 * D], BF16))
    stage_b = stack.enter_context(nc.sbuf_tensor("p0_stage_b", [128, NCH * D], BF16))
    stage_f = stack.enter_context(nc.sbuf_tensor("p0_stage_f", [128, NCH * D], F32))
    ccm1 = stack.enter_context(nc.sbuf_tensor("p0_ccm1", [128, 1], F32))
    ps_t = stack.enter_context(nc.psum_tensor("p0_ps_b", [128, 1024], BF16))
    ps_f = stack.enter_context(nc.psum_tensor("p0_ps_f", [128, 1024], F32))

    cns = S(sems["_stack"].enter_context(nc.semaphore("p0_cns")))

    @block.vector
    def _(v):
        v.memset(ccm1[:, 0:1], -1.0).then_inc(cns.h, 1)
    cns.add(1)

    jobs = [
        (c.w_qkv[0 * D:1 * D, :], c.wq_t, False, False),
        (c.w_qkv[1 * D:2 * D, :], c.wk_t, False, False),
        (c.w_qkv[2 * D:3 * D, :], c.wv_t, False, False),
        (c.w_out, c.wout_t, False, False),
        (c.thk, c.thk_t, False, False),
        (c.thv, c.thv_tn, True, False),
        (c.thq, c.thq_t, False, False),
        (c.w0, c.w0_t, False, True),
    ]
    consumed = []
    for j, (src, dst, neg, isf32) in enumerate(jobs):
        st_before = st.v
        for ec in range(8):
            step = j * 8 + ec
            buf = ec % 2

            @block.sync
            def _(s, src=src, ec=ec, buf=buf, step=step):
                if step >= 2:
                    sem, val = consumed[step - 2]
                    s.wait_ge(sem.h, val)
                ldp.issue(s.dma_start(out=nat_f[:, buf * D:(buf + 1) * D],
                                      in_=src[ec * 128:(ec + 1) * 128, :]))
            ld_slot = ldp.slot(step)

            if not isf32:
                need_tr_prev = tr.v - 1 if step >= 2 else 0

                @block.scalar
                def _(sc, buf=buf, neg=neg, ld_slot=ld_slot, need_tr_prev=need_tr_prev):
                    sc.wait_ge(ld_slot[0].h, ld_slot[1])
                    if need_tr_prev:
                        sc.wait_ge(tr.h, need_tr_prev)
                    sc.wait_ge(cns.h, 1)
                    if neg:
                        sc.activation(nat_b[:, buf * D:(buf + 1) * D],
                                      nat_f[:, buf * D:(buf + 1) * D],
                                      AF.Copy, scale=ccm1[:, 0:1]).then_inc(cast.h, 1)
                    else:
                        sc.activation(nat_b[:, buf * D:(buf + 1) * D],
                                      nat_f[:, buf * D:(buf + 1) * D],
                                      AF.Copy).then_inc(cast.h, 1)
                cast.add(1)
                consumed.append((cast, cast.v))
                need_cast = cast.v
                need_ev_prev = ev.v

                @block.tensor
                def _(te, need_cast=need_cast, need_ev_prev=need_ev_prev, buf=buf):
                    te.wait_ge(cast.h, need_cast)
                    if need_ev_prev:
                        te.wait_ge(ev.h, need_ev_prev)
                    for dc in range(8):
                        ins = te.transpose(ps_t[:, dc * 128:(dc + 1) * 128],
                                           nat_b[:, buf * D + dc * 128:buf * D + (dc + 1) * 128],
                                           c.g.id_bf[:, :])
                        if dc == 7:
                            ins.then_inc(tr.h, 1)
                tr.add(1)
                need_tr = tr.v
                need_st_prev = st_before

                @block.vector
                def _(v, need_tr=need_tr, ec=ec, need_st_prev=need_st_prev):
                    v.wait_ge(tr.h, need_tr)
                    if need_st_prev:
                        v.wait_ge(st.h, need_st_prev)
                    for dc in range(8):
                        ins = v.tensor_copy(stage_b[:, dc * D + ec * 128:dc * D + (ec + 1) * 128],
                                            ps_t[:, dc * 128:(dc + 1) * 128])
                        if dc == 7:
                            ins.then_inc(ev.h, 1)
                ev.add(1)
            else:
                need_ev_prev = ev.v

                @block.tensor
                def _(te, ld_slot=ld_slot, need_ev_prev=need_ev_prev, buf=buf):
                    te.wait_ge(ld_slot[0].h, ld_slot[1])
                    if need_ev_prev:
                        te.wait_ge(ev.h, need_ev_prev)
                    for dc in range(8):
                        ins = te.transpose(ps_f[:, dc * 128:(dc + 1) * 128],
                                           nat_f[:, buf * D + dc * 128:buf * D + (dc + 1) * 128],
                                           c.g.id_f32[:, :])
                        if dc == 7:
                            ins.then_inc(tr.h, 1)
                tr.add(1)
                consumed.append((tr, tr.v))
                need_tr = tr.v
                need_st_prev = st_before

                @block.vector
                def _(v, need_tr=need_tr, ec=ec, need_st_prev=need_st_prev):
                    v.wait_ge(tr.h, need_tr)
                    if need_st_prev:
                        v.wait_ge(st.h, need_st_prev)
                    for dc in range(8):
                        ins = v.tensor_copy(stage_f[:, dc * D + ec * 128:dc * D + (ec + 1) * 128],
                                            ps_f[:, dc * 128:(dc + 1) * 128])
                        if dc == 7:
                            ins.then_inc(ev.h, 1)
                ev.add(1)

        need_ev = ev.v

        @block.sync
        def _(s, dst=dst, isf32=isf32, need_ev=need_ev):
            s.wait_ge(ev.h, need_ev)
            s.dma_start(out=dst, in_=stage_f[:] if isf32 else stage_b[:]).then_inc(st.h, 16)
        st.add(16)

    @block.sync
    def _(s):
        s.wait_ge(st.h, st.v)


def emit_ln(block, ch, bufs, xt_ap, out_ap, wait_x, s_sq, s_col, s_rstd, s_out, cc_m05,
            apply_wait=None):
    sqdump, cols = bufs
    wsem, wval = wait_x

    vch, sch = ch["v"], ch["s"]

    @block.scalar
    def _(sc, wval=wval):
        sc.wait_ge(wsem.h, wval)
        sc.activation(sqdump[:], xt_ap, AF.Square, accum_out=cols[:, 0:1]).then_inc(s_sq.h, 1)
    s_sq.add(1)
    need_sq = s_sq.v

    @block.vector
    def _(v, wval=wval, need_sq=need_sq):
        v.wait_ge(wsem.h, wval)
        vch.inc(v.reduce_sum(cols[:, 1:2], xt_ap, axis=AX.X))
        v.wait_ge(s_sq.h, need_sq)
        vch.inc(v.tensor_scalar(cols[:, 5:6], cols[:, 0:1], INV_D, None, ALU.mult))
        vch.wait(v)
        vch.inc(v.tensor_scalar(cols[:, 3:4], cols[:, 1:2], INV_D, None, ALU.mult))
        vch.wait(v)
        vch.inc(v.tensor_tensor(cols[:, 4:5], cols[:, 3:4], cols[:, 3:4], ALU.mult))
        vch.wait(v)
        v.scalar_tensor_tensor(cols[:, 6:7], cols[:, 5:6], LN_EPS, cols[:, 4:5],
                               ALU.add, ALU.subtract).then_inc(s_col.h, 1)
    s_col.add(1)
    need_col = s_col.v

    @block.scalar
    def _(sc, need_col=need_col):
        sc.wait_ge(s_col.h, need_col)
        sch.inc(sc.activation(cols[:, 7:8], cols[:, 6:7], AF.Ln))
        sch.wait(sc)
        sc.activation(cols[:, 7:8], cols[:, 7:8], AF.Exp, scale=cc_m05[:, 0:1]).then_inc(s_rstd.h, 1)
    s_rstd.add(1)
    need_rstd = s_rstd.v

    @block.vector
    def _(v, need_rstd=need_rstd, apply_wait=apply_wait):
        v.wait_ge(s_rstd.h, need_rstd)
        if apply_wait is not None and apply_wait[1] > 0:
            v.wait_ge(apply_wait[0].h, apply_wait[1])
        v.tensor_scalar(out_ap, xt_ap, cols[:, 3:4], cols[:, 7:8],
                        ALU.subtract, ALU.mult).then_inc(s_out.h, 1)
    s_out.add(1)


def emit_a0(nc, block, c, sems, stack):
    wld = sems["a_wld"]; sq = sems["a_sq"]; col = sems["a_col"]
    rstd = sems["a_rstd"]; xn = sems["a_xn"]; tr = sems["a_tr"]; xnf = sems["a_xnf"]
    mm = sems["a_mm"]; kev = sems["a_kev"]; vev = sems["a_vev"]
    kvev = sems["a_kvev"]; kvst = sems["a_kvst"]; p0st = sems["p0_st"]
    xldp = pool(sems, nc, "a_xld4", 4)
    xnstp = pool(sems, nc, "a_xnst", 2)

    wk_sb = stack.enter_context(nc.sbuf_tensor("a_wk", [128, NCH * D], BF16))
    wv_sb = stack.enter_context(nc.sbuf_tensor("a_wv", [128, NCH * D], BF16))
    k_sb = stack.enter_context(nc.sbuf_tensor("a_k", [128, NT_SEQ * D], BF16))
    v_sb = stack.enter_context(nc.sbuf_tensor("a_v", [128, NT_SEQ * D], BF16))
    xt = stack.enter_context(nc.sbuf_tensor("a_xt", [128, 4 * D], F32))
    sqd = stack.enter_context(nc.sbuf_tensor("a_sqd", [128, D], BF16))
    cols = stack.enter_context(nc.sbuf_tensor("a_cols", [128, 8], F32))
    xn_tm = stack.enter_context(nc.sbuf_tensor("a_xn_tm", [128, 2 * D], BF16))
    xnf_sb = stack.enter_context(nc.sbuf_tensor("a_xnf", [128, 2 * D], BF16))
    kvs = stack.enter_context(nc.sbuf_tensor("a_kvs", [128, NCH * D], BF16))
    cc = stack.enter_context(nc.sbuf_tensor("a_cc", [128, 1], F32))
    ksum_bf = c.g.ksum_bf

    ps_tr = stack.enter_context(nc.psum_tensor("a_ps_tr", [128, 1024], BF16))
    ps_k = stack.enter_context(nc.psum_tensor("a_ps_k", [128, 512], F32))
    ps_v = stack.enter_context(nc.psum_tensor("a_ps_v", [128, 512], F32))
    ps_ks = stack.enter_context(nc.psum_tensor("a_ps_ks", [128, 8], F32))
    ps_kv = stack.enter_context(nc.psum_tensor("a_ps_kv", [128, 2 * 1024], F32))

    cns = S(sems["_stack"].enter_context(nc.semaphore("a_cns")))

    @block.vector
    def _(v):
        v.memset(cc[:, 0:1], -0.5)
        v.memset(cols[:, 2:3], 0.0).then_inc(cns.h, 1)
    cns.add(1)

    @block.scalar
    def _(sc):
        sc.wait_ge(cns.h, cns.v)
        sc.wait_ge(p0st.h, 48)
        sc.dma_start(out=wk_sb[:], in_=c.wk_t).then_inc(wld.h, 16)
        sc.dma_start(out=wv_sb[:], in_=c.wv_t).then_inc(wld.h, 16)
    wld.add(32)
    need_wld = wld.v

    proj_mm_done = {}
    for s in range(SEQ):
        if s == 1:
            mm_after_kv = mm.v
            kvst_after = kvst.v

            @block.vector
            def _(v, mm_after_kv=mm_after_kv):
                v.wait_ge(mm.h, mm_after_kv)

            @block.scalar
            def _(sc, mm_after_kv=mm_after_kv, kvst_after=kvst_after):
                sc.wait_ge(mm.h, mm_after_kv)
                sc.wait_ge(kvst.h, kvst_after)
        for i in range(NT_SEQ):
            gi = s * NT_SEQ + i
            buf = gi % 2
            xbuf = gi % 4
            xt_ap = xt[:, xbuf * D:(xbuf + 1) * D]
            free_val = xn.v - 3 if gi >= 4 else 0

            @block.sync
            def _(sy, s=s, i=i, xt_ap=xt_ap, free_val=free_val):
                if free_val:
                    sy.wait_ge(xn.h, free_val)
                xldp.issue(sy.dma_start(out=xt_ap, in_=c.x[s, i * 128:(i + 1) * 128, :]))

            xn_tm_ap = xn_tm[:, buf * D:(buf + 1) * D]
            xsl = xldp.slot(xldp.issued - 1)
            emit_ln(block, sems["ch"], (sqd, cols), xt_ap, xn_tm_ap, (xsl[0], xsl[1]),
                    sq, col, rstd, xn, cc,
                    apply_wait=(tr, tr.v - 1 if gi >= 2 else 0))
            need_xn = xn.v

            need_xnf_prev = xnf.v

            @block.tensor
            def _(te, xn_tm_ap=xn_tm_ap, need_xn=need_xn, need_xnf_prev=need_xnf_prev):
                te.wait_ge(xn.h, need_xn)
                if need_xnf_prev:
                    te.wait_ge(xnf.h, need_xnf_prev)
                for dc in range(8):
                    ins = te.transpose(ps_tr[:, dc * 128:(dc + 1) * 128],
                                       xn_tm_ap[:, dc * 128:(dc + 1) * 128], c.g.id_bf[:, :])
                    if dc == 7:
                        ins.then_inc(tr.h, 1)
            tr.add(1)
            need_tr = tr.v
            st_slot = xnstp.slot(xnstp.issued - 2) if gi >= 2 else None
            need_projmm = proj_mm_done.get(gi - 2, 0)

            @block.vector
            def _(v, need_tr=need_tr, buf=buf, st_slot=st_slot, need_projmm=need_projmm):
                v.wait_ge(tr.h, need_tr)
                if st_slot is not None:
                    v.wait_ge(st_slot[0].h, st_slot[1])
                if need_projmm:
                    v.wait_ge(mm.h, need_projmm)
                v.tensor_copy(xnf_sb[:, buf * D:(buf + 1) * D], ps_tr[:]).then_inc(xnf.h, 1)
            xnf.add(1)
            need_xnf = xnf.v

            @block.sync
            def _(sy, s=s, i=i, buf=buf, need_xnf=need_xnf):
                sy.wait_ge(xnf.h, need_xnf)
                xnstp.issue(sy.dma_start(
                    out=c.xn_d.rearrange("p (c t) -> p c t", c=NCH)[:, :, s * SLEN + i * 128:
                                                                   s * SLEN + (i + 1) * 128],
                    in_=xnf_sb[:, buf * D:(buf + 1) * D].rearrange("p (c t) -> p c t", t=128),
                ))

            xnf_ap = xnf_sb[:, buf * D:(buf + 1) * D]
            for half in range(2):
                need_kev_prev = kev.v
                need_vev_prev = vev.v

                @block.tensor
                def _(te, xnf_ap=xnf_ap, half=half, need_xnf=need_xnf,
                      need_kev_prev=need_kev_prev, need_vev_prev=need_vev_prev):
                    te.wait_ge(xnf.h, need_xnf)
                    te.wait_ge(wld.h, need_wld)
                    if need_kev_prev:
                        te.wait_ge(kev.h, need_kev_prev)
                    for dc in range(8):
                        ins = te.matmul(ps_k[:, :],
                                        xnf_ap[:, dc * 128:(dc + 1) * 128],
                                        wk_sb[:, dc * D + half * 512:dc * D + half * 512 + 512],
                                        start=(dc == 0), stop=(dc == 7))
                        if dc == 7:
                            ins.then_inc(mm.h, 1)
                    if need_vev_prev:
                        te.wait_ge(vev.h, need_vev_prev)
                    for dc in range(8):
                        ins = te.matmul(ps_v[:, :],
                                        xnf_ap[:, dc * 128:(dc + 1) * 128],
                                        wv_sb[:, dc * D + half * 512:dc * D + half * 512 + 512],
                                        start=(dc == 0), stop=(dc == 7))
                        if dc == 7:
                            ins.then_inc(mm.h, 1)
                mm.add(2)
                need_mm = mm.v

                @block.vector
                def _(v, i=i, half=half, need_mm=need_mm):
                    v.wait_ge(mm.h, need_mm - 1)
                    v.tensor_scalar(k_sb[:, i * D + half * 512:i * D + half * 512 + 512],
                                    ps_k[:, :], 0.0, ATT_EPS,
                                    ALU.max, ALU.add).then_inc(kev.h, 1)
                kev.add(1)

                @block.scalar
                def _(sc, i=i, half=half, need_mm=need_mm):
                    sc.wait_ge(mm.h, need_mm)
                    sc.activation(v_sb[:, i * D + half * 512:i * D + half * 512 + 512],
                                  ps_v[:, :], AF.Copy).then_inc(vev.h, 1)
                vev.add(1)
            proj_mm_done[gi] = mm.v

        need_kev_all = kev.v
        need_vev_all = vev.v

        @block.tensor
        def _(te, need_kev_all=need_kev_all):
            te.wait_ge(kev.h, need_kev_all)
            for ac in range(8):
                for i in range(NT_SEQ):
                    ins = te.matmul(ps_ks[:, ac:ac + 1],
                                    k_sb[:, i * D + ac * 128:i * D + (ac + 1) * 128],
                                    c.g.ones_col[:, :],
                                    start=(i == 0), stop=(i == NT_SEQ - 1))
                    if ac == 7 and i == NT_SEQ - 1:
                        ins.then_inc(mm.h, 1)
        mm.add(1)
        need_mm_ks = mm.v

        @block.vector
        def _(v, s=s, need_mm_ks=need_mm_ks):
            v.wait_ge(mm.h, need_mm_ks)
            v.tensor_copy(ksum_bf[:, s * 8:(s + 1) * 8], ps_ks[:, :]).then_inc(kvev.h, 1)
        kvev.add(1)

        for p in range(4):
            need_kvev = kvev.v

            @block.tensor
            def _(te, p=p, need_kvev=need_kvev, need_vev_all=need_vev_all):
                te.wait_ge(vev.h, need_vev_all)
                te.wait_ge(kvev.h, need_kvev)
                for jj in range(2):
                    ac = p * 2 + jj
                    for bh in range(2):
                        for i in range(NT_SEQ):
                            ins = te.matmul(
                                ps_kv[:, jj * 1024 + bh * 512:jj * 1024 + bh * 512 + 512],
                                k_sb[:, i * D + ac * 128:i * D + (ac + 1) * 128],
                                v_sb[:, i * D + bh * 512:i * D + bh * 512 + 512],
                                start=(i == 0), stop=(i == NT_SEQ - 1))
                            if jj == 1 and bh == 1 and i == NT_SEQ - 1:
                                ins.then_inc(mm.h, 1)
            mm.add(1)
            need_mm_kv = mm.v

            @block.scalar
            def _(sc, p=p, need_mm_kv=need_mm_kv):
                sc.wait_ge(mm.h, need_mm_kv)
                for jj in range(2):
                    ins = sc.activation(kvs[:, (p * 2 + jj) * 1024:(p * 2 + jj + 1) * 1024],
                                        ps_kv[:, jj * 1024:(jj + 1) * 1024], AF.Copy)
                    if jj == 1:
                        ins.then_inc(kvev.h, 1)
            kvev.add(1)

        need_kvev_st = kvev.v

        @block.sync
        def _(sy, s=s, need_kvev_st=need_kvev_st):
            sy.wait_ge(kvev.h, need_kvev_st)
            sy.dma_start(out=c.kv_d[s], in_=kvs[:]).then_inc(kvst.h, 16)
        kvst.add(16)

    @block.sync
    def _(sy):
        sy.wait_ge(kvst.h, kvst.v)
        xnstp.wait_all(sy)


def emit_f0(nc, block, c, sems, stack):
    mm = sems["f_mm"]; gps = sems["f_gps"]; gs2 = sems["f_gs2"]
    wtv = sems["f_wtv"]; b2s = sems["f_b2"]; hev = sems["f_hev"]; yac = sems["f_yac"]
    sil = sems["f_sil"]; wcast0 = sems["wcast0"]
    vch, sch = sems["ch"]["v"], sems["ch"]["s"]
    xnp = pool(sems, nc, "f_xn", 2)
    w1p = pool(sems, nc, "f_w1", 4)
    w2p = pool(sems, nc, "f_w2", 4)
    sgp = pool(sems, nc, "f_sg", 2)
    smp = pool(sems, nc, "f_sm", 1)
    smb = pool(sems, nc, "f_smb", 1)

    gw_sb = stack.enter_context(nc.sbuf_tensor("f_gw", [128, 64], BF16))
    b1n = stack.enter_context(nc.sbuf_tensor("f_b1n", [8, 512], F32))
    b1c = stack.enter_context(nc.sbuf_tensor("f_b1c", [128, 32], F32))
    b2n = stack.enter_context(nc.sbuf_tensor("f_b2n", [8, 1024], BF16))
    xnst = stack.enter_context(nc.sbuf_tensor("f_xnst", [128, 2 * 8 * 1024], BF16))
    w1e = stack.enter_context(nc.sbuf_tensor("f_w1e", [128, 2 * 4096], BF16))
    w2e = stack.enter_context(nc.sbuf_tensor("f_w2e", [128, 2 * 4096], BF16))
    h_bf = stack.enter_context(nc.sbuf_tensor("f_h", [128, 4 * 1024], BF16))
    g_acc = stack.enter_context(nc.sbuf_tensor("f_gacc", [128, 8 * 1024], F32))
    gsb = stack.enter_context(nc.sbuf_tensor("f_gsb", [128, 8], F32))
    gates = stack.enter_context(nc.sbuf_tensor("f_gates", [128, 8], F32))
    selt = stack.enter_context(nc.sbuf_tensor("f_selt", [128, 8], F32))
    gcols = stack.enter_context(nc.sbuf_tensor("f_gcols", [128, 8], F32))
    w_st = stack.enter_context(nc.sbuf_tensor("f_wst", [128, 64], F32))
    wbf = stack.enter_context(nc.sbuf_tensor("f_wbf", [128, 8], BF16))
    wT_st = stack.enter_context(nc.sbuf_tensor("f_wTst", [8, 1024], BF16))
    sg_t = stack.enter_context(nc.sbuf_tensor("f_sgt", [128, 2 * 1024], F32))

    ps = stack.enter_context(nc.psum_tensor("f_ps", [128, 3584], F32))
    ps_bf = stack.enter_context(nc.psum_tensor("f_psbf", [128, 1024], BF16))

    @block.gpsimd
    def _(g):
        g.wait_ge(sems["ph"].h, sems["ph"].v)
        g.wait_ge(wcast0.h, wcast0.v)
        smp.issue(g.dma_start(out=gw_sb[:].rearrange("p (c e) -> p c e", c=8),
                              in_=c.gate_w.rearrange("(c p) e -> p c e", p=128)))
        smp.issue(g.dma_start(out=b2n[:], in_=c.m0b2))
    j_b1 = None

    @block.sync
    def _(sy):
        nonlocal j_b1
        j_b1 = smb.issue(sy.dma_start(out=b1n[:], in_=c.m0b1))
    need_ev_b1 = []

    @block.tensor
    def _(te):
        smb.wait(te, j_b1)
        for fs in range(4):
            te.transpose(ps[:, 3072 + fs * 8:3072 + (fs + 1) * 8],
                         b1n[0:8, fs * 128:(fs + 1) * 128],
                         c.g.id_f32[0:8, 0:8]).then_inc(mm.h, 1)
            mm.add(1)
    need_mm_b1 = mm.v

    @block.vector
    def _(v):
        v.wait_ge(mm.h, need_mm_b1)
        for fs in range(4):
            ins = v.tensor_copy(b1c[:, fs * 8:(fs + 1) * 8],
                                ps[:, 3072 + fs * 8:3072 + (fs + 1) * 8])
            vch.inc(ins)

    expert_mm_done = {}
    st_consume_mm = {}
    prev_sil_all = 0
    prev_yac_all = 0
    for st in range(4):
        @block.sync
        def _(sy, st=st):
            if st >= 2:
                sy.wait_ge(mm.h, st_consume_mm[st - 2])
            xnp.issue(sy.dma_start(
                out=xnst[:, (st % 2) * 8192:(st % 2 + 1) * 8192]
                    .rearrange("p (c t) -> p c t", c=8),
                in_=c.xn_d.rearrange("p (c t) -> p c t", c=NCH)
                    [:, :, st * 1024:(st + 1) * 1024]))
        xn_slot = xnp.slot(xnp.issued - 1)
        xb = (st % 2) * 8192

        for t in range(8):
            gslot = (t % 2) * 512
            need_gps = gps.v - 1 if (st * 8 + t) >= 2 else 0
            need_hev_prev = hev.v

            @block.tensor
            def _(te, t=t, gslot=gslot, need_gps=need_gps, xn_slot=xn_slot,
                  xb=xb, need_hev_prev=need_hev_prev):
                te.wait_ge(xn_slot[0].h, xn_slot[1])
                smp.wait_all(te)
                smb.wait_all(te)
                if need_gps:
                    te.wait_ge(gps.h, need_gps)
                if need_hev_prev:
                    te.wait_ge(hev.h, need_hev_prev)
                for dc in range(8):
                    ins = te.matmul(ps[:, gslot:gslot + 8],
                                    xnst[:, xb + dc * 1024 + t * 128:xb + dc * 1024 + (t + 1) * 128],
                                    gw_sb[:, dc * 8:(dc + 1) * 8],
                                    start=(dc == 0), stop=(dc == 7))
                    if dc == 7:
                        ins.then_inc(mm.h, 1)
            mm.add(1)
            need_mm_g = mm.v

            @block.vector
            def _(v, gslot=gslot, need_mm_g=need_mm_g):
                v.wait_ge(mm.h, need_mm_g)
                vch.inc(v.reduce_max(gcols[:, 0:1], ps[:, gslot:gslot + 8], axis=AX.X))
                vch.wait(v)
                v.tensor_scalar(gsb[:], ps[:, gslot:gslot + 8], gcols[:, 0:1], None,
                                ALU.subtract).then_inc(gps.h, 1)
            gps.add(1)
            need_gps2 = gps.v

            @block.scalar
            def _(sc, need_gps2=need_gps2):
                sc.wait_ge(gps.h, need_gps2)
                sc.activation(gsb[:], gsb[:], AF.Exp).then_inc(gs2.h, 1)
            gs2.add(1)
            need_gs2 = gs2.v

            @block.vector
            def _(v, t=t, need_gs2=need_gs2):
                v.wait_ge(gs2.h, need_gs2)
                vch.inc(v.reduce_sum(gcols[:, 1:2], gsb[:], axis=AX.X))
                vch.wait(v)
                vch.inc(v.reciprocal(gcols[:, 2:3], gcols[:, 1:2]))
                vch.wait(v)
                vch.inc(v.tensor_scalar(gates[:], gsb[:], gcols[:, 2:3], None, ALU.mult))
                vch.wait(v)
                vch.inc(v.reduce_max(gcols[:, 3:4], gates[:], axis=AX.X))
                vch.wait(v)
                vch.inc(v.tensor_scalar(selt[:], gates[:], gcols[:, 3:4], None, ALU.is_ge))
                vch.wait(v)
                vch.inc(v.scalar_tensor_tensor(selt[:], selt[:], -1e30, gates[:],
                                               ALU.mult, ALU.add))
                vch.wait(v)
                vch.inc(v.reduce_max(gcols[:, 4:5], selt[:], axis=AX.X))
                vch.wait(v)
                vch.inc(v.tensor_scalar(selt[:], gates[:], gcols[:, 4:5], None, ALU.is_ge))
                vch.wait(v)
                vch.inc(v.tensor_tensor(gcols[:, 5:6], gcols[:, 3:4], gcols[:, 4:5], ALU.add))
                vch.wait(v)
                vch.inc(v.tensor_scalar(w_st[:, t * 8:(t + 1) * 8], selt[:], gcols[:, 5:6],
                                        None, ALU.mult))
                vch.wait(v)
                v.tensor_copy(wbf[:], w_st[:, t * 8:(t + 1) * 8]).then_inc(wtv.h, 1)
            wtv.add(1)
            need_wtv = wtv.v
            b2slot = 1024 + (t % 2) * 1024
            need_b2prev = b2s.v - 1 if (st * 8 + t) >= 2 else 0
            need_yac_prev = prev_yac_all

            @block.tensor
            def _(te, t=t, need_wtv=need_wtv, b2slot=b2slot, need_b2prev=need_b2prev,
                  need_yac_prev=need_yac_prev):
                te.wait_ge(wtv.h, need_wtv)
                te.transpose(ps_bf[0:8, 0:128], wbf[:], c.g.id_bf[:, :]).then_inc(mm.h, 1)
            mm.add(1)
            need_mm_wt = mm.v

            @block.vector
            def _(v, t=t, need_mm_wt=need_mm_wt):
                v.wait_ge(mm.h, need_mm_wt)
                v.tensor_copy(wT_st[0:8, t * 128:(t + 1) * 128],
                              ps_bf[0:8, 0:128]).then_inc(wtv.h, 1)
            wtv.add(1)
            need_wtv2 = wtv.v

            @block.tensor
            def _(te, t=t, need_wtv2=need_wtv2, b2slot=b2slot, need_b2prev=need_b2prev,
                  need_yac_prev=need_yac_prev):
                te.wait_ge(wtv.h, need_wtv2)
                if need_b2prev:
                    te.wait_ge(b2s.h, need_b2prev)
                if need_yac_prev:
                    te.wait_ge(yac.h, need_yac_prev)
                for dh in range(2):
                    ins = te.matmul(ps[:, b2slot + dh * 512:b2slot + dh * 512 + 512],
                                    wT_st[0:8, t * 128:(t + 1) * 128],
                                    b2n[0:8, dh * 512:dh * 512 + 512],
                                    start=True, stop=True)
                    if dh == 1:
                        ins.then_inc(mm.h, 1)
            mm.add(1)
            need_mm_b2 = mm.v
            need_sil_prev = prev_sil_all

            @block.scalar
            def _(sc, st=st, t=t, b2slot=b2slot, need_mm_b2=need_mm_b2,
                  need_sil_prev=need_sil_prev):
                sc.wait_ge(mm.h, need_mm_b2)
                if need_sil_prev:
                    sc.wait_ge(sil.h, need_sil_prev)
                sc.activation(g_acc[:, t * 1024:(t + 1) * 1024],
                              ps[:, b2slot:b2slot + 1024], AF.Copy).then_inc(b2s.h, 1)
            b2s.add(1)
        b2_all = b2s.v
        need_yacST = prev_yac_all

        for e in range(8):
            gidx = st * 8 + e
            free_mm = expert_mm_done.get(gidx - 2, 0)

            @block.scalar
            def _(sc, e=e, free_mm=free_mm):
                if free_mm:
                    sc.wait_ge(mm.h, free_mm)
                sc.wait_ge(wcast0.h, wcast0.v)
                w1p.issue(sc.dma_start(out=w1e[:, (e % 2) * 4096:(e % 2 + 1) * 4096],
                                       in_=c.m0w1_b[e]))
                w2p.issue(sc.dma_start(out=w2e[:, (e % 2) * 4096:(e % 2 + 1) * 4096],
                                       in_=c.m0w2_b[e]))
            w1_slot = w1p.slot(w1p.issued - 1)
            w2_slot = w2p.slot(w2p.issued - 1)
            eb = (e % 2) * 4096

            for fs in range(4):
                hbase = (fs % 2) * 1024
                need_hev = hev.v - 1 if (gidx * 4 + fs) >= 2 else 0
                need_gps_all = gps.v if (e == 0 and fs < 2) else 0
                need_b2_h = b2_all if (e == 0 and fs == 1) else 0
                need_yac_h = yac.v if (fs == 1 and e > 0) else 0
                for wd in range(2):
                    hslot = hbase + wd * 512

                    @block.tensor
                    def _(te, fs=fs, wd=wd, hslot=hslot, eb=eb, xb=xb,
                          w1_slot=w1_slot, xn_slot=xn_slot, need_hev=need_hev,
                          need_gps_all=need_gps_all, need_b2_h=need_b2_h,
                          need_yac_h=need_yac_h):
                        te.wait_ge(w1_slot[0].h, w1_slot[1])
                        te.wait_ge(xn_slot[0].h, xn_slot[1])
                        if need_gps_all:
                            te.wait_ge(gps.h, need_gps_all)
                        if need_b2_h:
                            te.wait_ge(b2s.h, need_b2_h)
                        if wd == 0 and need_yac_h:
                            te.wait_ge(yac.h, need_yac_h)
                        if wd == 0 and need_hev:
                            te.wait_ge(hev.h, need_hev)
                        for dc in range(8):
                            ins = te.matmul(
                                ps[:, hslot:hslot + 512],
                                w1e[:, eb + dc * 512 + fs * 128:eb + dc * 512 + (fs + 1) * 128],
                                xnst[:, xb + dc * 1024 + wd * 512:xb + dc * 1024 + wd * 512 + 512],
                                start=(dc == 0), stop=(dc == 7))
                            if dc == 7:
                                ins.then_inc(mm.h, 1)
                    mm.add(1)
                need_mm_h = mm.v

                @block.scalar
                def _(sc, e=e, fs=fs, hbase=hbase, need_mm_h=need_mm_h):
                    sc.wait_ge(mm.h, need_mm_h)
                    sc.activation(h_bf[:, fs * 1024:(fs + 1) * 1024],
                                  ps[:, hbase:hbase + 1024], AF.Gelu,
                                  bias=b1c[:, fs * 8 + e:fs * 8 + e + 1]).then_inc(hev.h, 1)
                hev.add(1)
            need_hev_wd = hev.v

            for wd in range(2):
                for tl in range(4):
                    tile = wd * 4 + tl
                    yslot = 1024 + (tl % 2) * 1024
                    need_yac = yac.v - 1 if (e * 8 + wd * 4 + tl) >= 2 or st > 0 else 0
                    need_b2_all = b2_all

                    @block.tensor
                    def _(te, tl=tl, eb=eb, yslot=yslot, need_hev_wd=need_hev_wd,
                          need_yac=need_yac, w2_slot=w2_slot, need_b2_all=need_b2_all):
                        te.wait_ge(hev.h, need_hev_wd)
                        te.wait_ge(w2_slot[0].h, w2_slot[1])
                        te.wait_ge(b2s.h, need_b2_all)
                        if need_yac:
                            te.wait_ge(yac.h, need_yac)
                        k = 0
                        for fs in range(4):
                            for dh in range(2):
                                ins = te.matmul(
                                    ps[:, yslot + dh * 512:yslot + dh * 512 + 512],
                                    h_bf[:, fs * 1024 + wd * 512 + tl * 128:
                                         fs * 1024 + wd * 512 + (tl + 1) * 128],
                                    w2e[:, eb + fs * 1024 + dh * 512:eb + fs * 1024 + dh * 512 + 512],
                                    start=(fs == 0), stop=(fs == 3))
                                k += 1
                                if k == 8:
                                    ins.then_inc(mm.h, 1)
                    mm.add(1)
                    need_mm_y = mm.v

                    @block.vector
                    def _(v, st=st, e=e, tile=tile, yslot=yslot, need_mm_y=need_mm_y,
                          need_b2_all=need_b2_all):
                        v.wait_ge(mm.h, need_mm_y)
                        v.wait_ge(b2s.h, need_b2_all)
                        v.wait_ge(yac.h, yac.v)
                        v.scalar_tensor_tensor(
                            g_acc[:, tile * 1024:(tile + 1) * 1024],
                            ps[:, yslot:yslot + 1024],
                            w_st[:, tile * 8 + e:tile * 8 + e + 1],
                            g_acc[:, tile * 1024:(tile + 1) * 1024],
                            ALU.mult, ALU.add).then_inc(yac.h, 1)
                    yac.add(1)
            expert_mm_done[gidx] = mm.v
        st_consume_mm[st] = mm.v

        last_yac = yac.v
        for t in range(8):
            sslot = (t % 2) * 1024
            sg_free = sgp.slot(sgp.issued - 2) if sgp.issued >= 2 else None

            @block.scalar
            def _(sc, t=t, sslot=sslot, last_yac=last_yac, sg_free=sg_free):
                sc.wait_ge(yac.h, last_yac)
                if sg_free is not None:
                    sc.wait_ge(sg_free[0].h, sg_free[1])
                sc.activation(sg_t[:, sslot:sslot + 1024],
                              g_acc[:, t * 1024:(t + 1) * 1024], AF.Silu).then_inc(sil.h, 1)
            sil.add(1)
            need_sil = sil.v

            @block.sync
            def _(sy, st=st, t=t, sslot=sslot, need_sil=need_sil):
                sy.wait_ge(sil.h, need_sil)
                sgp.issue(sy.dma_start(
                    out=c.sg0_d[st * 1024 + t * 128:st * 1024 + (t + 1) * 128, :],
                    in_=sg_t[:, sslot:sslot + 1024]))
        prev_sil_all = sil.v
        prev_yac_all = yac.v

    @block.sync
    def _(sy):
        sgp.wait_all(sy)


def emit_b0(nc, block, c, sems, stack):
    mm = sems["b_mm"]; qev = sems["b_qev"]; zev = sems["b_zev"]; aev = sems["b_aev"]
    x1s = sems["b_x1"]; sq = sems["a_sq"]; col = sems["a_col"]; rstd = sems["a_rstd"]
    xn = sems["a_xn"]; tr = sems["a_tr"]; xnf = sems["a_xnf"]
    wldb = sems["a_wld"]; p0st = sems["p0_st"]
    vch = sems["ch"]["v"]
    xnwp = pool(sems, nc, "b_xnw", 2)
    xwp = pool(sems, nc, "b_xw", 2)
    sgwp = pool(sems, nc, "b_sgw", 2)
    kvp = pool(sems, nc, "b_kv", 1)
    x1p = pool(sems, nc, "b_x1st", 2)
    xn1p = pool(sems, nc, "b_xn1st", 2)

    wq_sb = stack.enter_context(nc.sbuf_tensor("b_wq", [128, NCH * D], BF16))
    wout_sb = stack.enter_context(nc.sbuf_tensor("b_wout", [128, NCH * D], BF16))
    kv_sb = stack.enter_context(nc.sbuf_tensor("b_kv", [128, NCH * D], BF16))
    boutf = stack.enter_context(nc.sbuf_tensor("b_boutf", [1, D], F32))
    boutr = stack.enter_context(nc.sbuf_tensor("b_boutr", [1, D], BF16))
    xn_w = stack.enter_context(nc.sbuf_tensor("b_xnw", [128, 2 * 4096], BF16))
    xw = stack.enter_context(nc.sbuf_tensor("b_xw", [128, 2 * 4096], F32))
    sgw = stack.enter_context(nc.sbuf_tensor("b_sgw", [128, 2 * 4096], F32))
    q_bf = stack.enter_context(nc.sbuf_tensor("b_q", [128, 8 * 512], BF16))
    at_bf = stack.enter_context(nc.sbuf_tensor("b_at", [128, 8 * 512], BF16))
    zcol = stack.enter_context(nc.sbuf_tensor("b_zcol", [128, 4], F32))
    x1tmp = stack.enter_context(nc.sbuf_tensor("b_x1tmp", [128, D], F32))
    x1t = stack.enter_context(nc.sbuf_tensor("b_x1t", [128, 2 * D], F32))
    sqd = stack.enter_context(nc.sbuf_tensor("b_sqd", [128, D], BF16))
    cols = stack.enter_context(nc.sbuf_tensor("b_cols", [128, 8], F32))
    xn1_tm = stack.enter_context(nc.sbuf_tensor("b_xn1tm", [128, 2 * D], BF16))
    xn1f = stack.enter_context(nc.sbuf_tensor("b_xn1f", [128, 2 * D], BF16))
    cc = stack.enter_context(nc.sbuf_tensor("b_cc", [128, 1], F32))

    ps = stack.enter_context(nc.psum_tensor("b_ps", [128, 3584], F32))
    ps_bf = stack.enter_context(nc.psum_tensor("b_psbf", [128, 1024], BF16))

    cns = S(sems["_stack"].enter_context(nc.semaphore("b_cns")))
    bld = S(sems["_stack"].enter_context(nc.semaphore("b_bld")))

    @block.vector
    def _(v):
        v.memset(cc[:, 0:1], -0.5)
        v.memset(cols[:, 2:3], 0.0).then_inc(cns.h, 1)
    cns.add(1)

    @block.scalar
    def _(sc):
        sc.wait_ge(cns.h, cns.v)
        sc.wait_ge(p0st.h, p0st.v)
        sc.dma_start(out=wq_sb[:], in_=c.wq_t).then_inc(wldb.h, 16)
        sc.dma_start(out=wout_sb[:], in_=c.wout_t).then_inc(wldb.h, 16)
        sc.dma_start(out=boutf[:], in_=c.b_out).then_inc(wldb.h, 16)
        sc.wait_ge(wldb.h, wldb.v + 48)
        sc.activation(boutr[:], boutf[:], AF.Copy).then_inc(bld.h, 1)
    wld_all = wldb.add(48)
    bld.add(1)

    b0_state["xnw_done"] = {}
    b0_state["gate_done"] = {}
    b0_state["tile_gate"] = {}
    b0_state["sev_prev"] = 0
    ln_bufs = (sqd, cols)
    for s in range(SEQ):
        kv_free_mm = mm.v

        @block.sync
        def _(sy, s=s, kv_free_mm=kv_free_mm):
            if kv_free_mm:
                sy.wait_ge(mm.h, kv_free_mm)
            kvp.issue(sy.dma_start(out=kv_sb[:], in_=c.kv_d[s]))
        kv_slot = kvp.slot(kvp.issued - 1)

        for w in range(4):
            gw = s * 4 + w
            wslot = gw % 2
            @block.sync
            def _(sy, s=s, w=w, wslot=wslot, gw=gw):
                if gw >= 2:
                    sy.wait_ge(mm.h, b0_state["xnw_done"][gw - 2])
                xnwp.issue(sy.dma_start(
                    out=xn_w[:, wslot * 4096:(wslot + 1) * 4096]
                        .rearrange("p (c t) -> p c t", c=8),
                    in_=c.xn_d.rearrange("p (c t) -> p c t", c=NCH)
                        [:, :, s * SLEN + w * 512:s * SLEN + (w + 1) * 512]))
                if gw >= 2:
                    sy.wait_ge(x1s.h, b0_state["gate_done"][gw - 2])
                xwp.issue(sy.dma_start(
                    out=xw[:, wslot * 4096:(wslot + 1) * 4096]
                        .rearrange("p (t d) -> p t d", t=4),
                    in_=c.x[s, w * 512:(w + 1) * 512, :]
                        .rearrange("(t p) d -> p t d", p=128)))
                sgwp.issue(sy.dma_start(
                    out=sgw[:, wslot * 4096:(wslot + 1) * 4096]
                        .rearrange("p (t d) -> p t d", t=4),
                    in_=c.sg0_d[s * SLEN + w * 512:s * SLEN + (w + 1) * 512, :]
                        .rearrange("(t p) d -> p t d", p=128)))
            xn_slot = xnwp.slot(xnwp.issued - 1)
            xw_slot = xwp.slot(xwp.issued - 1)
            sg_slot = sgwp.slot(sgwp.issued - 1)
            xb = wslot * 4096

            for qc in range(8):
                qslot = (qc % 2) * 512
                need_qev = qev.v - 1 if (gw * 8 + qc) >= 2 else 0

                @block.tensor
                def _(te, qc=qc, qslot=qslot, xb=xb, need_qev=need_qev,
                      xn_slot=xn_slot, kv_slot=kv_slot):
                    te.wait_ge(xn_slot[0].h, xn_slot[1])
                    te.wait_ge(wldb.h, wld_all)
                    if need_qev:
                        te.wait_ge(qev.h, need_qev)
                    for dc in range(8):
                        ins = te.matmul(
                            ps[:, qslot:qslot + 512],
                            wq_sb[:, dc * D + qc * 128:dc * D + (qc + 1) * 128],
                            xn_w[:, xb + dc * 512:xb + (dc + 1) * 512],
                            start=(dc == 0), stop=(dc == 7))
                        if dc == 7:
                            ins.then_inc(mm.h, 1)
                mm.add(1)
                need_mm_q = mm.v

                @block.vector
                def _(v, qc=qc, qslot=qslot, need_mm_q=need_mm_q):
                    v.wait_ge(mm.h, need_mm_q)
                    v.tensor_scalar(q_bf[:, qc * 512:(qc + 1) * 512],
                                    ps[:, qslot:qslot + 512], 0.0, ATT_EPS,
                                    ALU.max, ALU.add).then_inc(qev.h, 1)
                qev.add(1)
            qev_all = qev.v

            need_zev_prev = zev.v

            @block.tensor
            def _(te, s=s, need_zev_prev=need_zev_prev, qev_all=qev_all):
                te.wait_ge(qev.h, qev_all)
                if need_zev_prev:
                    te.wait_ge(zev.h, need_zev_prev)
                for tl in range(4):
                    for ac in range(8):
                        ins = te.matmul(
                            ps[:, 3072 + tl:3072 + tl + 1],
                            q_bf[:, ac * 512 + tl * 128:ac * 512 + (tl + 1) * 128],
                            c.g.ksum_bf[:, s * 8 + ac:s * 8 + ac + 1],
                            start=(ac == 0), stop=(ac == 7))
                        if tl == 3 and ac == 7:
                            ins.then_inc(mm.h, 1)
            mm.add(1)
            need_mm_z = mm.v

            @block.vector
            def _(v, need_mm_z=need_mm_z):
                v.wait_ge(mm.h, need_mm_z)
                vch.inc(v.tensor_scalar(zcol[:], ps[:, 3072:3076], ATT_EPS, None, ALU.add))
                vch.wait(v)
                v.reciprocal(zcol[:], zcol[:]).then_inc(zev.h, 1)
            zev.add(1)
            zev_now = zev.v

            sev_prev = b0_state["sev_prev"]
            for bc in range(8):
                aslot = 1024 + (bc % 2) * 512
                need_aev = aev.v - 1 if (gw * 8 + bc) >= 2 else 0

                @block.tensor
                def _(te, bc=bc, aslot=aslot, need_aev=need_aev, sev_prev=sev_prev,
                      kv_slot=kv_slot, qev_all=qev_all):
                    te.wait_ge(kv_slot[0].h, kv_slot[1])
                    te.wait_ge(qev.h, qev_all)
                    if need_aev:
                        te.wait_ge(aev.h, need_aev)
                    if bc < 2 and sev_prev:
                        te.wait_ge(x1s.h, sev_prev)
                    for ac in range(8):
                        ins = te.matmul(
                            ps[:, aslot:aslot + 512],
                            kv_sb[:, ac * D + bc * 128:ac * D + (bc + 1) * 128],
                            q_bf[:, ac * 512:(ac + 1) * 512],
                            start=(ac == 0), stop=(ac == 7))
                        if ac == 7:
                            ins.then_inc(mm.h, 1)
                mm.add(1)
                need_mm_a = mm.v

                @block.scalar
                def _(sc, bc=bc, aslot=aslot, need_mm_a=need_mm_a):
                    sc.wait_ge(mm.h, need_mm_a)
                    sc.activation(at_bf[:, bc * 512:(bc + 1) * 512],
                                  ps[:, aslot:aslot + 512], AF.Copy).then_inc(aev.h, 1)
                aev.add(1)
            aev_all = aev.v

            for tl in range(4):
                gt = s * SLEN + w * 512 + tl * 128
                gtile = gw * 4 + tl
                sslot = 2048 if (tl % 2 == 0) else 1024

                @block.tensor
                def _(te, tl=tl, sslot=sslot, aev_all=aev_all, gtile=gtile):
                    te.wait_ge(aev.h, aev_all)
                    te.wait_ge(bld.h, 1)
                    if gtile >= 2:
                        te.wait_ge(x1s.h, b0_state["tile_gate"][gtile - 2])
                    for bc in range(8):
                        for dh in range(2):
                            te.matmul(
                                ps[:, sslot + dh * 512:sslot + dh * 512 + 512],
                                at_bf[:, bc * 512 + tl * 128:bc * 512 + (tl + 1) * 128],
                                wout_sb[:, bc * D + dh * 512:bc * D + dh * 512 + 512],
                                start=(bc == 0), stop=False)
                    for dh in range(2):
                        ins = te.matmul(ps[:, sslot + dh * 512:sslot + dh * 512 + 512],
                                        c.g.ones_row[0:1, :],
                                        boutr[0:1, dh * 512:dh * 512 + 512],
                                        start=False, stop=True)
                        if dh == 1:
                            ins.then_inc(mm.h, 1)
                mm.add(1)
                need_mm_s = mm.v

                x1_ap = x1t[:, (tl % 2) * D:(tl % 2 + 1) * D]
                x1_free = x1p.slot(x1p.issued - 2) if x1p.issued >= 2 else None
                need_xn_prev = xn.v - 1 if gtile >= 2 else 0

                @block.vector
                def _(v, tl=tl, sslot=sslot, need_mm_s=need_mm_s, x1_ap=x1_ap,
                      xw_slot=xw_slot, sg_slot=sg_slot, x1_free=x1_free, xb=xb,
                      need_xn_prev=need_xn_prev, zev_now=zev_now):
                    v.wait_ge(mm.h, need_mm_s)
                    v.wait_ge(zev.h, zev_now)
                    v.wait_ge(sg_slot[0].h, sg_slot[1])
                    v.wait_ge(xw_slot[0].h, xw_slot[1])
                    if x1_free is not None:
                        v.wait_ge(x1_free[0].h, x1_free[1])
                    if need_xn_prev:
                        v.wait_ge(xn.h, need_xn_prev)
                    vch.wait(v)
                    vch.inc(v.scalar_tensor_tensor(
                        x1tmp[:], ps[:, sslot:sslot + 1024], zcol[:, tl:tl + 1],
                        sgw[:, xb + tl * 1024:xb + (tl + 1) * 1024], ALU.mult, ALU.mult))
                    vch.wait(v)
                    v.tensor_tensor(x1_ap, x1tmp[:],
                                    xw[:, xb + tl * 1024:xb + (tl + 1) * 1024],
                                    ALU.add).then_inc(x1s.h, 1)
                x1s.add(1)
                need_x1 = x1s.v

                @block.sync
                def _(sy, gt=gt, x1_ap=x1_ap, need_x1=need_x1):
                    sy.wait_ge(x1s.h, need_x1)
                    x1p.issue(sy.dma_start(out=c.x1_d[gt:gt + 128, :], in_=x1_ap))

                xn1_ap = xn1_tm[:, (tl % 2) * D:(tl % 2 + 1) * D]
                emit_ln(block, sems["ch"], ln_bufs, x1_ap, xn1_ap, (x1s, need_x1),
                        sq, col, rstd, xn, cc,
                        apply_wait=(tr, tr.v - 1 if gtile >= 2 else 0))
                need_xn1 = xn.v
                need_xnf_prev = xnf.v

                @block.tensor
                def _(te, xn1_ap=xn1_ap, need_xn1=need_xn1, need_xnf_prev=need_xnf_prev):
                    te.wait_ge(xn.h, need_xn1)
                    if need_xnf_prev:
                        te.wait_ge(xnf.h, need_xnf_prev)
                    for dc in range(8):
                        ins = te.transpose(ps_bf[:, dc * 128:(dc + 1) * 128],
                                           xn1_ap[:, dc * 128:(dc + 1) * 128],
                                           c.g.id_bf[:, :])
                        if dc == 7:
                            ins.then_inc(tr.h, 1)
                tr.add(1)
                need_tr = tr.v
                st_slot = xn1p.slot(xn1p.issued - 2) if xn1p.issued >= 2 else None

                @block.vector
                def _(v, tl=tl, need_tr=need_tr, st_slot=st_slot):
                    v.wait_ge(tr.h, need_tr)
                    if st_slot is not None:
                        v.wait_ge(st_slot[0].h, st_slot[1])
                    v.tensor_copy(xn1f[:, (tl % 2) * D:(tl % 2 + 1) * D],
                                  ps_bf[:]).then_inc(xnf.h, 1)
                xnf.add(1)
                need_xnf = xnf.v

                @block.sync
                def _(sy, tl=tl, gt=gt, need_xnf=need_xnf):
                    sy.wait_ge(xnf.h, need_xnf)
                    xn1p.issue(sy.dma_start(
                        out=c.xn1_d.rearrange("p (c t) -> p c t", c=NCH)[:, :, gt:gt + 128],
                        in_=xn1f[:, (tl % 2) * D:(tl % 2 + 1) * D]
                            .rearrange("p (c t) -> p c t", t=128)))
                b0_state["tile_gate"][gtile] = x1s.v
            b0_state["xnw_done"][gw] = mm.v
            b0_state["gate_done"][gw] = x1s.v
            b0_state["sev_prev"] = x1s.v

    @block.sync
    def _(sy):
        x1p.wait_all(sy)
        xn1p.wait_all(sy)


b0_state = {"xnw_done": {}, "gate_done": {}, "tile_gate": {}, "sev_prev": 0}


def emit_f1(nc, block, c, sems, stack):
    mm = sems["f_mm"]; gps = sems["f_gps"]; gs2 = sems["f_gs2"]; wtv = sems["f_wtv"]
    b2s = sems["f_b2"]; hev = sems["f_hev"]; yac = sems["f_yac"]; sil = sems["f_sil"]
    wcast1 = sems["wcast1"]
    vch = sems["ch"]["v"]
    xnp = pool(sems, nc, "g_xn2", 2)
    w1p = pool(sems, nc, "f_w1", 4)
    w2p = pool(sems, nc, "g_w2", 4)
    sgp = pool(sems, nc, "g_sg", 1)
    smp = pool(sems, nc, "f_sm", 1)
    smb = pool(sems, nc, "f_smb", 1)

    gw_sb = stack.enter_context(nc.sbuf_tensor("g_gw", [128, 64], BF16))
    b1c = stack.enter_context(nc.sbuf_tensor("g_b1c", [128, 256], F32))
    b2n = stack.enter_context(nc.sbuf_tensor("g_b2n", [8, 1024], BF16))
    ps = stack.enter_context(nc.psum_tensor("g_ps", [128, 4096], F32))

    @block.gpsimd
    def _(g):
        g.wait_ge(sems["ph"].h, sems["ph"].v)
        g.wait_ge(wcast1.h, wcast1.v)
        smp.issue(g.dma_start(out=gw_sb[:].rearrange("p (c e) -> p c e", c=8),
                              in_=c.wg.rearrange("(c p) e -> p c e", p=128)))
        smp.issue(g.dma_start(out=b2n[:], in_=c.m1b2))

    with contextlib.ExitStack() as b1stack:
        b1n = b1stack.enter_context(nc.sbuf_tensor("g_b1n", [8, 4096], F32))
        j_b1 = None

        @block.sync
        def _(sy):
            nonlocal j_b1
            j_b1 = smb.issue(sy.dma_start(out=b1n[:], in_=c.m1b1))

        @block.tensor
        def _(te):
            smb.wait(te, j_b1)
            for fs in range(32):
                te.transpose(ps[:, 3072 + fs * 8:3072 + fs * 8 + 8],
                             b1n[0:8, fs * 128:(fs + 1) * 128],
                             c.g.id_f32[0:8, 0:8]).then_inc(mm.h, 1)
                mm.add(1)
        need_mm_b1 = mm.v

        @block.vector
        def _(v):
            v.wait_ge(mm.h, need_mm_b1)
            for fs in range(32):
                vch.inc(v.tensor_copy(b1c[:, fs * 8:(fs + 1) * 8],
                                      ps[:, 3072 + fs * 8:3072 + fs * 8 + 8]))
    b1c_done = vch.s.v
    b1_tr_mm = mm.v

    xnst = stack.enter_context(nc.sbuf_tensor("g_xnst", [128, 2 * 8192], BF16))
    w1sl = stack.enter_context(nc.sbuf_tensor("g_w1sl", [128, 2 * 8192], BF16))
    w2sl = stack.enter_context(nc.sbuf_tensor("g_w2sl", [128, 2 * 8192], BF16))
    h_bf = stack.enter_context(nc.sbuf_tensor("g_h", [128, 32 * 1024], BF16))
    g_acc = stack.enter_context(nc.sbuf_tensor("g_gacc", [128, 8 * 1024], F32))
    gsb = stack.enter_context(nc.sbuf_tensor("g_gsb", [128, 8], F32))
    gates = stack.enter_context(nc.sbuf_tensor("g_gates", [128, 8], F32))
    selt = stack.enter_context(nc.sbuf_tensor("g_selt", [128, 8], F32))
    gcols = stack.enter_context(nc.sbuf_tensor("g_gcols", [128, 8], F32))
    w_st = stack.enter_context(nc.sbuf_tensor("g_wst", [128, 64], F32))
    wT_st = stack.enter_context(nc.sbuf_tensor("g_wTst", [8, 1024], BF16))
    sg_t = stack.enter_context(nc.sbuf_tensor("g_sgt", [128, 1024], F32))


    f1_state["w1_free"] = {}
    f1_state["w2_free"] = {}
    expert_mm_done = {}
    st_consume_mm = {}
    prev_sil_all = 0
    prev_yac_all = 0
    for st in range(4):
        @block.sync
        def _(sy, st=st):
            if st >= 2:
                sy.wait_ge(mm.h, st_consume_mm[st - 2])
            else:
                sy.wait_ge(mm.h, b1_tr_mm)
            xnp.issue(sy.dma_start(
                out=xnst[:, (st % 2) * 8192:(st % 2 + 1) * 8192]
                    .rearrange("p (c t) -> p c t", c=8),
                in_=c.xn1_d.rearrange("p (c t) -> p c t", c=NCH)
                    [:, :, st * 1024:(st + 1) * 1024]))
        xn_slot = xnp.slot(xnp.issued - 1)
        xb = (st % 2) * 8192

        for t in range(8):
            gslot = 3328 + (t % 2) * 128
            need_gps = gps.v - 1 if (st * 8 + t) >= 2 else 0
            need_yac_prev = prev_yac_all
            need_b1c = b1c_done if (st == 0 and t == 0) else 0

            @block.tensor
            def _(te, t=t, gslot=gslot, need_gps=need_gps, xn_slot=xn_slot,
                  xb=xb, need_yac_prev=need_yac_prev, need_b1c=need_b1c):
                te.wait_ge(xn_slot[0].h, xn_slot[1])
                smp.wait_all(te)
                if need_b1c:
                    te.wait_ge(vch.s.h, need_b1c)
                if need_gps:
                    te.wait_ge(gps.h, need_gps)
                if need_yac_prev:
                    te.wait_ge(yac.h, need_yac_prev)
                for dc in range(8):
                    ins = te.matmul(ps[:, gslot:gslot + 8],
                                    xnst[:, xb + dc * 1024 + t * 128:xb + dc * 1024 + (t + 1) * 128],
                                    gw_sb[:, dc * 8:(dc + 1) * 8],
                                    start=(dc == 0), stop=(dc == 7))
                    if dc == 7:
                        ins.then_inc(mm.h, 1)
            mm.add(1)
            need_mm_g = mm.v

            @block.vector
            def _(v, gslot=gslot, need_mm_g=need_mm_g):
                v.wait_ge(mm.h, need_mm_g)
                vch.inc(v.reduce_max(gcols[:, 0:1], ps[:, gslot:gslot + 8], axis=AX.X))
                vch.wait(v)
                v.tensor_scalar(gsb[:], ps[:, gslot:gslot + 8], gcols[:, 0:1], None,
                                ALU.subtract).then_inc(gps.h, 1)
            gps.add(1)
            need_gps2 = gps.v

            @block.scalar
            def _(sc, need_gps2=need_gps2):
                sc.wait_ge(gps.h, need_gps2)
                sc.activation(gsb[:], gsb[:], AF.Exp).then_inc(gs2.h, 1)
            gs2.add(1)
            need_gs2 = gs2.v

            @block.vector
            def _(v, t=t, need_gs2=need_gs2):
                v.wait_ge(gs2.h, need_gs2)
                vch.inc(v.reduce_sum(gcols[:, 1:2], gsb[:], axis=AX.X))
                vch.wait(v)
                vch.inc(v.reciprocal(gcols[:, 2:3], gcols[:, 1:2]))
                vch.wait(v)
                vch.inc(v.tensor_scalar(gates[:], gsb[:], gcols[:, 2:3], None, ALU.mult))
                vch.wait(v)
                vch.inc(v.reduce_max(gcols[:, 3:4], gates[:], axis=AX.X))
                vch.wait(v)
                vch.inc(v.tensor_scalar(selt[:], gates[:], gcols[:, 3:4], None, ALU.is_ge))
                vch.wait(v)
                vch.inc(v.scalar_tensor_tensor(selt[:], selt[:], -1e30, gates[:],
                                               ALU.mult, ALU.add))
                vch.wait(v)
                vch.inc(v.reduce_max(gcols[:, 4:5], selt[:], axis=AX.X))
                vch.wait(v)
                vch.inc(v.tensor_scalar(selt[:], gates[:], gcols[:, 4:5], None, ALU.is_ge))
                vch.wait(v)
                vch.inc(v.tensor_tensor(gcols[:, 5:6], gcols[:, 3:4], gcols[:, 4:5], ALU.add))
                vch.wait(v)
                vch.inc(v.tensor_scalar(gcols[:, 6:7], gcols[:, 5:6], MOE_EPS, None, ALU.add))
                vch.wait(v)
                vch.inc(v.reciprocal(gcols[:, 7:8], gcols[:, 6:7]))
                vch.wait(v)
                v.scalar_tensor_tensor(w_st[:, t * 8:(t + 1) * 8], gates[:],
                                        gcols[:, 7:8], selt[:],
                                        ALU.mult, ALU.mult).then_inc(wtv.h, 1)
            wtv.add(1)
            need_wtv = wtv.v
            wtslot = 3584 + (t % 2) * 128

            @block.tensor
            def _(te, t=t, need_wtv=need_wtv, wtslot=wtslot):
                te.wait_ge(wtv.h, need_wtv)
                te.transpose(ps[0:8, wtslot:wtslot + 128],
                             w_st[:, t * 8:(t + 1) * 8],
                             c.g.id_f32[:, :]).then_inc(mm.h, 1)
            mm.add(1)
            need_mm_wt = mm.v

            @block.vector
            def _(v, t=t, need_mm_wt=need_mm_wt, wtslot=wtslot):
                v.wait_ge(mm.h, need_mm_wt)
                v.tensor_copy(wT_st[0:8, t * 128:(t + 1) * 128],
                              ps[0:8, wtslot:wtslot + 128]).then_inc(wtv.h, 1)
            wtv.add(1)
            need_wtv2 = wtv.v
            b2slot = 2048 if (t % 2 == 0) else 1024
            need_b2prev = b2s.v - 1 if (st * 8 + t) >= 2 else 0

            @block.tensor
            def _(te, t=t, need_wtv2=need_wtv2, b2slot=b2slot, need_b2prev=need_b2prev):
                te.wait_ge(wtv.h, need_wtv2)
                if need_b2prev:
                    te.wait_ge(b2s.h, need_b2prev)
                for dh in range(2):
                    ins = te.matmul(ps[:, b2slot + dh * 512:b2slot + dh * 512 + 512],
                                    wT_st[0:8, t * 128:(t + 1) * 128],
                                    b2n[0:8, dh * 512:dh * 512 + 512],
                                    start=True, stop=True)
                    if dh == 1:
                        ins.then_inc(mm.h, 1)
            mm.add(1)
            need_mm_b2 = mm.v
            need_sil_prev = prev_sil_all

            @block.scalar
            def _(sc, t=t, b2slot=b2slot, need_mm_b2=need_mm_b2,
                  need_sil_prev=need_sil_prev):
                sc.wait_ge(mm.h, need_mm_b2)
                if need_sil_prev:
                    sc.wait_ge(sil.h, need_sil_prev)
                sc.activation(g_acc[:, t * 1024:(t + 1) * 1024],
                              ps[:, b2slot:b2slot + 1024], AF.Copy).then_inc(b2s.h, 1)
            b2s.add(1)
        b2_all = b2s.v

        for e in range(8):
            gidx = st * 8 + e
            yac_at_e = yac.v
            for q in range(4):
                jq = gidx * 4 + q
                free_mm = f1_state["w1_free"].get(jq - 2, 0)

                @block.scalar
                def _(sc, e=e, q=q, free_mm=free_mm):
                    if free_mm:
                        sc.wait_ge(mm.h, free_mm)
                    sc.wait_ge(wcast1.h, wcast1.v)
                    w1p.issue(sc.dma_start(
                        out=w1sl[:, (q % 2) * 8192:(q % 2 + 1) * 8192],
                        in_=c.m1w1_b[e][:, q * 8192:(q + 1) * 8192]))
                w1_slot = w1p.slot(w1p.issued - 1)

                for fi in range(8):
                    fs = q * 8 + fi
                    wb1 = (q % 2) * 8192 + fi * 1024
                    hbase = (fs % 2) * 1024
                    for wd in range(2):
                        hslot = hbase + wd * 512
                        need_hev = hev.v - 1 if (gidx * 32 + fs) >= 2 else 0
                        need_yac_all = yac_at_e if (fs == 0 and wd == 0) else 0
                        need_b2_h = b2_all if (e == 0 and fs <= 1) else 0

                        @block.tensor
                        def _(te, wb1=wb1, wd=wd, hslot=hslot, xb=xb, w1_slot=w1_slot,
                              xn_slot=xn_slot, need_hev=need_hev,
                              need_yac_all=need_yac_all, fi=fi, need_b2_h=need_b2_h):
                            if fi == 0 and wd == 0:
                                te.wait_ge(w1_slot[0].h, w1_slot[1])
                                te.wait_ge(xn_slot[0].h, xn_slot[1])
                            if need_b2_h:
                                te.wait_ge(b2s.h, need_b2_h)
                            if need_yac_all:
                                te.wait_ge(yac.h, need_yac_all)
                            if wd == 0 and need_hev:
                                te.wait_ge(hev.h, need_hev)
                            for dc in range(8):
                                ins = te.matmul(
                                    ps[:, hslot:hslot + 512],
                                    w1sl[:, wb1 + dc * 128:wb1 + (dc + 1) * 128],
                                    xnst[:, xb + dc * 1024 + wd * 512:xb + dc * 1024 + wd * 512 + 512],
                                    start=(dc == 0), stop=(dc == 7))
                                if dc == 7:
                                    ins.then_inc(mm.h, 1)
                        mm.add(1)
                    need_mm_h = mm.v

                    @block.scalar
                    def _(sc, e=e, fs=fs, hbase=hbase, need_mm_h=need_mm_h):
                        sc.wait_ge(mm.h, need_mm_h)
                        sc.activation(h_bf[:, fs * 1024:(fs + 1) * 1024],
                                      ps[:, hbase:hbase + 1024], AF.Gelu,
                                      bias=b1c[:, fs * 8 + e:fs * 8 + e + 1]).then_inc(hev.h, 1)
                    hev.add(1)
                f1_state["w1_free"][jq] = mm.v
            hev_all = hev.v

            for wd in range(2):
                yac_at_wd = yac.v
                for q in range(4):
                    jq2 = (st * 16 + e * 2 + wd) * 4 + q
                    free_mm2 = f1_state["w2_free"].get(jq2 - 2, 0)

                    @block.gpsimd
                    def _(g, e=e, q=q, free_mm2=free_mm2):
                        if free_mm2:
                            g.wait_ge(mm.h, free_mm2)
                        g.wait_ge(wcast1.h, wcast1.v)
                        w2p.issue(g.dma_start(
                            out=w2sl[:, (q % 2) * 8192:(q % 2 + 1) * 8192],
                            in_=c.m1w2_b[e][:, q * 8192:(q + 1) * 8192]))
                    w2_slot = w2p.slot(w2p.issued - 1)

                    for fi in range(8):
                        fs = q * 8 + fi
                        wb2 = (q % 2) * 8192 + fi * 1024

                        @block.tensor
                        def _(te, fs=fs, fi=fi, wd=wd, wb2=wb2, w2_slot=w2_slot,
                              hev_all=hev_all, e=e, b2_all=b2_all, yac_at_wd=yac_at_wd):
                            if fi == 0:
                                te.wait_ge(w2_slot[0].h, w2_slot[1])
                                te.wait_ge(hev.h, hev_all)
                                te.wait_ge(b2s.h, b2_all)
                            if fs == 0 and yac_at_wd:
                                te.wait_ge(yac.h, yac_at_wd)
                            for tl in range(4):
                                for dh in range(2):
                                    ins = te.matmul(
                                        ps[:, tl * 1024 + dh * 512:tl * 1024 + dh * 512 + 512],
                                        h_bf[:, fs * 1024 + wd * 512 + tl * 128:
                                             fs * 1024 + wd * 512 + (tl + 1) * 128],
                                        w2sl[:, wb2 + dh * 512:wb2 + dh * 512 + 512],
                                        start=(fs == 0), stop=(fs == 31))
                                    if tl == 3 and dh == 1:
                                        ins.then_inc(mm.h, 1)
                        mm.add(1)
                        f1_state["w2_free"][jq2] = mm.v
                need_mm_y = mm.v

                for tl in range(4):
                    tile = wd * 4 + tl

                    @block.vector
                    def _(v, st=st, e=e, tl=tl, tile=tile, need_mm_y=need_mm_y,
                          b2_all=b2_all):
                        v.wait_ge(mm.h, need_mm_y)
                        v.wait_ge(b2s.h, b2_all)
                        v.wait_ge(yac.h, yac.v)
                        v.scalar_tensor_tensor(
                            g_acc[:, tile * 1024:(tile + 1) * 1024],
                            ps[:, tl * 1024:(tl + 1) * 1024],
                            w_st[:, tile * 8 + e:tile * 8 + e + 1],
                            g_acc[:, tile * 1024:(tile + 1) * 1024],
                            ALU.mult, ALU.add).then_inc(yac.h, 1)
                    yac.add(1)
            expert_mm_done[gidx] = mm.v
        st_consume_mm[st] = mm.v

        last_yac = yac.v
        for t in range(8):
            sg_free = sgp.slot(sgp.issued - 1) if sgp.issued >= 1 else None

            @block.scalar
            def _(sc, t=t, last_yac=last_yac, sg_free=sg_free):
                sc.wait_ge(yac.h, last_yac)
                if sg_free is not None:
                    sc.wait_ge(sg_free[0].h, sg_free[1])
                sc.activation(sg_t[:],
                              g_acc[:, t * 1024:(t + 1) * 1024], AF.Silu).then_inc(sil.h, 1)
            sil.add(1)
            need_sil = sil.v

            @block.sync
            def _(sy, st=st, t=t, need_sil=need_sil):
                sy.wait_ge(sil.h, need_sil)
                sgp.issue(sy.dma_start(
                    out=c.sg1_d[st * 1024 + t * 128:st * 1024 + (t + 1) * 128, :],
                    in_=sg_t[:]))
        prev_sil_all = sil.v
        prev_yac_all = yac.v

    @block.sync
    def _(sy):
        sgp.wait_all(sy)


f1_state = {"w1_free": {}, "w2_free": {}, "y_free": {}}


def emit_t1(nc, block, c, sems, stack):
    mm = sems["t_mm"]; ev = sems["t_ev"]; vev = sems["t_vev"]
    wup = sems["t_wup"]; x2s = sems["t_x2"]; p0st = sems["p0_st"]
    vch = sems["ch"]["v"]
    xnp = pool(sems, nc, "t_xn", 2)
    x1p = pool(sems, nc, "t_x1", 2)
    sgp = pool(sems, nc, "t_sg", 2)
    outp = pool(sems, nc, "t_out", 1)
    thp = pool(sems, nc, "t_th", 1)
    wcp = pool(sems, nc, "t_wc", 1)

    thk_sb = stack.enter_context(nc.sbuf_tensor("t_thk", [128, NCH * D], BF16))
    thq_sb = stack.enter_context(nc.sbuf_tensor("t_thq", [128, NCH * D], BF16))
    thvn_sb = stack.enter_context(nc.sbuf_tensor("t_thvn", [128, NCH * D], BF16))
    wcur2 = stack.enter_context(nc.sbuf_tensor("t_wcur", [128, 2 * NCH * D], F32))
    wcurb2 = stack.enter_context(nc.sbuf_tensor("t_wcurb", [128, 2 * NCH * D], BF16))
    bcurf2 = stack.enter_context(nc.sbuf_tensor("t_bcurf", [1, 2 * D], F32))
    bcurb2 = stack.enter_context(nc.sbuf_tensor("t_bcurb", [1, 2 * D], BF16))
    xn1c = stack.enter_context(nc.sbuf_tensor("t_xn1c", [128, 2 * D], BF16))
    x1c = stack.enter_context(nc.sbuf_tensor("t_x1c", [128, 2 * D], F32))
    sgc = stack.enter_context(nc.sbuf_tensor("t_sgc", [128, 2 * D], F32))
    ktm = stack.enter_context(nc.sbuf_tensor("t_ktm", [128, 2 * D], BF16))
    qtm = stack.enter_context(nc.sbuf_tensor("t_qtm", [128, 2 * D], BF16))
    kfm = stack.enter_context(nc.sbuf_tensor("t_kfm", [128, 2 * D], BF16))
    qfm = stack.enter_context(nc.sbuf_tensor("t_qfm", [128, 2 * D], BF16))
    ghat = stack.enter_context(nc.sbuf_tensor("t_ghat", [128, D], BF16))
    gpb = stack.enter_context(nc.sbuf_tensor("t_gp", [128, D], BF16))
    br_sb = stack.enter_context(nc.sbuf_tensor("t_br", [128, 7 * 128], BF16))
    bc_sb = stack.enter_context(nc.sbuf_tensor("t_bc", [128, 7 * 128], BF16))
    mk_bf = stack.enter_context(nc.sbuf_tensor("t_mk", [128, 128], BF16))
    tmp128 = stack.enter_context(nc.sbuf_tensor("t_tmp128", [128, 128], F32))
    x2t = stack.enter_context(nc.sbuf_tensor("t_x2t", [128, D], F32))
    ccl2 = stack.enter_context(nc.sbuf_tensor("t_ccl2", [128, 1], F32))

    ps = stack.enter_context(nc.psum_tensor("t_ps", [128, 3584], F32))
    psb = stack.enter_context(nc.psum_tensor("t_psb", [128, 1024], BF16))

    st8 = {"A_mm": {}, "F_vev": {}, "A_done": {}}
    tcns = S(sems["_stack"].enter_context(nc.semaphore("t_cns")))

    @block.vector
    def _(v):
        v.memset(ccl2[:, 0:1], LAM2).then_inc(tcns.h, 1)
    tcns.add(1)

    @block.scalar
    def _(sc):
        sc.wait_ge(tcns.h, tcns.v)
        sc.wait_ge(p0st.h, p0st.v)
        thp.issue(sc.dma_start(out=thk_sb[:], in_=c.thk_t))
        thp.issue(sc.dma_start(out=thq_sb[:], in_=c.thq_t))
        thp.issue(sc.dma_start(out=thvn_sb[:], in_=c.thv_tn))

    def emit_front(gch):
        ch, s = divmod(gch, 2)[0], gch % 2
        ch = gch // 2
        slot = gch % 2
        gt = s * SLEN + ch * 128
        xc = slot * D
        ld_free_mm = st8["A_mm"].get(gch - 2, 0)

        @block.sync
        def _(sy, gch=gch, slot=slot, gt=gt, ld_free_mm=ld_free_mm):
            if ld_free_mm:
                sy.wait_ge(mm.h, ld_free_mm)
            xnp.issue(sy.dma_start(
                out=xn1c[:, slot * D:(slot + 1) * D].rearrange("p (c t) -> p c t", t=128),
                in_=c.xn1_d.rearrange("p (c t) -> p c t", c=NCH)[:, :, gt:gt + 128]))
            if st8.get("gate_done", {}).get(gch - 2, 0):
                sy.wait_ge(x2s.h, st8["gate_done"][gch - 2])
            x1p.issue(sy.dma_start(out=x1c[:, slot * D:(slot + 1) * D],
                                   in_=c.x1_d[gt:gt + 128, :]))
            sgp.issue(sy.dma_start(out=sgc[:, slot * D:(slot + 1) * D],
                                   in_=c.sg1_d[gt:gt + 128, :]))
        xn_slot = xnp.slot(xnp.issued - 1)
        st8.setdefault("xn_slot", {})[gch] = xn_slot
        st8.setdefault("x1_slot", {})[gch] = x1p.slot(x1p.issued - 1)
        st8.setdefault("sg_slot", {})[gch] = sgp.slot(sgp.issued - 1)
        prev_q_ev = max(st8["F_vev"].get(gch - 1, 0), st8.get("u45_vev", 0))

        @block.tensor
        def _(te, xc=xc, xn_slot=xn_slot, prev_q_ev=prev_q_ev, ld_free_mm=ld_free_mm):
            thp.wait_all(te)
            te.wait_ge(xn_slot[0].h, xn_slot[1])
            if prev_q_ev:
                te.wait_ge(vev.h, prev_q_ev)
            for half in range(2):
                for dc in range(8):
                    ins = te.matmul(
                        ps[:, 2048 + half * 512:2048 + half * 512 + 512],
                        xn1c[:, xc + dc * 128:xc + (dc + 1) * 128],
                        thk_sb[:, dc * D + half * 512:dc * D + half * 512 + 512],
                        start=(dc == 0), stop=(dc == 7))
                    if half == 1 and dc == 7:
                        ins.then_inc(mm.h, 1)
        mm.add(1)
        need_mm_k = mm.v

        @block.vector
        def _(v, slot=slot, need_mm_k=need_mm_k):
            v.wait_ge(mm.h, need_mm_k)
            v.tensor_copy(ktm[:, slot * D:(slot + 1) * D], ps[:, 2048:3072]).then_inc(vev.h, 1)
        vev.add(1)
        k_ev = vev.v

        @block.tensor
        def _(te, slot=slot, k_ev=k_ev):
            te.wait_ge(vev.h, k_ev)
            for dc in range(8):
                ins = te.transpose(psb[:, dc * 128:(dc + 1) * 128],
                                   ktm[:, slot * D + dc * 128:slot * D + (dc + 1) * 128],
                                   c.g.id_bf[:, :])
                if dc == 7:
                    ins.then_inc(mm.h, 1)
        mm.add(1)
        need_mm_ktr = mm.v

        @block.vector
        def _(v, slot=slot, need_mm_ktr=need_mm_ktr):
            v.wait_ge(mm.h, need_mm_ktr)
            v.tensor_copy(kfm[:, slot * D:(slot + 1) * D], psb[:]).then_inc(vev.h, 1)
        vev.add(1)
        kfm_ev = vev.v

        @block.tensor
        def _(te, xc=xc, k_ev=k_ev):
            te.wait_ge(vev.h, k_ev)
            for half in range(2):
                for dc in range(8):
                    ins = te.matmul(
                        ps[:, 2048 + half * 512:2048 + half * 512 + 512],
                        xn1c[:, xc + dc * 128:xc + (dc + 1) * 128],
                        thq_sb[:, dc * D + half * 512:dc * D + half * 512 + 512],
                        start=(dc == 0), stop=(dc == 7))
                    if half == 1 and dc == 7:
                        ins.then_inc(mm.h, 1)
        mm.add(1)
        need_mm_q = mm.v

        @block.vector
        def _(v, slot=slot, need_mm_q=need_mm_q):
            v.wait_ge(mm.h, need_mm_q)
            v.tensor_copy(qtm[:, slot * D:(slot + 1) * D], ps[:, 2048:3072]).then_inc(vev.h, 1)
        vev.add(1)
        q_ev = vev.v

        @block.tensor
        def _(te, slot=slot, q_ev=q_ev, kfm_ev=kfm_ev):
            te.wait_ge(vev.h, max(q_ev, kfm_ev))
            for dc in range(8):
                ins = te.transpose(psb[:, dc * 128:(dc + 1) * 128],
                                   qtm[:, slot * D + dc * 128:slot * D + (dc + 1) * 128],
                                   c.g.id_bf[:, :])
                if dc == 7:
                    ins.then_inc(mm.h, 1)
        mm.add(1)
        need_mm_qtr = mm.v

        @block.vector
        def _(v, slot=slot, need_mm_qtr=need_mm_qtr):
            v.wait_ge(mm.h, need_mm_qtr)
            v.tensor_copy(qfm[:, slot * D:(slot + 1) * D], psb[:]).then_inc(vev.h, 1)
        vev.add(1)
        st8["F_vev"][gch] = vev.v

    def emit_main(gch, next_front):
        s = gch % 2
        ch = gch // 2
        slot = gch % 2
        gt = s * SLEN + ch * 128
        xc = slot * D
        kf = kfm[:, slot * D:(slot + 1) * D]
        qf = qfm[:, slot * D:(slot + 1) * D]
        kt = ktm[:, slot * D:(slot + 1) * D]
        wcur = wcur2[:, s * NCH * D:(s + 1) * NCH * D]
        wcur_bf = wcurb2[:, s * NCH * D:(s + 1) * NCH * D]
        bcur_f = bcurf2[:, s * D:(s + 1) * D]
        bcur_b = bcurb2[:, s * D:(s + 1) * D]
        fv = st8["F_vev"][gch]
        prev_bc = st8.get("bcur_vev", 0)

        @block.tensor
        def _(te, kf=kf, qf=qf, fv=fv, prev_bc=prev_bc):
            te.wait_ge(vev.h, fv)
            if prev_bc:
                te.wait_ge(vev.h, prev_bc)
            for ac in range(8):
                ins = te.matmul(ps[:, 3072:3200], kf[:, ac * 128:(ac + 1) * 128],
                                kf[:, ac * 128:(ac + 1) * 128],
                                start=(ac == 0), stop=(ac == 7))
            for ac in range(8):
                ins = te.matmul(ps[:, 3200:3328], kf[:, ac * 128:(ac + 1) * 128],
                                qf[:, ac * 128:(ac + 1) * 128],
                                start=(ac == 0), stop=(ac == 7))
                if ac == 7:
                    ins.then_inc(mm.h, 1)
        mm.add(1)
        need_mm_s = mm.v

        @block.vector
        def _(v, need_mm_s=need_mm_s, vev_snap=vev.v):
            v.wait_ge(mm.h, need_mm_s)
            v.wait_ge(vev.h, vev_snap)
            vch.inc(v.tensor_scalar(tmp128[:], ps[:, 3072:3200], -LAM2, -LAM2,
                                    ALU.mult, ALU.add))
            vch.wait(v)
            vch.inc(v.tensor_tensor(br_sb[:, 0:128], tmp128[:], c.g.mask_sl[:], ALU.mult))
            vch.wait(v)
            vch.inc(v.tensor_tensor(bc_sb[:, 0:128], tmp128[:], c.g.mask_su[:], ALU.mult))
            vch.wait(v)
            vch.inc(v.tensor_scalar(tmp128[:], ps[:, 3200:3328], 1.0, None, ALU.add))
            vch.wait(v)
            v.tensor_tensor(mk_bf[:], tmp128[:], c.g.mask_iun[:],
                            ALU.mult).then_inc(vev.h, 1)
        vev.add(1)
        masks_done = vev.v

        need_wup = st8.get(("wup_done", s), 0)
        need_ev_prev = ev.v

        @block.tensor
        def _(te, xc=xc, kf=kf, need_wup=need_wup, need_ev_prev=need_ev_prev):
            te.wait_ge(wup.h, need_wup)
            if need_ev_prev:
                te.wait_ge(ev.h, need_ev_prev)
            for eh in range(2):
                first = True
                for ac in range(8):
                    te.matmul(ps[:, eh * 512:eh * 512 + 512],
                              kf[:, ac * 128:(ac + 1) * 128],
                              wcur_bf[:, ac * D + eh * 512:ac * D + eh * 512 + 512],
                              start=first, stop=False)
                    first = False
                te.matmul(ps[:, eh * 512:eh * 512 + 512], c.g.ones_row[0:1, :],
                          bcur_b[0:1, eh * 512:eh * 512 + 512], start=False, stop=False)
                for dc in range(8):
                    ins = te.matmul(
                        ps[:, eh * 512:eh * 512 + 512],
                        xn1c[:, xc + dc * 128:xc + (dc + 1) * 128],
                        thvn_sb[:, dc * D + eh * 512:dc * D + eh * 512 + 512],
                        start=False, stop=(dc == 7))
                    if eh == 1 and dc == 7:
                        ins.then_inc(mm.h, 1)
        mm.add(1)

        prev_u_ev = st8.get("u_vev", 0)

        @block.tensor
        def _(te, qf=qf, prev_u_ev=prev_u_ev):
            if prev_u_ev:
                te.wait_ge(vev.h, prev_u_ev)
            for eh in range(2):
                first = True
                for ac in range(8):
                    te.matmul(ps[:, 1024 + eh * 512:1024 + eh * 512 + 512],
                              qf[:, ac * 128:(ac + 1) * 128],
                              wcur_bf[:, ac * D + eh * 512:ac * D + eh * 512 + 512],
                              start=first, stop=False)
                    first = False
                ins = te.matmul(ps[:, 1024 + eh * 512:1024 + eh * 512 + 512],
                                c.g.ones_row[0:1, :],
                                bcur_b[0:1, eh * 512:eh * 512 + 512],
                                start=False, stop=True)
                if eh == 1:
                    ins.then_inc(mm.h, 1)
        mm.add(1)
        psc_done = mm.v

        app_done = psc_done
        b_done = [masks_done] + [0] * 6
        for L in range(7):
            @block.scalar
            def _(sc, L=L, app_done=app_done):
                sc.wait_ge(mm.h, app_done)
                sc.activation(ghat[:], ps[:, 0:1024], AF.Copy).then_inc(ev.h, 1)
            ev.add(1)
            need_ev_c = ev.v

            @block.tensor
            def _(te, L=L, need_ev_c=need_ev_c, need_b=b_done[L]):
                te.wait_ge(ev.h, need_ev_c)
                te.wait_ge(vev.h, need_b)
                for eh in range(2):
                    ins = te.matmul(ps[:, eh * 512:eh * 512 + 512],
                                    bc_sb[:, L * 128:(L + 1) * 128],
                                    ghat[:, eh * 512:eh * 512 + 512],
                                    start=False, stop=True, skip_group_check=True)
                    if eh == 1:
                        ins.then_inc(mm.h, 1)
            mm.add(1)
            app_done = mm.v

            if L < 6:
                @block.tensor
                def _(te, L=L, need_b=b_done[L]):
                    te.wait_ge(vev.h, need_b)
                    te.matmul(ps[:, 3328:3456], bc_sb[:, L * 128:(L + 1) * 128],
                              br_sb[:, L * 128:(L + 1) * 128], start=True, stop=True)
                    te.matmul(ps[:, 3456:3584], br_sb[:, L * 128:(L + 1) * 128],
                              bc_sb[:, L * 128:(L + 1) * 128],
                              start=True, stop=True).then_inc(mm.h, 1)
                mm.add(1)
                need_mm_sq = mm.v

                @block.vector
                def _(v, L=L, need_mm_sq=need_mm_sq):
                    v.wait_ge(mm.h, need_mm_sq)
                    v.tensor_copy(br_sb[:, (L + 1) * 128:(L + 2) * 128], ps[:, 3328:3456])
                    v.tensor_copy(bc_sb[:, (L + 1) * 128:(L + 2) * 128],
                                  ps[:, 3456:3584]).then_inc(vev.h, 1)
                vev.add(1)
                b_done[L + 1] = vev.v
        need_mm_chain = mm.v


        @block.scalar
        def _(sc, need_mm_chain=need_mm_chain):
            sc.wait_ge(mm.h, need_mm_chain)
            sc.activation(gpb[:], ps[:, 0:1024], AF.Copy,
                          scale=ccl2[:, 0:1]).then_inc(ev.h, 1)
        ev.add(1)
        gp_done = ev.v

        @block.tensor
        def _(te, gp_done=gp_done, masks_done=masks_done):
            te.wait_ge(ev.h, gp_done)
            for eh in range(2):
                ins = te.matmul(ps[:, 1024 + eh * 512:1024 + eh * 512 + 512],
                                mk_bf[:], gpb[:, eh * 512:eh * 512 + 512],
                                start=False, stop=True, skip_group_check=True)
                if eh == 1:
                    ins.then_inc(mm.h, 1)
        mm.add(1)
        need_mm_out = mm.v

        x1_slot = st8["x1_slot"][gch]
        sg_slot = st8["sg_slot"][gch]
        o_free = outp.slot(outp.issued - 1) if outp.issued >= 1 else None

        @block.vector
        def _(v, slot=slot, need_mm_out=need_mm_out, x1_slot=x1_slot,
              sg_slot=sg_slot, o_free=o_free):
            v.wait_ge(mm.h, need_mm_out)
            v.wait_ge(x1_slot[0].h, x1_slot[1])
            v.wait_ge(sg_slot[0].h, sg_slot[1])
            if o_free is not None:
                v.wait_ge(o_free[0].h, o_free[1])
            vch.wait(v)
            vch.inc(v.tensor_tensor(x2t[:], ps[:, 1024:2048],
                                    sgc[:, slot * D:(slot + 1) * D], ALU.mult))
            vch.wait(v)
            v.tensor_tensor(x2t[:], x2t[:],
                            x1c[:, slot * D:(slot + 1) * D], ALU.add).then_inc(x2s.h, 1)
        x2s.add(1)
        need_x2 = x2s.v

        @block.sync
        def _(sy, s=s, ch=ch, need_x2=need_x2):
            sy.wait_ge(x2s.h, need_x2)
            outp.issue(sy.dma_start(out=c.out[s, ch * 128:(ch + 1) * 128, :],
                                    in_=x2t[:]))

        f45 = st8["F_vev"].get(gch + 1, st8["F_vev"][gch])
        u_evac = []
        for ac in range(8):
            uslot = 1024 if ac % 2 == 0 else 2048
            need_vev_u = u_evac[ac - 2] if ac >= 2 else (f45 if ac == 1 else 0)

            @block.tensor
            def _(te, ac=ac, uslot=uslot, gp_done=gp_done, need_vev_u=need_vev_u,
                  need_x2=need_x2):
                if ac == 0:
                    te.wait_ge(x2s.h, need_x2)
                    te.wait_ge(ev.h, gp_done)
                if need_vev_u:
                    te.wait_ge(vev.h, need_vev_u)
                for half in range(2):
                    ins = te.matmul(ps[:, uslot + half * 512:uslot + half * 512 + 512],
                                    kt[:, ac * 128:(ac + 1) * 128],
                                    gpb[:, half * 512:half * 512 + 512],
                                    start=True, stop=True)
                    if half == 1:
                        ins.then_inc(mm.h, 1)
            mm.add(1)
            need_mm_u = mm.v

            @block.vector
            def _(v, ac=ac, uslot=uslot, need_mm_u=need_mm_u):
                v.wait_ge(mm.h, need_mm_u)
                v.tensor_tensor(wcur[:, ac * D:(ac + 1) * D],
                                wcur[:, ac * D:(ac + 1) * D],
                                ps[:, uslot:uslot + 1024], ALU.subtract).then_inc(vev.h, 1)
            vev.add(1)
            u_evac.append(vev.v)
            need_vev_w = vev.v

            @block.scalar
            def _(sc, ac=ac, need_vev_w=need_vev_w):
                sc.wait_ge(vev.h, need_vev_w)
                sc.activation(wcur_bf[:, ac * D:(ac + 1) * D],
                              wcur[:, ac * D:(ac + 1) * D], AF.Copy).then_inc(wup.h, 1)
            wup.add(1)
        st8["u_vev"] = vev.v
        st8["u45_vev"] = u_evac[-1]

        bank6_free_e = ev.v

        for half in range(2):
            @block.tensor
            def _(te, half=half, gp_done=gp_done, masks_done=masks_done,
                  bank6_free_e=bank6_free_e, vev_snap=vev.v):
                te.wait_ge(ev.h, bank6_free_e)
                te.wait_ge(vev.h, vev_snap if half == 1 else masks_done)
                te.matmul(ps[0:1, 3072:3584], c.g.ones_col[:, :],
                          gpb[:, half * 512:half * 512 + 512],
                          start=True, stop=True).then_inc(mm.h, 1)
            mm.add(1)
            need_mm_bc = mm.v

            @block.vector
            def _(v, half=half, need_mm_bc=need_mm_bc):
                v.wait_ge(mm.h, need_mm_bc)
                v.tensor_tensor(bcur_f[0:1, half * 512:half * 512 + 512],
                                bcur_f[0:1, half * 512:half * 512 + 512],
                                ps[0:1, 3072:3584], ALU.subtract).then_inc(vev.h, 1)
            vev.add(1)
        st8["bcur_vev"] = vev.v
        need_vev_bc = vev.v

        @block.scalar
        def _(sc, need_vev_bc=need_vev_bc, bcur_f=bcur_f, bcur_b=bcur_b):
            sc.wait_ge(vev.h, need_vev_bc)
            sc.activation(bcur_b[:], bcur_f[:], AF.Copy).then_inc(wup.h, 1)
        wup.add(1)

        st8[("wup_done", s)] = wup.v
        st8["A_mm"][gch] = mm.v
        st8.setdefault("gate_done", {})[gch] = x2s.v

        if next_front is not None:
            emit_front(next_front)

    for s in range(SEQ):
        @block.sync
        def _(sy, s=s):
            wcp.wait_all(sy)
            wcp.issue(sy.dma_start(out=wcur2[:, s * NCH * D:(s + 1) * NCH * D],
                                   in_=c.w0_t))
            wcp.issue(sy.dma_start(out=bcurf2[:, s * D:(s + 1) * D], in_=c.b0))
        w_slot = wcp.slot(wcp.issued - 1)

        @block.scalar
        def _(sc, s=s, w_slot=w_slot):
            sc.wait_ge(w_slot[0].h, w_slot[1])
            for dc in range(8):
                sc.activation(wcurb2[:, s * NCH * D + dc * D:s * NCH * D + (dc + 1) * D],
                              wcur2[:, s * NCH * D + dc * D:s * NCH * D + (dc + 1) * D],
                              AF.Copy)
            sc.activation(bcurb2[:, s * D:(s + 1) * D],
                          bcurf2[:, s * D:(s + 1) * D], AF.Copy).then_inc(wup.h, 1)
        wup.add(1)
        st8[("wup_done", s)] = wup.v

    emit_front(0)
    emit_front(1)
    for gch in range(2 * NT_SEQ):
        nxt = gch + 2 if gch + 2 < 2 * NT_SEQ else None
        emit_main(gch, nxt)

    @block.sync
    def _(sy):
        outp.wait_all(sy)


t1_state = {"xn_done": {}, "gate_done": {}, "tm_free": {}}


def build_program(phases=ALL_PHASES, dbg=()):
    nc = bass.Bass("TRN2", target_bir_lowering=False)
    c = declare_io(nc, dbg)
    with contextlib.ExitStack() as stack:
        sems = {"_stack": stack}
        for name in [
            "ph", "wcast0", "wcast1",
            "p0_ld", "p0_cast", "p0_tr", "p0_ev", "p0_st",
            "a_wld", "a_xld", "a_sq", "a_col", "a_rstd", "a_xn", "a_tr",
            "a_xnf", "a_xnst", "a_mm", "a_kev", "a_vev", "a_kvev", "a_kvst",
            "f_mm", "f_gps", "f_gs2", "f_wtv", "f_b2", "f_hev",
            "f_yac", "f_sil",
            "b_mm", "b_qev", "b_zev", "b_aev", "b_x1",
            "t_mm", "t_ev", "t_vev", "t_wup", "t_x2",
        ]:
            sems[name] = S(stack.enter_context(nc.semaphore(name)))

        sems["ch"] = {
            "v": Chain(stack.enter_context(nc.semaphore("chv"))),
            "s": Chain(stack.enter_context(nc.semaphore("chs"))),
            "p": Chain(stack.enter_context(nc.semaphore("chp"))),
            "t": Chain(stack.enter_context(nc.semaphore("cht"))),
        }
        c.g = Ctx()
        c.g.id_bf = stack.enter_context(nc.sbuf_tensor("g_id_bf", [128, 128], BF16))
        c.g.id_f32 = stack.enter_context(nc.sbuf_tensor("g_id_f32", [128, 128], F32))
        c.g.ones_row = stack.enter_context(nc.sbuf_tensor("g_ones_row", [1, 128], BF16))
        c.g.ones_col = stack.enter_context(nc.sbuf_tensor("g_ones_col", [128, 1], BF16))
        c.g.mask_sl = stack.enter_context(nc.sbuf_tensor("g_mask_sl", [128, 128], F32))
        c.g.mask_su = stack.enter_context(nc.sbuf_tensor("g_mask_su", [128, 128], F32))
        c.g.mask_iun = stack.enter_context(nc.sbuf_tensor("g_mask_iun", [128, 128], F32))
        c.g.ksum_bf = stack.enter_context(nc.sbuf_tensor("g_ksum", [128, 16], BF16))

        block = stack.enter_context(nc.Block())

        cld = S(stack.enter_context(nc.semaphore("cld")))

        @block.sync
        def _(s):
            for dst, src in [(c.g.id_bf, c.id_bf), (c.g.id_f32, c.id_f32),
                             (c.g.ones_row, c.ones_row), (c.g.ones_col, c.ones_col),
                             (c.g.mask_sl, c.mask_sl), (c.g.mask_su, c.mask_su),
                             (c.g.mask_iun, c.mask_iun)]:
                s.dma_start(out=dst[:], in_=src).then_inc(cld.h, 16)
                cld.add(16)
        for name in ("sync", "tensor", "vector", "scalar", "gpsimd"):
            sec = getattr(block, name)

            @sec
            def _(eng):
                eng.wait_ge(cld.h, cld.v)

        for ph_name in phases:
            with contextlib.ExitStack() as pst:
                EMITTERS[ph_name](nc, block, c, sems, pst)
                barrier(nc, block, sems["ph"])
    return nc, c


def make_in_maps(inputs):
    consts = host_consts()
    x_full = np.ascontiguousarray(inputs["L0_attn_x"])
    base = {
        "w_qkv": inputs["L0_attn_W_qkv"], "w_out": inputs["L0_attn_W_out"],
        "b_out": np.ascontiguousarray(inputs["L0_attn_b_out"]).reshape(1, D),
        "gate_w": inputs["L0_ffn_gate_w"],
        "m0w1": inputs["L0_ffn_w1"], "m0b1": inputs["L0_ffn_b1"],
        "m0w2": inputs["L0_ffn_w2"], "m0b2": inputs["L0_ffn_b2"],
        "thk": inputs["L1_attn_theta_K"], "thv": inputs["L1_attn_theta_V"],
        "thq": inputs["L1_attn_theta_Q"], "w0": inputs["L1_attn_W0"],
        "b0": np.ascontiguousarray(inputs["L1_attn_b0"]).reshape(1, D),
        "wg": inputs["L1_ffn_Wg"],
        "m1w1": inputs["L1_ffn_W1"], "m1b1": inputs["L1_ffn_b1"],
        "m1w2": inputs["L1_ffn_W2"], "m1b2": inputs["L1_ffn_b2"],
    }
    base.update(consts)
    in_maps = []
    for cid in range(NCORE):
        m = dict(base)
        m["x"] = x_full[cid * SEQ:(cid + 1) * SEQ]
        in_maps.append(m)
    return in_maps


def kernel(**inputs):
    nc, _ = build_program()
    in_maps = make_in_maps(inputs)
    res = run_bass_kernel_spmd(nc, in_maps, core_ids=list(range(NCORE)))
    outs = [np.asarray(res.results[cid]["out"], dtype=np.float32) for cid in range(NCORE)]
    return np.concatenate(outs, axis=0)


EMITTERS = {"p0": emit_p0, "a0": emit_a0, "f0": emit_f0, "b0": emit_b0,
            "f1": emit_f1, "t1": emit_t1}


# version 19
# speedup vs baseline: 1.1739x; 1.1739x over previous
import contextlib
import numpy as np
import ml_dtypes
import concourse.bass as bass
import concourse.mybir as mybir
from concourse.bass_utils import run_bass_kernel_spmd

F32 = mybir.dt.float32
BF16 = mybir.dt.bfloat16
AF = mybir.ActivationFunctionType
ALU = mybir.AluOpType
AX = mybir.AxisListType

NCORE = 8
SEQ = 2
SLEN = 2048
D = 1024
NCH = 8
TOK = SEQ * SLEN
NT_SEQ = SLEN // 128
LN_EPS = 1e-5
ATT_EPS = 1e-6
MOE_EPS = 1e-9
LR = 0.01
LAM2 = float(np.float32(np.float32(2.0 / D) * np.float32(LR)))
INV_D = 1.0 / D

ALL_PHASES = ("p0", "a0", "f0", "b0", "f1", "t1")


class S:

    def __init__(self, h):
        self.h = h
        self.v = 0

    def add(self, n=1):
        self.v += n
        return self.v


class Ctx:
    pass


class DPool:

    def __init__(self, nc, stack, name, n):
        self.sems = [S(stack.enter_context(nc.semaphore(f"{name}{i}")))
                     for i in range(n)]
        self.n = n
        self.issued = 0

    def issue(self, ins):
        s = self.sems[self.issued % self.n]
        ins.then_inc(s.h, 16)
        s.add(16)
        self.issued += 1
        return self.issued - 1

    def slot(self, j):
        return self.sems[j % self.n], 16 * (j // self.n + 1)

    def wait(self, eng, j):
        s, v = self.slot(j)
        eng.wait_ge(s.h, v)

    def wait_all(self, eng):
        for s in self.sems:
            if s.v:
                eng.wait_ge(s.h, s.v)


def pool(sems, nc, name, n):
    key = "pool_" + name
    if key not in sems:
        sems[key] = DPool(nc, sems["_stack"], name, n)
    p = sems[key]
    assert p.n == n, (name, p.n, n)
    return p


class Chain:

    def __init__(self, h):
        self.s = S(h)

    def inc(self, ins):
        ins.then_inc(self.s.h, 1)
        self.s.add(1)

    def wait(self, eng):
        if self.s.v:
            eng.wait_ge(self.s.h, self.s.v)


def host_consts():
    i = np.arange(128)
    r, c = i[:, None], i[None, :]
    return {
        "id_bf": np.eye(128, dtype=ml_dtypes.bfloat16),
        "id_f32": np.eye(128, dtype=np.float32),
        "ones_row": np.ones((1, 128), dtype=ml_dtypes.bfloat16),
        "ones_col": np.ones((128, 1), dtype=ml_dtypes.bfloat16),
        "mask_sl": (c < r).astype(np.float32),
        "mask_su": (c > r).astype(np.float32),
        "mask_iun": -(r <= c).astype(np.float32),
    }


def declare_io(nc, dbg):
    c = Ctx()
    dt_ = lambda name, shape, dt, kind: nc.dram_tensor(name, shape, dt, kind=kind).ap()
    c.x = dt_("x", [SEQ, SLEN, D], F32, "ExternalInput")
    c.w_qkv = dt_("w_qkv", [3 * D, D], F32, "ExternalInput")
    c.w_out = dt_("w_out", [D, D], F32, "ExternalInput")
    c.b_out = dt_("b_out", [1, D], F32, "ExternalInput")
    c.gate_w = dt_("gate_w", [D, 8], F32, "ExternalInput")
    c.m0w1 = dt_("m0w1", [8, D, 512], F32, "ExternalInput")
    c.m0b1 = dt_("m0b1", [8, 512], F32, "ExternalInput")
    c.m0w2 = dt_("m0w2", [8, 512, D], F32, "ExternalInput")
    c.m0b2 = dt_("m0b2", [8, D], F32, "ExternalInput")
    c.thk = dt_("thk", [D, D], F32, "ExternalInput")
    c.thv = dt_("thv", [D, D], F32, "ExternalInput")
    c.thq = dt_("thq", [D, D], F32, "ExternalInput")
    c.w0 = dt_("w0", [D, D], F32, "ExternalInput")
    c.b0 = dt_("b0", [1, D], F32, "ExternalInput")
    c.wg = dt_("wg", [D, 8], F32, "ExternalInput")
    c.m1w1 = dt_("m1w1", [8, D, 4096], F32, "ExternalInput")
    c.m1b1 = dt_("m1b1", [8, 4096], F32, "ExternalInput")
    c.m1w2 = dt_("m1w2", [8, 4096, D], F32, "ExternalInput")
    c.m1b2 = dt_("m1b2", [8, D], F32, "ExternalInput")
    c.id_bf = dt_("id_bf", [128, 128], BF16, "ExternalInput")
    c.id_f32 = dt_("id_f32", [128, 128], F32, "ExternalInput")
    c.ones_row = dt_("ones_row", [1, 128], BF16, "ExternalInput")
    c.ones_col = dt_("ones_col", [128, 1], BF16, "ExternalInput")
    c.mask_sl = dt_("mask_sl", [128, 128], F32, "ExternalInput")
    c.mask_su = dt_("mask_su", [128, 128], F32, "ExternalInput")
    c.mask_iun = dt_("mask_iun", [128, 128], F32, "ExternalInput")
    c.out = dt_("out", [SEQ, SLEN, D], F32, "ExternalOutput")

    def scr(name, shape, dt):
        kind = "ExternalOutput" if (name in dbg) else "Internal"
        return dt_(name, shape, dt, kind)

    c.wq_t = scr("wq_t", [128, NCH * D], BF16)
    c.wk_t = scr("wk_t", [128, NCH * D], BF16)
    c.wv_t = scr("wv_t", [128, NCH * D], BF16)
    c.wout_t = scr("wout_t", [128, NCH * D], BF16)
    c.thk_t = scr("thk_t", [128, NCH * D], BF16)
    c.thq_t = scr("thq_t", [128, NCH * D], BF16)
    c.thv_tn = scr("thv_tn", [128, NCH * D], BF16)
    c.w0_t = scr("w0_t", [128, NCH * D], F32)
    c.m0w1_b = scr("m0w1_b", [8, 128, 8 * 512], BF16)
    c.m0w2_b = scr("m0w2_b", [8, 128, 4 * 1024], BF16)
    c.m1w1_b = scr("m1w1_b", [8, 128, 32 * 1024], BF16)
    c.m1w2_b = scr("m1w2_b", [8, 128, 32 * 1024], BF16)
    c.xn_d = scr("xn_d", [128, NCH * TOK], BF16)
    c.xn1_d = scr("xn1_d", [128, NCH * TOK], BF16)
    c.x1_d = scr("x1_d", [TOK, D], F32)
    c.sg0_d = scr("sg0_d", [TOK, D], F32)
    c.sg1_d = scr("sg1_d", [TOK, D], F32)
    c.kv_d = scr("kv_d", [SEQ, 128, NCH * D], BF16)
    return c


def barrier(nc, block, ph):
    engines = ("sync", "tensor", "vector", "scalar")
    ph.add(len(engines))
    target = ph.v
    for name in engines:
        sec = getattr(block, name)

        @sec
        def _(eng, target=target):
            eng.nop().then_inc(ph.h, 1)
            eng.wait_ge(ph.h, target)


def emit_p0(nc, block, c, sems, stack):
    cast, tr, ev, st = (sems[k] for k in
                        ("p0_cast", "p0_tr", "p0_ev", "p0_st"))
    ldp = DPool(nc, sems["_stack"], "p0ld", 2)

    wcast0, wcast1 = sems["wcast0"], sems["wcast1"]

    @block.gpsimd
    def _(g):
        for e in range(8):
            g.dma_start(out=c.m0w1_b[e].rearrange("p (c f) -> p c f", c=8),
                        in_=c.m0w1[e].rearrange("(c p) f -> p c f", p=128)).then_inc(wcast0.h, 16)
            g.dma_start(out=c.m0w2_b[e].rearrange("p (c d) -> p c d", c=4),
                        in_=c.m0w2[e].rearrange("(c p) d -> p c d", p=128)).then_inc(wcast0.h, 16)
            wcast0.add(32)
        for e in range(8):
            for ch in range(8):
                g.dma_start(
                    out=c.m1w1_b[e].rearrange("p (fs blk) -> p fs blk", fs=32)
                        [:, :, ch * 128:(ch + 1) * 128],
                    in_=c.m1w1[e][ch * 128:(ch + 1) * 128, :]
                        .rearrange("p (fs fi) -> p fs fi", fi=128),
                ).then_inc(wcast1.h, 16)
            g.dma_start(out=c.m1w2_b[e].rearrange("p (c d) -> p c d", c=32),
                        in_=c.m1w2[e].rearrange("(c p) d -> p c d", p=128)).then_inc(wcast1.h, 16)
            wcast1.add(16 * 9)

    nat_f = stack.enter_context(nc.sbuf_tensor("p0_nat_f", [128, 2 * D], F32))
    nat_b = stack.enter_context(nc.sbuf_tensor("p0_nat_b", [128, 2 * D], BF16))
    stage_b = stack.enter_context(nc.sbuf_tensor("p0_stage_b", [128, NCH * D], BF16))
    stage_f = stack.enter_context(nc.sbuf_tensor("p0_stage_f", [128, NCH * D], F32))
    ccm1 = stack.enter_context(nc.sbuf_tensor("p0_ccm1", [128, 1], F32))
    ps_t = stack.enter_context(nc.psum_tensor("p0_ps_b", [128, 1024], BF16))
    ps_f = stack.enter_context(nc.psum_tensor("p0_ps_f", [128, 1024], F32))

    cns = S(sems["_stack"].enter_context(nc.semaphore("p0_cns")))

    @block.vector
    def _(v):
        v.memset(ccm1[:, 0:1], -1.0).then_inc(cns.h, 1)
    cns.add(1)

    jobs = [
        (c.w_qkv[0 * D:1 * D, :], c.wq_t, False, False),
        (c.w_qkv[1 * D:2 * D, :], c.wk_t, False, False),
        (c.w_qkv[2 * D:3 * D, :], c.wv_t, False, False),
        (c.w_out, c.wout_t, False, False),
        (c.thk, c.thk_t, False, False),
        (c.thv, c.thv_tn, True, False),
        (c.thq, c.thq_t, False, False),
        (c.w0, c.w0_t, False, True),
    ]
    consumed = []
    for j, (src, dst, neg, isf32) in enumerate(jobs):
        st_before = st.v
        for ec in range(8):
            step = j * 8 + ec
            buf = ec % 2

            @block.sync
            def _(s, src=src, ec=ec, buf=buf, step=step):
                if step >= 2:
                    sem, val = consumed[step - 2]
                    s.wait_ge(sem.h, val)
                ldp.issue(s.dma_start(out=nat_f[:, buf * D:(buf + 1) * D],
                                      in_=src[ec * 128:(ec + 1) * 128, :]))
            ld_slot = ldp.slot(step)

            if not isf32:
                need_tr_prev = tr.v - 1 if step >= 2 else 0

                @block.scalar
                def _(sc, buf=buf, neg=neg, ld_slot=ld_slot, need_tr_prev=need_tr_prev):
                    sc.wait_ge(ld_slot[0].h, ld_slot[1])
                    if need_tr_prev:
                        sc.wait_ge(tr.h, need_tr_prev)
                    sc.wait_ge(cns.h, 1)
                    if neg:
                        sc.activation(nat_b[:, buf * D:(buf + 1) * D],
                                      nat_f[:, buf * D:(buf + 1) * D],
                                      AF.Copy, scale=ccm1[:, 0:1]).then_inc(cast.h, 1)
                    else:
                        sc.activation(nat_b[:, buf * D:(buf + 1) * D],
                                      nat_f[:, buf * D:(buf + 1) * D],
                                      AF.Copy).then_inc(cast.h, 1)
                cast.add(1)
                consumed.append((cast, cast.v))
                need_cast = cast.v
                need_ev_prev = ev.v

                @block.tensor
                def _(te, need_cast=need_cast, need_ev_prev=need_ev_prev, buf=buf):
                    te.wait_ge(cast.h, need_cast)
                    if need_ev_prev:
                        te.wait_ge(ev.h, need_ev_prev)
                    for dc in range(8):
                        ins = te.transpose(ps_t[:, dc * 128:(dc + 1) * 128],
                                           nat_b[:, buf * D + dc * 128:buf * D + (dc + 1) * 128],
                                           c.g.id_bf[:, :])
                        if dc == 7:
                            ins.then_inc(tr.h, 1)
                tr.add(1)
                need_tr = tr.v
                need_st_prev = st_before

                @block.vector
                def _(v, need_tr=need_tr, ec=ec, need_st_prev=need_st_prev):
                    v.wait_ge(tr.h, need_tr)
                    if need_st_prev:
                        v.wait_ge(st.h, need_st_prev)
                    for dc in range(8):
                        ins = v.tensor_copy(stage_b[:, dc * D + ec * 128:dc * D + (ec + 1) * 128],
                                            ps_t[:, dc * 128:(dc + 1) * 128])
                        if dc == 7:
                            ins.then_inc(ev.h, 1)
                ev.add(1)
            else:
                need_ev_prev = ev.v

                @block.tensor
                def _(te, ld_slot=ld_slot, need_ev_prev=need_ev_prev, buf=buf):
                    te.wait_ge(ld_slot[0].h, ld_slot[1])
                    if need_ev_prev:
                        te.wait_ge(ev.h, need_ev_prev)
                    for dc in range(8):
                        ins = te.transpose(ps_f[:, dc * 128:(dc + 1) * 128],
                                           nat_f[:, buf * D + dc * 128:buf * D + (dc + 1) * 128],
                                           c.g.id_f32[:, :])
                        if dc == 7:
                            ins.then_inc(tr.h, 1)
                tr.add(1)
                consumed.append((tr, tr.v))
                need_tr = tr.v
                need_st_prev = st_before

                @block.vector
                def _(v, need_tr=need_tr, ec=ec, need_st_prev=need_st_prev):
                    v.wait_ge(tr.h, need_tr)
                    if need_st_prev:
                        v.wait_ge(st.h, need_st_prev)
                    for dc in range(8):
                        ins = v.tensor_copy(stage_f[:, dc * D + ec * 128:dc * D + (ec + 1) * 128],
                                            ps_f[:, dc * 128:(dc + 1) * 128])
                        if dc == 7:
                            ins.then_inc(ev.h, 1)
                ev.add(1)

        need_ev = ev.v

        @block.sync
        def _(s, dst=dst, isf32=isf32, need_ev=need_ev):
            s.wait_ge(ev.h, need_ev)
            s.dma_start(out=dst, in_=stage_f[:] if isf32 else stage_b[:]).then_inc(st.h, 16)
        st.add(16)

    @block.sync
    def _(s):
        s.wait_ge(st.h, st.v)


def emit_ln(block, ch, bufs, xt_ap, out_ap, wait_x, s_sq, s_col, s_rstd, s_out, cc_m05,
            apply_wait=None):
    sqdump, cols = bufs
    wsem, wval = wait_x

    vch, sch = ch["v"], ch["s"]

    @block.scalar
    def _(sc, wval=wval):
        sc.wait_ge(wsem.h, wval)
        sc.activation(sqdump[:], xt_ap, AF.Square, accum_out=cols[:, 0:1]).then_inc(s_sq.h, 1)
    s_sq.add(1)
    need_sq = s_sq.v

    @block.vector
    def _(v, wval=wval, need_sq=need_sq):
        v.wait_ge(wsem.h, wval)
        vch.inc(v.reduce_sum(cols[:, 1:2], xt_ap, axis=AX.X))
        v.wait_ge(s_sq.h, need_sq)
        vch.inc(v.tensor_scalar(cols[:, 5:6], cols[:, 0:1], INV_D, None, ALU.mult))
        vch.wait(v)
        vch.inc(v.tensor_scalar(cols[:, 3:4], cols[:, 1:2], INV_D, None, ALU.mult))
        vch.wait(v)
        vch.inc(v.tensor_tensor(cols[:, 4:5], cols[:, 3:4], cols[:, 3:4], ALU.mult))
        vch.wait(v)
        v.scalar_tensor_tensor(cols[:, 6:7], cols[:, 5:6], LN_EPS, cols[:, 4:5],
                               ALU.add, ALU.subtract).then_inc(s_col.h, 1)
    s_col.add(1)
    need_col = s_col.v

    @block.scalar
    def _(sc, need_col=need_col):
        sc.wait_ge(s_col.h, need_col)
        sch.inc(sc.activation(cols[:, 7:8], cols[:, 6:7], AF.Ln))
        sch.wait(sc)
        sc.activation(cols[:, 7:8], cols[:, 7:8], AF.Exp, scale=cc_m05[:, 0:1]).then_inc(s_rstd.h, 1)
    s_rstd.add(1)
    need_rstd = s_rstd.v

    @block.vector
    def _(v, need_rstd=need_rstd, apply_wait=apply_wait):
        v.wait_ge(s_rstd.h, need_rstd)
        if apply_wait is not None and apply_wait[1] > 0:
            v.wait_ge(apply_wait[0].h, apply_wait[1])
        v.tensor_scalar(out_ap, xt_ap, cols[:, 3:4], cols[:, 7:8],
                        ALU.subtract, ALU.mult).then_inc(s_out.h, 1)
    s_out.add(1)


def emit_a0(nc, block, c, sems, stack):
    wld = sems["a_wld"]; sq = sems["a_sq"]; col = sems["a_col"]
    rstd = sems["a_rstd"]; xn = sems["a_xn"]; tr = sems["a_tr"]; xnf = sems["a_xnf"]
    mm = sems["a_mm"]; kev = sems["a_kev"]; vev = sems["a_vev"]
    kvev = sems["a_kvev"]; kvst = sems["a_kvst"]; p0st = sems["p0_st"]
    xldp = pool(sems, nc, "a_xld4", 4)
    xnstp = pool(sems, nc, "a_xnst", 2)

    wk_sb = stack.enter_context(nc.sbuf_tensor("a_wk", [128, NCH * D], BF16))
    wv_sb = stack.enter_context(nc.sbuf_tensor("a_wv", [128, NCH * D], BF16))
    k_sb = stack.enter_context(nc.sbuf_tensor("a_k", [128, NT_SEQ * D], BF16))
    v_sb = stack.enter_context(nc.sbuf_tensor("a_v", [128, NT_SEQ * D], BF16))
    xt = stack.enter_context(nc.sbuf_tensor("a_xt", [128, 4 * D], F32))
    sqd = stack.enter_context(nc.sbuf_tensor("a_sqd", [128, D], BF16))
    cols = stack.enter_context(nc.sbuf_tensor("a_cols", [128, 8], F32))
    xn_tm = stack.enter_context(nc.sbuf_tensor("a_xn_tm", [128, 2 * D], BF16))
    xnf_sb = stack.enter_context(nc.sbuf_tensor("a_xnf", [128, 2 * D], BF16))
    kvs = stack.enter_context(nc.sbuf_tensor("a_kvs", [128, NCH * D], BF16))
    cc = stack.enter_context(nc.sbuf_tensor("a_cc", [128, 1], F32))
    ksum_bf = c.g.ksum_bf

    ps_tr = stack.enter_context(nc.psum_tensor("a_ps_tr", [128, 1024], BF16))
    ps_k = stack.enter_context(nc.psum_tensor("a_ps_k", [128, 512], F32))
    ps_v = stack.enter_context(nc.psum_tensor("a_ps_v", [128, 512], F32))
    ps_ks = stack.enter_context(nc.psum_tensor("a_ps_ks", [128, 8], F32))
    ps_kv = stack.enter_context(nc.psum_tensor("a_ps_kv", [128, 2 * 1024], F32))

    cns = S(sems["_stack"].enter_context(nc.semaphore("a_cns")))

    @block.vector
    def _(v):
        v.memset(cc[:, 0:1], -0.5)
        v.memset(cols[:, 2:3], 0.0).then_inc(cns.h, 1)
    cns.add(1)

    @block.scalar
    def _(sc):
        sc.wait_ge(cns.h, cns.v)
        sc.wait_ge(p0st.h, 48)
        sc.dma_start(out=wk_sb[:], in_=c.wk_t).then_inc(wld.h, 16)
        sc.dma_start(out=wv_sb[:], in_=c.wv_t).then_inc(wld.h, 16)
    wld.add(32)
    need_wld = wld.v

    proj_mm_done = {}
    for s in range(SEQ):
        if s == 1:
            mm_after_kv = mm.v
            kvst_after = kvst.v

            @block.vector
            def _(v, mm_after_kv=mm_after_kv):
                v.wait_ge(mm.h, mm_after_kv)

            @block.scalar
            def _(sc, mm_after_kv=mm_after_kv, kvst_after=kvst_after):
                sc.wait_ge(mm.h, mm_after_kv)
                sc.wait_ge(kvst.h, kvst_after)
        for i in range(NT_SEQ):
            gi = s * NT_SEQ + i
            buf = gi % 2
            xbuf = gi % 4
            xt_ap = xt[:, xbuf * D:(xbuf + 1) * D]
            free_val = xn.v - 3 if gi >= 4 else 0

            @block.sync
            def _(sy, s=s, i=i, xt_ap=xt_ap, free_val=free_val):
                if free_val:
                    sy.wait_ge(xn.h, free_val)
                xldp.issue(sy.dma_start(out=xt_ap, in_=c.x[s, i * 128:(i + 1) * 128, :]))

            xn_tm_ap = xn_tm[:, buf * D:(buf + 1) * D]
            xsl = xldp.slot(xldp.issued - 1)
            emit_ln(block, sems["ch"], (sqd, cols), xt_ap, xn_tm_ap, (xsl[0], xsl[1]),
                    sq, col, rstd, xn, cc,
                    apply_wait=(tr, tr.v - 1 if gi >= 2 else 0))
            need_xn = xn.v

            need_xnf_prev = xnf.v

            @block.tensor
            def _(te, xn_tm_ap=xn_tm_ap, need_xn=need_xn, need_xnf_prev=need_xnf_prev):
                te.wait_ge(xn.h, need_xn)
                if need_xnf_prev:
                    te.wait_ge(xnf.h, need_xnf_prev)
                for dc in range(8):
                    ins = te.transpose(ps_tr[:, dc * 128:(dc + 1) * 128],
                                       xn_tm_ap[:, dc * 128:(dc + 1) * 128], c.g.id_bf[:, :])
                    if dc == 7:
                        ins.then_inc(tr.h, 1)
            tr.add(1)
            need_tr = tr.v
            st_slot = xnstp.slot(xnstp.issued - 2) if gi >= 2 else None
            need_projmm = proj_mm_done.get(gi - 2, 0)

            @block.vector
            def _(v, need_tr=need_tr, buf=buf, st_slot=st_slot, need_projmm=need_projmm):
                v.wait_ge(tr.h, need_tr)
                if st_slot is not None:
                    v.wait_ge(st_slot[0].h, st_slot[1])
                if need_projmm:
                    v.wait_ge(mm.h, need_projmm)
                v.tensor_copy(xnf_sb[:, buf * D:(buf + 1) * D], ps_tr[:]).then_inc(xnf.h, 1)
            xnf.add(1)
            need_xnf = xnf.v

            @block.sync
            def _(sy, s=s, i=i, buf=buf, need_xnf=need_xnf):
                sy.wait_ge(xnf.h, need_xnf)
                xnstp.issue(sy.dma_start(
                    out=c.xn_d.rearrange("p (c t) -> p c t", c=NCH)[:, :, s * SLEN + i * 128:
                                                                   s * SLEN + (i + 1) * 128],
                    in_=xnf_sb[:, buf * D:(buf + 1) * D].rearrange("p (c t) -> p c t", t=128),
                ))

            xnf_ap = xnf_sb[:, buf * D:(buf + 1) * D]
            for half in range(2):
                need_kev_prev = kev.v
                need_vev_prev = vev.v

                @block.tensor
                def _(te, xnf_ap=xnf_ap, half=half, need_xnf=need_xnf,
                      need_kev_prev=need_kev_prev, need_vev_prev=need_vev_prev):
                    te.wait_ge(xnf.h, need_xnf)
                    te.wait_ge(wld.h, need_wld)
                    if need_kev_prev:
                        te.wait_ge(kev.h, need_kev_prev)
                    for dc in range(8):
                        ins = te.matmul(ps_k[:, :],
                                        xnf_ap[:, dc * 128:(dc + 1) * 128],
                                        wk_sb[:, dc * D + half * 512:dc * D + half * 512 + 512],
                                        start=(dc == 0), stop=(dc == 7))
                        if dc == 7:
                            ins.then_inc(mm.h, 1)
                    if need_vev_prev:
                        te.wait_ge(vev.h, need_vev_prev)
                    for dc in range(8):
                        ins = te.matmul(ps_v[:, :],
                                        xnf_ap[:, dc * 128:(dc + 1) * 128],
                                        wv_sb[:, dc * D + half * 512:dc * D + half * 512 + 512],
                                        start=(dc == 0), stop=(dc == 7))
                        if dc == 7:
                            ins.then_inc(mm.h, 1)
                mm.add(2)
                need_mm = mm.v

                @block.vector
                def _(v, i=i, half=half, need_mm=need_mm):
                    v.wait_ge(mm.h, need_mm - 1)
                    v.tensor_scalar(k_sb[:, i * D + half * 512:i * D + half * 512 + 512],
                                    ps_k[:, :], 0.0, ATT_EPS,
                                    ALU.max, ALU.add).then_inc(kev.h, 1)
                kev.add(1)

                @block.scalar
                def _(sc, i=i, half=half, need_mm=need_mm):
                    sc.wait_ge(mm.h, need_mm)
                    sc.activation(v_sb[:, i * D + half * 512:i * D + half * 512 + 512],
                                  ps_v[:, :], AF.Copy).then_inc(vev.h, 1)
                vev.add(1)
            proj_mm_done[gi] = mm.v

        need_kev_all = kev.v
        need_vev_all = vev.v

        @block.tensor
        def _(te, need_kev_all=need_kev_all):
            te.wait_ge(kev.h, need_kev_all)
            for ac in range(8):
                for i in range(NT_SEQ):
                    ins = te.matmul(ps_ks[:, ac:ac + 1],
                                    k_sb[:, i * D + ac * 128:i * D + (ac + 1) * 128],
                                    c.g.ones_col[:, :],
                                    start=(i == 0), stop=(i == NT_SEQ - 1))
                    if ac == 7 and i == NT_SEQ - 1:
                        ins.then_inc(mm.h, 1)
        mm.add(1)
        need_mm_ks = mm.v

        @block.vector
        def _(v, s=s, need_mm_ks=need_mm_ks):
            v.wait_ge(mm.h, need_mm_ks)
            v.tensor_copy(ksum_bf[:, s * 8:(s + 1) * 8], ps_ks[:, :]).then_inc(kvev.h, 1)
        kvev.add(1)

        for p in range(4):
            need_kvev = kvev.v

            @block.tensor
            def _(te, p=p, need_kvev=need_kvev, need_vev_all=need_vev_all):
                te.wait_ge(vev.h, need_vev_all)
                te.wait_ge(kvev.h, need_kvev)
                for jj in range(2):
                    ac = p * 2 + jj
                    for bh in range(2):
                        for i in range(NT_SEQ):
                            ins = te.matmul(
                                ps_kv[:, jj * 1024 + bh * 512:jj * 1024 + bh * 512 + 512],
                                k_sb[:, i * D + ac * 128:i * D + (ac + 1) * 128],
                                v_sb[:, i * D + bh * 512:i * D + bh * 512 + 512],
                                start=(i == 0), stop=(i == NT_SEQ - 1))
                            if jj == 1 and bh == 1 and i == NT_SEQ - 1:
                                ins.then_inc(mm.h, 1)
            mm.add(1)
            need_mm_kv = mm.v

            @block.scalar
            def _(sc, p=p, need_mm_kv=need_mm_kv):
                sc.wait_ge(mm.h, need_mm_kv)
                for jj in range(2):
                    ins = sc.activation(kvs[:, (p * 2 + jj) * 1024:(p * 2 + jj + 1) * 1024],
                                        ps_kv[:, jj * 1024:(jj + 1) * 1024], AF.Copy)
                    if jj == 1:
                        ins.then_inc(kvev.h, 1)
            kvev.add(1)

        need_kvev_st = kvev.v

        @block.sync
        def _(sy, s=s, need_kvev_st=need_kvev_st):
            sy.wait_ge(kvev.h, need_kvev_st)
            sy.dma_start(out=c.kv_d[s], in_=kvs[:]).then_inc(kvst.h, 16)
        kvst.add(16)

    @block.sync
    def _(sy):
        sy.wait_ge(kvst.h, kvst.v)
        xnstp.wait_all(sy)


def emit_f0(nc, block, c, sems, stack):
    mm = sems["f_mm"]; gps = sems["f_gps"]; gs2 = sems["f_gs2"]
    wtv = sems["f_wtv"]; b2s = sems["f_b2"]; hev = sems["f_hev"]; yac = sems["f_yac"]
    sil = sems["f_sil"]; wcast0 = sems["wcast0"]
    vch, sch = sems["ch"]["v"], sems["ch"]["s"]
    xnp = pool(sems, nc, "f_xn", 2)
    w1p = pool(sems, nc, "f_w1", 4)
    w2p = pool(sems, nc, "f_w2", 4)
    sgp = pool(sems, nc, "f_sg", 2)
    smp = pool(sems, nc, "f_sm", 1)
    smb = pool(sems, nc, "f_smb", 1)

    gw_sb = stack.enter_context(nc.sbuf_tensor("f_gw", [128, 64], BF16))
    b1n = stack.enter_context(nc.sbuf_tensor("f_b1n", [8, 512], F32))
    b1c = stack.enter_context(nc.sbuf_tensor("f_b1c", [128, 32], F32))
    b2n = stack.enter_context(nc.sbuf_tensor("f_b2n", [8, 1024], BF16))
    xnst = stack.enter_context(nc.sbuf_tensor("f_xnst", [128, 2 * 8 * 1024], BF16))
    w1e = stack.enter_context(nc.sbuf_tensor("f_w1e", [128, 2 * 4096], BF16))
    w2e = stack.enter_context(nc.sbuf_tensor("f_w2e", [128, 2 * 4096], BF16))
    h_bf = stack.enter_context(nc.sbuf_tensor("f_h", [128, 4 * 512], BF16))
    g_acc = stack.enter_context(nc.sbuf_tensor("f_gacc", [128, 8 * 1024], F32))
    gsb = stack.enter_context(nc.sbuf_tensor("f_gsb", [128, 8], F32))
    gates = stack.enter_context(nc.sbuf_tensor("f_gates", [128, 8], F32))
    selt = stack.enter_context(nc.sbuf_tensor("f_selt", [128, 8], F32))
    gcols = stack.enter_context(nc.sbuf_tensor("f_gcols", [128, 8], F32))
    w_st = stack.enter_context(nc.sbuf_tensor("f_wst", [128, 64], F32))
    wbf = stack.enter_context(nc.sbuf_tensor("f_wbf", [128, 8], BF16))
    wT_st = stack.enter_context(nc.sbuf_tensor("f_wTst", [8, 1024], BF16))
    sg_t = stack.enter_context(nc.sbuf_tensor("f_sgt", [128, 2 * 1024], F32))

    ps = stack.enter_context(nc.psum_tensor("f_ps", [128, 3584], F32))
    ps_bf = stack.enter_context(nc.psum_tensor("f_psbf", [128, 1024], BF16))

    @block.gpsimd
    def _(g):
        g.wait_ge(sems["ph"].h, sems["ph"].v)
        g.wait_ge(wcast0.h, wcast0.v)
        smp.issue(g.dma_start(out=gw_sb[:].rearrange("p (c e) -> p c e", c=8),
                              in_=c.gate_w.rearrange("(c p) e -> p c e", p=128)))
        smp.issue(g.dma_start(out=b2n[:], in_=c.m0b2))
    j_b1 = None

    @block.sync
    def _(sy):
        nonlocal j_b1
        j_b1 = smb.issue(sy.dma_start(out=b1n[:], in_=c.m0b1))
    need_ev_b1 = []

    @block.tensor
    def _(te):
        smb.wait(te, j_b1)
        for fs in range(4):
            te.transpose(ps[:, 3072 + fs * 8:3072 + (fs + 1) * 8],
                         b1n[0:8, fs * 128:(fs + 1) * 128],
                         c.g.id_f32[0:8, 0:8]).then_inc(mm.h, 1)
            mm.add(1)
    need_mm_b1 = mm.v

    @block.vector
    def _(v):
        v.wait_ge(mm.h, need_mm_b1)
        for fs in range(4):
            ins = v.tensor_copy(b1c[:, fs * 8:(fs + 1) * 8],
                                ps[:, 3072 + fs * 8:3072 + (fs + 1) * 8])
            vch.inc(ins)

    expert_mm_done = {}
    st_consume_mm = {}
    prev_sil_all = 0
    prev_yac_all = 0
    for st in range(4):
        @block.sync
        def _(sy, st=st):
            if st >= 2:
                sy.wait_ge(mm.h, st_consume_mm[st - 2])
            xnp.issue(sy.dma_start(
                out=xnst[:, (st % 2) * 8192:(st % 2 + 1) * 8192]
                    .rearrange("p (c t) -> p c t", c=8),
                in_=c.xn_d.rearrange("p (c t) -> p c t", c=NCH)
                    [:, :, st * 1024:(st + 1) * 1024]))
        xn_slot = xnp.slot(xnp.issued - 1)
        xb = (st % 2) * 8192

        for t in range(8):
            gslot = (t % 2) * 512
            need_gps = gps.v - 1 if (st * 8 + t) >= 2 else 0
            need_hev_prev = hev.v

            @block.tensor
            def _(te, t=t, gslot=gslot, need_gps=need_gps, xn_slot=xn_slot,
                  xb=xb, need_hev_prev=need_hev_prev):
                te.wait_ge(xn_slot[0].h, xn_slot[1])
                smp.wait_all(te)
                smb.wait_all(te)
                if need_gps:
                    te.wait_ge(gps.h, need_gps)
                if need_hev_prev:
                    te.wait_ge(hev.h, need_hev_prev)
                for dc in range(8):
                    ins = te.matmul(ps[:, gslot:gslot + 8],
                                    xnst[:, xb + dc * 1024 + t * 128:xb + dc * 1024 + (t + 1) * 128],
                                    gw_sb[:, dc * 8:(dc + 1) * 8],
                                    start=(dc == 0), stop=(dc == 7))
                    if dc == 7:
                        ins.then_inc(mm.h, 1)
            mm.add(1)
            need_mm_g = mm.v

            @block.vector
            def _(v, gslot=gslot, need_mm_g=need_mm_g):
                v.wait_ge(mm.h, need_mm_g)
                vch.inc(v.reduce_max(gcols[:, 0:1], ps[:, gslot:gslot + 8], axis=AX.X))
                vch.wait(v)
                v.tensor_scalar(gsb[:], ps[:, gslot:gslot + 8], gcols[:, 0:1], None,
                                ALU.subtract).then_inc(gps.h, 1)
            gps.add(1)
            need_gps2 = gps.v

            @block.scalar
            def _(sc, need_gps2=need_gps2):
                sc.wait_ge(gps.h, need_gps2)
                sc.activation(gsb[:], gsb[:], AF.Exp).then_inc(gs2.h, 1)
            gs2.add(1)
            need_gs2 = gs2.v

            @block.vector
            def _(v, t=t, need_gs2=need_gs2):
                v.wait_ge(gs2.h, need_gs2)
                vch.inc(v.reduce_sum(gcols[:, 1:2], gsb[:], axis=AX.X))
                vch.wait(v)
                vch.inc(v.reciprocal(gcols[:, 2:3], gcols[:, 1:2]))
                vch.wait(v)
                vch.inc(v.tensor_scalar(gates[:], gsb[:], gcols[:, 2:3], None, ALU.mult))
                vch.wait(v)
                vch.inc(v.reduce_max(gcols[:, 3:4], gates[:], axis=AX.X))
                vch.wait(v)
                vch.inc(v.tensor_scalar(selt[:], gates[:], gcols[:, 3:4], None, ALU.is_ge))
                vch.wait(v)
                vch.inc(v.scalar_tensor_tensor(selt[:], selt[:], -1e30, gates[:],
                                               ALU.mult, ALU.add))
                vch.wait(v)
                vch.inc(v.reduce_max(gcols[:, 4:5], selt[:], axis=AX.X))
                vch.wait(v)
                vch.inc(v.tensor_scalar(selt[:], gates[:], gcols[:, 4:5], None, ALU.is_ge))
                vch.wait(v)
                vch.inc(v.tensor_tensor(gcols[:, 5:6], gcols[:, 3:4], gcols[:, 4:5], ALU.add))
                vch.wait(v)
                vch.inc(v.tensor_scalar(w_st[:, t * 8:(t + 1) * 8], selt[:], gcols[:, 5:6],
                                        None, ALU.mult))
                vch.wait(v)
                v.tensor_copy(wbf[:], w_st[:, t * 8:(t + 1) * 8]).then_inc(wtv.h, 1)
            wtv.add(1)
            need_wtv = wtv.v
            b2slot = 1024 + (t % 2) * 1024
            need_b2prev = b2s.v - 1 if (st * 8 + t) >= 2 else 0
            need_yac_prev = prev_yac_all

            @block.tensor
            def _(te, t=t, need_wtv=need_wtv, b2slot=b2slot, need_b2prev=need_b2prev,
                  need_yac_prev=need_yac_prev):
                te.wait_ge(wtv.h, need_wtv)
                te.transpose(ps_bf[0:8, 0:128], wbf[:], c.g.id_bf[:, :]).then_inc(mm.h, 1)
            mm.add(1)
            need_mm_wt = mm.v

            @block.vector
            def _(v, t=t, need_mm_wt=need_mm_wt):
                v.wait_ge(mm.h, need_mm_wt)
                v.tensor_copy(wT_st[0:8, t * 128:(t + 1) * 128],
                              ps_bf[0:8, 0:128]).then_inc(wtv.h, 1)
            wtv.add(1)
            need_wtv2 = wtv.v

            @block.tensor
            def _(te, t=t, need_wtv2=need_wtv2, b2slot=b2slot, need_b2prev=need_b2prev,
                  need_yac_prev=need_yac_prev):
                te.wait_ge(wtv.h, need_wtv2)
                if need_b2prev:
                    te.wait_ge(b2s.h, need_b2prev)
                if need_yac_prev:
                    te.wait_ge(yac.h, need_yac_prev)
                for dh in range(2):
                    ins = te.matmul(ps[:, b2slot + dh * 512:b2slot + dh * 512 + 512],
                                    wT_st[0:8, t * 128:(t + 1) * 128],
                                    b2n[0:8, dh * 512:dh * 512 + 512],
                                    start=True, stop=True)
                    if dh == 1:
                        ins.then_inc(mm.h, 1)
            mm.add(1)
            need_mm_b2 = mm.v
            need_sil_prev = prev_sil_all

            @block.scalar
            def _(sc, st=st, t=t, b2slot=b2slot, need_mm_b2=need_mm_b2,
                  need_sil_prev=need_sil_prev):
                sc.wait_ge(mm.h, need_mm_b2)
                if need_sil_prev:
                    sc.wait_ge(sil.h, need_sil_prev)
                sc.activation(g_acc[:, t * 1024:(t + 1) * 1024],
                              ps[:, b2slot:b2slot + 1024], AF.Copy).then_inc(b2s.h, 1)
            b2s.add(1)
        b2_all = b2s.v
        need_yacST = prev_yac_all

        for e in range(8):
            gidx = st * 8 + e
            free_mm = expert_mm_done.get(gidx - 2, 0)

            @block.sync
            def _(sy, e=e, free_mm=free_mm):
                if free_mm:
                    sy.wait_ge(mm.h, free_mm)
                sy.wait_ge(wcast0.h, wcast0.v)
                w1p.issue(sy.dma_start(out=w1e[:, (e % 2) * 4096:(e % 2 + 1) * 4096],
                                       in_=c.m0w1_b[e]))
                w2p.issue(sy.dma_start(out=w2e[:, (e % 2) * 4096:(e % 2 + 1) * 4096],
                                       in_=c.m0w2_b[e]))
            w1_slot = w1p.slot(w1p.issued - 1)
            w2_slot = w2p.slot(w2p.issued - 1)
            eb = (e % 2) * 4096

            for wd in range(2):
                for fs in range(4):
                    hslot = (fs % 2) * 512
                    need_hev = hev.v - 1 if (wd * 4 + fs) >= 2 or e > 0 or st > 0 else 0
                    need_gps_all = gps.v if (e == 0 and wd == 0 and fs < 2) else 0

                    @block.tensor
                    def _(te, fs=fs, wd=wd, hslot=hslot, eb=eb, xb=xb,
                          w1_slot=w1_slot, xn_slot=xn_slot, need_hev=need_hev,
                          need_gps_all=need_gps_all):
                        te.wait_ge(w1_slot[0].h, w1_slot[1])
                        te.wait_ge(xn_slot[0].h, xn_slot[1])
                        if need_gps_all:
                            te.wait_ge(gps.h, need_gps_all)
                        if need_hev:
                            te.wait_ge(hev.h, need_hev)
                        for dc in range(8):
                            ins = te.matmul(
                                ps[:, hslot:hslot + 512],
                                w1e[:, eb + dc * 512 + fs * 128:eb + dc * 512 + (fs + 1) * 128],
                                xnst[:, xb + dc * 1024 + wd * 512:xb + dc * 1024 + wd * 512 + 512],
                                start=(dc == 0), stop=(dc == 7))
                            if dc == 7:
                                ins.then_inc(mm.h, 1)
                    mm.add(1)
                    need_mm_h = mm.v

                    @block.scalar
                    def _(sc, e=e, fs=fs, hslot=hslot, need_mm_h=need_mm_h):
                        sc.wait_ge(mm.h, need_mm_h)
                        sc.activation(h_bf[:, fs * 512:(fs + 1) * 512],
                                      ps[:, hslot:hslot + 512], AF.Gelu,
                                      bias=b1c[:, fs * 8 + e:fs * 8 + e + 1]).then_inc(hev.h, 1)
                    hev.add(1)
                need_hev_wd = hev.v

                for tl in range(4):
                    tile = wd * 4 + tl
                    yslot = 1024 + (tl % 2) * 1024
                    need_yac = yac.v - 1 if (e * 8 + wd * 4 + tl) >= 2 or st > 0 else 0
                    need_b2_all = b2_all

                    @block.tensor
                    def _(te, tl=tl, eb=eb, yslot=yslot, need_hev_wd=need_hev_wd,
                          need_yac=need_yac, w2_slot=w2_slot, need_b2_all=need_b2_all):
                        te.wait_ge(hev.h, need_hev_wd)
                        te.wait_ge(w2_slot[0].h, w2_slot[1])
                        te.wait_ge(b2s.h, need_b2_all)
                        if need_yac:
                            te.wait_ge(yac.h, need_yac)
                        k = 0
                        for fs in range(4):
                            for dh in range(2):
                                ins = te.matmul(
                                    ps[:, yslot + dh * 512:yslot + dh * 512 + 512],
                                    h_bf[:, fs * 512 + tl * 128:fs * 512 + (tl + 1) * 128],
                                    w2e[:, eb + fs * 1024 + dh * 512:eb + fs * 1024 + dh * 512 + 512],
                                    start=(fs == 0), stop=(fs == 3))
                                k += 1
                                if k == 8:
                                    ins.then_inc(mm.h, 1)
                    mm.add(1)
                    need_mm_y = mm.v

                    @block.vector
                    def _(v, st=st, e=e, tile=tile, yslot=yslot, need_mm_y=need_mm_y,
                          need_b2_all=need_b2_all):
                        v.wait_ge(mm.h, need_mm_y)
                        v.wait_ge(b2s.h, need_b2_all)
                        v.wait_ge(yac.h, yac.v)
                        v.scalar_tensor_tensor(
                            g_acc[:, tile * 1024:(tile + 1) * 1024],
                            ps[:, yslot:yslot + 1024],
                            w_st[:, tile * 8 + e:tile * 8 + e + 1],
                            g_acc[:, tile * 1024:(tile + 1) * 1024],
                            ALU.mult, ALU.add).then_inc(yac.h, 1)
                    yac.add(1)
            expert_mm_done[gidx] = mm.v
        st_consume_mm[st] = mm.v

        last_yac = yac.v
        for t in range(8):
            sslot = (t % 2) * 1024
            sg_free = sgp.slot(sgp.issued - 2) if sgp.issued >= 2 else None

            @block.scalar
            def _(sc, t=t, sslot=sslot, last_yac=last_yac, sg_free=sg_free):
                sc.wait_ge(yac.h, last_yac)
                if sg_free is not None:
                    sc.wait_ge(sg_free[0].h, sg_free[1])
                sc.activation(sg_t[:, sslot:sslot + 1024],
                              g_acc[:, t * 1024:(t + 1) * 1024], AF.Silu).then_inc(sil.h, 1)
            sil.add(1)
            need_sil = sil.v

            @block.sync
            def _(sy, st=st, t=t, sslot=sslot, need_sil=need_sil):
                sy.wait_ge(sil.h, need_sil)
                sgp.issue(sy.dma_start(
                    out=c.sg0_d[st * 1024 + t * 128:st * 1024 + (t + 1) * 128, :],
                    in_=sg_t[:, sslot:sslot + 1024]))
        prev_sil_all = sil.v
        prev_yac_all = yac.v

    @block.sync
    def _(sy):
        sgp.wait_all(sy)


def emit_b0(nc, block, c, sems, stack):
    mm = sems["b_mm"]; qev = sems["b_qev"]; zev = sems["b_zev"]; aev = sems["b_aev"]
    x1s = sems["b_x1"]; sq = sems["a_sq"]; col = sems["a_col"]; rstd = sems["a_rstd"]
    xn = sems["a_xn"]; tr = sems["a_tr"]; xnf = sems["a_xnf"]
    wldb = sems["a_wld"]; p0st = sems["p0_st"]
    vch = sems["ch"]["v"]
    xnwp = pool(sems, nc, "b_xnw", 2)
    xwp = pool(sems, nc, "b_xw", 2)
    sgwp = pool(sems, nc, "b_sgw", 2)
    kvp = pool(sems, nc, "b_kv", 1)
    x1p = pool(sems, nc, "b_x1st", 2)
    xn1p = pool(sems, nc, "b_xn1st", 2)

    wq_sb = stack.enter_context(nc.sbuf_tensor("b_wq", [128, NCH * D], BF16))
    wout_sb = stack.enter_context(nc.sbuf_tensor("b_wout", [128, NCH * D], BF16))
    kv_sb = stack.enter_context(nc.sbuf_tensor("b_kv", [128, NCH * D], BF16))
    boutf = stack.enter_context(nc.sbuf_tensor("b_boutf", [1, D], F32))
    boutr = stack.enter_context(nc.sbuf_tensor("b_boutr", [1, D], BF16))
    xn_w = stack.enter_context(nc.sbuf_tensor("b_xnw", [128, 2 * 4096], BF16))
    xw = stack.enter_context(nc.sbuf_tensor("b_xw", [128, 2 * 4096], F32))
    sgw = stack.enter_context(nc.sbuf_tensor("b_sgw", [128, 2 * 4096], F32))
    q_bf = stack.enter_context(nc.sbuf_tensor("b_q", [128, 8 * 512], BF16))
    at_bf = stack.enter_context(nc.sbuf_tensor("b_at", [128, 8 * 512], BF16))
    zcol = stack.enter_context(nc.sbuf_tensor("b_zcol", [128, 4], F32))
    x1tmp = stack.enter_context(nc.sbuf_tensor("b_x1tmp", [128, D], F32))
    x1t = stack.enter_context(nc.sbuf_tensor("b_x1t", [128, 2 * D], F32))
    sqd = stack.enter_context(nc.sbuf_tensor("b_sqd", [128, D], BF16))
    cols = stack.enter_context(nc.sbuf_tensor("b_cols", [128, 8], F32))
    xn1_tm = stack.enter_context(nc.sbuf_tensor("b_xn1tm", [128, 2 * D], BF16))
    xn1f = stack.enter_context(nc.sbuf_tensor("b_xn1f", [128, 2 * D], BF16))
    cc = stack.enter_context(nc.sbuf_tensor("b_cc", [128, 1], F32))

    ps = stack.enter_context(nc.psum_tensor("b_ps", [128, 3584], F32))
    ps_bf = stack.enter_context(nc.psum_tensor("b_psbf", [128, 1024], BF16))

    cns = S(sems["_stack"].enter_context(nc.semaphore("b_cns")))
    bld = S(sems["_stack"].enter_context(nc.semaphore("b_bld")))

    @block.vector
    def _(v):
        v.memset(cc[:, 0:1], -0.5)
        v.memset(cols[:, 2:3], 0.0).then_inc(cns.h, 1)
    cns.add(1)

    @block.scalar
    def _(sc):
        sc.wait_ge(cns.h, cns.v)
        sc.wait_ge(p0st.h, p0st.v)
        sc.dma_start(out=wq_sb[:], in_=c.wq_t).then_inc(wldb.h, 16)
        sc.dma_start(out=wout_sb[:], in_=c.wout_t).then_inc(wldb.h, 16)
        sc.dma_start(out=boutf[:], in_=c.b_out).then_inc(wldb.h, 16)
        sc.wait_ge(wldb.h, wldb.v + 48)
        sc.activation(boutr[:], boutf[:], AF.Copy).then_inc(bld.h, 1)
    wld_all = wldb.add(48)
    bld.add(1)

    b0_state["xnw_done"] = {}
    b0_state["gate_done"] = {}
    b0_state["tile_gate"] = {}
    b0_state["sev_prev"] = 0
    ln_bufs = (sqd, cols)
    for s in range(SEQ):
        kv_free_mm = mm.v

        @block.sync
        def _(sy, s=s, kv_free_mm=kv_free_mm):
            if kv_free_mm:
                sy.wait_ge(mm.h, kv_free_mm)
            kvp.issue(sy.dma_start(out=kv_sb[:], in_=c.kv_d[s]))
        kv_slot = kvp.slot(kvp.issued - 1)

        for w in range(4):
            gw = s * 4 + w
            wslot = gw % 2
            @block.sync
            def _(sy, s=s, w=w, wslot=wslot, gw=gw):
                if gw >= 2:
                    sy.wait_ge(mm.h, b0_state["xnw_done"][gw - 2])
                xnwp.issue(sy.dma_start(
                    out=xn_w[:, wslot * 4096:(wslot + 1) * 4096]
                        .rearrange("p (c t) -> p c t", c=8),
                    in_=c.xn_d.rearrange("p (c t) -> p c t", c=NCH)
                        [:, :, s * SLEN + w * 512:s * SLEN + (w + 1) * 512]))
                if gw >= 2:
                    sy.wait_ge(x1s.h, b0_state["gate_done"][gw - 2])
                xwp.issue(sy.dma_start(
                    out=xw[:, wslot * 4096:(wslot + 1) * 4096]
                        .rearrange("p (t d) -> p t d", t=4),
                    in_=c.x[s, w * 512:(w + 1) * 512, :]
                        .rearrange("(t p) d -> p t d", p=128)))
                sgwp.issue(sy.dma_start(
                    out=sgw[:, wslot * 4096:(wslot + 1) * 4096]
                        .rearrange("p (t d) -> p t d", t=4),
                    in_=c.sg0_d[s * SLEN + w * 512:s * SLEN + (w + 1) * 512, :]
                        .rearrange("(t p) d -> p t d", p=128)))
            xn_slot = xnwp.slot(xnwp.issued - 1)
            xw_slot = xwp.slot(xwp.issued - 1)
            sg_slot = sgwp.slot(sgwp.issued - 1)
            xb = wslot * 4096

            for qc in range(8):
                qslot = (qc % 2) * 512
                need_qev = qev.v - 1 if (gw * 8 + qc) >= 2 else 0

                @block.tensor
                def _(te, qc=qc, qslot=qslot, xb=xb, need_qev=need_qev,
                      xn_slot=xn_slot, kv_slot=kv_slot):
                    te.wait_ge(xn_slot[0].h, xn_slot[1])
                    te.wait_ge(wldb.h, wld_all)
                    if need_qev:
                        te.wait_ge(qev.h, need_qev)
                    for dc in range(8):
                        ins = te.matmul(
                            ps[:, qslot:qslot + 512],
                            wq_sb[:, dc * D + qc * 128:dc * D + (qc + 1) * 128],
                            xn_w[:, xb + dc * 512:xb + (dc + 1) * 512],
                            start=(dc == 0), stop=(dc == 7))
                        if dc == 7:
                            ins.then_inc(mm.h, 1)
                mm.add(1)
                need_mm_q = mm.v

                @block.vector
                def _(v, qc=qc, qslot=qslot, need_mm_q=need_mm_q):
                    v.wait_ge(mm.h, need_mm_q)
                    v.tensor_scalar(q_bf[:, qc * 512:(qc + 1) * 512],
                                    ps[:, qslot:qslot + 512], 0.0, ATT_EPS,
                                    ALU.max, ALU.add).then_inc(qev.h, 1)
                qev.add(1)
            qev_all = qev.v

            need_zev_prev = zev.v

            @block.tensor
            def _(te, s=s, need_zev_prev=need_zev_prev, qev_all=qev_all):
                te.wait_ge(qev.h, qev_all)
                if need_zev_prev:
                    te.wait_ge(zev.h, need_zev_prev)
                for tl in range(4):
                    for ac in range(8):
                        ins = te.matmul(
                            ps[:, 3072 + tl:3072 + tl + 1],
                            q_bf[:, ac * 512 + tl * 128:ac * 512 + (tl + 1) * 128],
                            c.g.ksum_bf[:, s * 8 + ac:s * 8 + ac + 1],
                            start=(ac == 0), stop=(ac == 7))
                        if tl == 3 and ac == 7:
                            ins.then_inc(mm.h, 1)
            mm.add(1)
            need_mm_z = mm.v

            @block.vector
            def _(v, need_mm_z=need_mm_z):
                v.wait_ge(mm.h, need_mm_z)
                vch.inc(v.tensor_scalar(zcol[:], ps[:, 3072:3076], ATT_EPS, None, ALU.add))
                vch.wait(v)
                v.reciprocal(zcol[:], zcol[:]).then_inc(zev.h, 1)
            zev.add(1)
            zev_now = zev.v

            sev_prev = b0_state["sev_prev"]
            for bc in range(8):
                aslot = 1024 + (bc % 2) * 512
                need_aev = aev.v - 1 if (gw * 8 + bc) >= 2 else 0

                @block.tensor
                def _(te, bc=bc, aslot=aslot, need_aev=need_aev, sev_prev=sev_prev,
                      kv_slot=kv_slot, qev_all=qev_all):
                    te.wait_ge(kv_slot[0].h, kv_slot[1])
                    te.wait_ge(qev.h, qev_all)
                    if need_aev:
                        te.wait_ge(aev.h, need_aev)
                    if bc < 2 and sev_prev:
                        te.wait_ge(x1s.h, sev_prev)
                    for ac in range(8):
                        ins = te.matmul(
                            ps[:, aslot:aslot + 512],
                            kv_sb[:, ac * D + bc * 128:ac * D + (bc + 1) * 128],
                            q_bf[:, ac * 512:(ac + 1) * 512],
                            start=(ac == 0), stop=(ac == 7))
                        if ac == 7:
                            ins.then_inc(mm.h, 1)
                mm.add(1)
                need_mm_a = mm.v

                @block.scalar
                def _(sc, bc=bc, aslot=aslot, need_mm_a=need_mm_a):
                    sc.wait_ge(mm.h, need_mm_a)
                    sc.activation(at_bf[:, bc * 512:(bc + 1) * 512],
                                  ps[:, aslot:aslot + 512], AF.Copy).then_inc(aev.h, 1)
                aev.add(1)
            aev_all = aev.v

            for tl in range(4):
                gt = s * SLEN + w * 512 + tl * 128
                gtile = gw * 4 + tl
                sslot = 2048 if (tl % 2 == 0) else 1024

                @block.tensor
                def _(te, tl=tl, sslot=sslot, aev_all=aev_all, gtile=gtile):
                    te.wait_ge(aev.h, aev_all)
                    te.wait_ge(bld.h, 1)
                    if gtile >= 2:
                        te.wait_ge(x1s.h, b0_state["tile_gate"][gtile - 2])
                    for bc in range(8):
                        for dh in range(2):
                            te.matmul(
                                ps[:, sslot + dh * 512:sslot + dh * 512 + 512],
                                at_bf[:, bc * 512 + tl * 128:bc * 512 + (tl + 1) * 128],
                                wout_sb[:, bc * D + dh * 512:bc * D + dh * 512 + 512],
                                start=(bc == 0), stop=False)
                    for dh in range(2):
                        ins = te.matmul(ps[:, sslot + dh * 512:sslot + dh * 512 + 512],
                                        c.g.ones_row[0:1, :],
                                        boutr[0:1, dh * 512:dh * 512 + 512],
                                        start=False, stop=True)
                        if dh == 1:
                            ins.then_inc(mm.h, 1)
                mm.add(1)
                need_mm_s = mm.v

                x1_ap = x1t[:, (tl % 2) * D:(tl % 2 + 1) * D]
                x1_free = x1p.slot(x1p.issued - 2) if x1p.issued >= 2 else None
                need_xn_prev = xn.v - 1 if gtile >= 2 else 0

                @block.vector
                def _(v, tl=tl, sslot=sslot, need_mm_s=need_mm_s, x1_ap=x1_ap,
                      xw_slot=xw_slot, sg_slot=sg_slot, x1_free=x1_free, xb=xb,
                      need_xn_prev=need_xn_prev, zev_now=zev_now):
                    v.wait_ge(mm.h, need_mm_s)
                    v.wait_ge(zev.h, zev_now)
                    v.wait_ge(sg_slot[0].h, sg_slot[1])
                    v.wait_ge(xw_slot[0].h, xw_slot[1])
                    if x1_free is not None:
                        v.wait_ge(x1_free[0].h, x1_free[1])
                    if need_xn_prev:
                        v.wait_ge(xn.h, need_xn_prev)
                    vch.wait(v)
                    vch.inc(v.scalar_tensor_tensor(
                        x1tmp[:], ps[:, sslot:sslot + 1024], zcol[:, tl:tl + 1],
                        sgw[:, xb + tl * 1024:xb + (tl + 1) * 1024], ALU.mult, ALU.mult))
                    vch.wait(v)
                    v.tensor_tensor(x1_ap, x1tmp[:],
                                    xw[:, xb + tl * 1024:xb + (tl + 1) * 1024],
                                    ALU.add).then_inc(x1s.h, 1)
                x1s.add(1)
                need_x1 = x1s.v

                @block.sync
                def _(sy, gt=gt, x1_ap=x1_ap, need_x1=need_x1):
                    sy.wait_ge(x1s.h, need_x1)
                    x1p.issue(sy.dma_start(out=c.x1_d[gt:gt + 128, :], in_=x1_ap))

                xn1_ap = xn1_tm[:, (tl % 2) * D:(tl % 2 + 1) * D]
                emit_ln(block, sems["ch"], ln_bufs, x1_ap, xn1_ap, (x1s, need_x1),
                        sq, col, rstd, xn, cc,
                        apply_wait=(tr, tr.v - 1 if gtile >= 2 else 0))
                need_xn1 = xn.v
                need_xnf_prev = xnf.v

                @block.tensor
                def _(te, xn1_ap=xn1_ap, need_xn1=need_xn1, need_xnf_prev=need_xnf_prev):
                    te.wait_ge(xn.h, need_xn1)
                    if need_xnf_prev:
                        te.wait_ge(xnf.h, need_xnf_prev)
                    for dc in range(8):
                        ins = te.transpose(ps_bf[:, dc * 128:(dc + 1) * 128],
                                           xn1_ap[:, dc * 128:(dc + 1) * 128],
                                           c.g.id_bf[:, :])
                        if dc == 7:
                            ins.then_inc(tr.h, 1)
                tr.add(1)
                need_tr = tr.v
                st_slot = xn1p.slot(xn1p.issued - 2) if xn1p.issued >= 2 else None

                @block.vector
                def _(v, tl=tl, need_tr=need_tr, st_slot=st_slot):
                    v.wait_ge(tr.h, need_tr)
                    if st_slot is not None:
                        v.wait_ge(st_slot[0].h, st_slot[1])
                    v.tensor_copy(xn1f[:, (tl % 2) * D:(tl % 2 + 1) * D],
                                  ps_bf[:]).then_inc(xnf.h, 1)
                xnf.add(1)
                need_xnf = xnf.v

                @block.sync
                def _(sy, tl=tl, gt=gt, need_xnf=need_xnf):
                    sy.wait_ge(xnf.h, need_xnf)
                    xn1p.issue(sy.dma_start(
                        out=c.xn1_d.rearrange("p (c t) -> p c t", c=NCH)[:, :, gt:gt + 128],
                        in_=xn1f[:, (tl % 2) * D:(tl % 2 + 1) * D]
                            .rearrange("p (c t) -> p c t", t=128)))
                b0_state["tile_gate"][gtile] = x1s.v
            b0_state["xnw_done"][gw] = mm.v
            b0_state["gate_done"][gw] = x1s.v
            b0_state["sev_prev"] = x1s.v

    @block.sync
    def _(sy):
        x1p.wait_all(sy)
        xn1p.wait_all(sy)


b0_state = {"xnw_done": {}, "gate_done": {}, "tile_gate": {}, "sev_prev": 0}


def emit_f1(nc, block, c, sems, stack):
    mm = sems["f_mm"]; gps = sems["f_gps"]; gs2 = sems["f_gs2"]; wtv = sems["f_wtv"]
    b2s = sems["f_b2"]; hev = sems["f_hev"]; yac = sems["f_yac"]; sil = sems["f_sil"]
    wcast1 = sems["wcast1"]
    vch = sems["ch"]["v"]
    xnp = pool(sems, nc, "g_xn2", 2)
    w1p = pool(sems, nc, "f_w1", 4)
    w2p = pool(sems, nc, "g_w2", 4)
    sgp = pool(sems, nc, "g_sg", 1)
    smp = pool(sems, nc, "f_sm", 1)
    smb = pool(sems, nc, "f_smb", 1)

    gw_sb = stack.enter_context(nc.sbuf_tensor("g_gw", [128, 64], BF16))
    b1c = stack.enter_context(nc.sbuf_tensor("g_b1c", [128, 256], F32))
    b2n = stack.enter_context(nc.sbuf_tensor("g_b2n", [8, 1024], BF16))
    ps = stack.enter_context(nc.psum_tensor("g_ps", [128, 4096], F32))

    @block.gpsimd
    def _(g):
        g.wait_ge(sems["ph"].h, sems["ph"].v)
        g.wait_ge(wcast1.h, wcast1.v)
        smp.issue(g.dma_start(out=gw_sb[:].rearrange("p (c e) -> p c e", c=8),
                              in_=c.wg.rearrange("(c p) e -> p c e", p=128)))
        smp.issue(g.dma_start(out=b2n[:], in_=c.m1b2))

    with contextlib.ExitStack() as b1stack:
        b1n = b1stack.enter_context(nc.sbuf_tensor("g_b1n", [8, 4096], F32))
        j_b1 = None

        @block.sync
        def _(sy):
            nonlocal j_b1
            j_b1 = smb.issue(sy.dma_start(out=b1n[:], in_=c.m1b1))

        @block.tensor
        def _(te):
            smb.wait(te, j_b1)
            for fs in range(32):
                te.transpose(ps[:, 3072 + fs * 8:3072 + fs * 8 + 8],
                             b1n[0:8, fs * 128:(fs + 1) * 128],
                             c.g.id_f32[0:8, 0:8]).then_inc(mm.h, 1)
                mm.add(1)
        need_mm_b1 = mm.v

        @block.vector
        def _(v):
            v.wait_ge(mm.h, need_mm_b1)
            for fs in range(32):
                vch.inc(v.tensor_copy(b1c[:, fs * 8:(fs + 1) * 8],
                                      ps[:, 3072 + fs * 8:3072 + fs * 8 + 8]))
    b1c_done = vch.s.v
    b1_tr_mm = mm.v

    xnst = stack.enter_context(nc.sbuf_tensor("g_xnst", [128, 2 * 8192], BF16))
    w1sl = stack.enter_context(nc.sbuf_tensor("g_w1sl", [128, 2 * 8192], BF16))
    w2sl = stack.enter_context(nc.sbuf_tensor("g_w2sl", [128, 2 * 8192], BF16))
    h_bf = stack.enter_context(nc.sbuf_tensor("g_h", [128, 32 * 1024], BF16))
    g_acc = stack.enter_context(nc.sbuf_tensor("g_gacc", [128, 8 * 1024], F32))
    gsb = stack.enter_context(nc.sbuf_tensor("g_gsb", [128, 8], F32))
    gates = stack.enter_context(nc.sbuf_tensor("g_gates", [128, 8], F32))
    selt = stack.enter_context(nc.sbuf_tensor("g_selt", [128, 8], F32))
    gcols = stack.enter_context(nc.sbuf_tensor("g_gcols", [128, 8], F32))
    w_st = stack.enter_context(nc.sbuf_tensor("g_wst", [128, 64], F32))
    wT_st = stack.enter_context(nc.sbuf_tensor("g_wTst", [8, 1024], BF16))
    sg_t = stack.enter_context(nc.sbuf_tensor("g_sgt", [128, 1024], F32))


    f1_state["w1_free"] = {}
    f1_state["w2_free"] = {}
    expert_mm_done = {}
    st_consume_mm = {}
    prev_sil_all = 0
    prev_yac_all = 0
    for st in range(4):
        @block.sync
        def _(sy, st=st):
            if st >= 2:
                sy.wait_ge(mm.h, st_consume_mm[st - 2])
            else:
                sy.wait_ge(mm.h, b1_tr_mm)
            xnp.issue(sy.dma_start(
                out=xnst[:, (st % 2) * 8192:(st % 2 + 1) * 8192]
                    .rearrange("p (c t) -> p c t", c=8),
                in_=c.xn1_d.rearrange("p (c t) -> p c t", c=NCH)
                    [:, :, st * 1024:(st + 1) * 1024]))
        xn_slot = xnp.slot(xnp.issued - 1)
        xb = (st % 2) * 8192

        for t in range(8):
            gslot = 3328 + (t % 2) * 128
            need_gps = gps.v - 1 if (st * 8 + t) >= 2 else 0
            need_yac_prev = prev_yac_all
            need_b1c = b1c_done if (st == 0 and t == 0) else 0

            @block.tensor
            def _(te, t=t, gslot=gslot, need_gps=need_gps, xn_slot=xn_slot,
                  xb=xb, need_yac_prev=need_yac_prev, need_b1c=need_b1c):
                te.wait_ge(xn_slot[0].h, xn_slot[1])
                smp.wait_all(te)
                if need_b1c:
                    te.wait_ge(vch.s.h, need_b1c)
                if need_gps:
                    te.wait_ge(gps.h, need_gps)
                if need_yac_prev:
                    te.wait_ge(yac.h, need_yac_prev)
                for dc in range(8):
                    ins = te.matmul(ps[:, gslot:gslot + 8],
                                    xnst[:, xb + dc * 1024 + t * 128:xb + dc * 1024 + (t + 1) * 128],
                                    gw_sb[:, dc * 8:(dc + 1) * 8],
                                    start=(dc == 0), stop=(dc == 7))
                    if dc == 7:
                        ins.then_inc(mm.h, 1)
            mm.add(1)
            need_mm_g = mm.v

            @block.vector
            def _(v, gslot=gslot, need_mm_g=need_mm_g):
                v.wait_ge(mm.h, need_mm_g)
                vch.inc(v.reduce_max(gcols[:, 0:1], ps[:, gslot:gslot + 8], axis=AX.X))
                vch.wait(v)
                v.tensor_scalar(gsb[:], ps[:, gslot:gslot + 8], gcols[:, 0:1], None,
                                ALU.subtract).then_inc(gps.h, 1)
            gps.add(1)
            need_gps2 = gps.v

            @block.scalar
            def _(sc, need_gps2=need_gps2):
                sc.wait_ge(gps.h, need_gps2)
                sc.activation(gsb[:], gsb[:], AF.Exp).then_inc(gs2.h, 1)
            gs2.add(1)
            need_gs2 = gs2.v

            @block.vector
            def _(v, t=t, need_gs2=need_gs2):
                v.wait_ge(gs2.h, need_gs2)
                vch.inc(v.reduce_sum(gcols[:, 1:2], gsb[:], axis=AX.X))
                vch.wait(v)
                vch.inc(v.reciprocal(gcols[:, 2:3], gcols[:, 1:2]))
                vch.wait(v)
                vch.inc(v.tensor_scalar(gates[:], gsb[:], gcols[:, 2:3], None, ALU.mult))
                vch.wait(v)
                vch.inc(v.reduce_max(gcols[:, 3:4], gates[:], axis=AX.X))
                vch.wait(v)
                vch.inc(v.tensor_scalar(selt[:], gates[:], gcols[:, 3:4], None, ALU.is_ge))
                vch.wait(v)
                vch.inc(v.scalar_tensor_tensor(selt[:], selt[:], -1e30, gates[:],
                                               ALU.mult, ALU.add))
                vch.wait(v)
                vch.inc(v.reduce_max(gcols[:, 4:5], selt[:], axis=AX.X))
                vch.wait(v)
                vch.inc(v.tensor_scalar(selt[:], gates[:], gcols[:, 4:5], None, ALU.is_ge))
                vch.wait(v)
                vch.inc(v.tensor_tensor(gcols[:, 5:6], gcols[:, 3:4], gcols[:, 4:5], ALU.add))
                vch.wait(v)
                vch.inc(v.tensor_scalar(gcols[:, 6:7], gcols[:, 5:6], MOE_EPS, None, ALU.add))
                vch.wait(v)
                vch.inc(v.reciprocal(gcols[:, 7:8], gcols[:, 6:7]))
                vch.wait(v)
                v.scalar_tensor_tensor(w_st[:, t * 8:(t + 1) * 8], gates[:],
                                        gcols[:, 7:8], selt[:],
                                        ALU.mult, ALU.mult).then_inc(wtv.h, 1)
            wtv.add(1)
            need_wtv = wtv.v
            wtslot = 3584 + (t % 2) * 128

            @block.tensor
            def _(te, t=t, need_wtv=need_wtv, wtslot=wtslot):
                te.wait_ge(wtv.h, need_wtv)
                te.transpose(ps[0:8, wtslot:wtslot + 128],
                             w_st[:, t * 8:(t + 1) * 8],
                             c.g.id_f32[:, :]).then_inc(mm.h, 1)
            mm.add(1)
            need_mm_wt = mm.v

            @block.vector
            def _(v, t=t, need_mm_wt=need_mm_wt, wtslot=wtslot):
                v.wait_ge(mm.h, need_mm_wt)
                v.tensor_copy(wT_st[0:8, t * 128:(t + 1) * 128],
                              ps[0:8, wtslot:wtslot + 128]).then_inc(wtv.h, 1)
            wtv.add(1)
            need_wtv2 = wtv.v
            b2slot = 2048 if (t % 2 == 0) else 1024
            need_b2prev = b2s.v - 1 if (st * 8 + t) >= 2 else 0

            @block.tensor
            def _(te, t=t, need_wtv2=need_wtv2, b2slot=b2slot, need_b2prev=need_b2prev):
                te.wait_ge(wtv.h, need_wtv2)
                if need_b2prev:
                    te.wait_ge(b2s.h, need_b2prev)
                for dh in range(2):
                    ins = te.matmul(ps[:, b2slot + dh * 512:b2slot + dh * 512 + 512],
                                    wT_st[0:8, t * 128:(t + 1) * 128],
                                    b2n[0:8, dh * 512:dh * 512 + 512],
                                    start=True, stop=True)
                    if dh == 1:
                        ins.then_inc(mm.h, 1)
            mm.add(1)
            need_mm_b2 = mm.v
            need_sil_prev = prev_sil_all

            @block.scalar
            def _(sc, t=t, b2slot=b2slot, need_mm_b2=need_mm_b2,
                  need_sil_prev=need_sil_prev):
                sc.wait_ge(mm.h, need_mm_b2)
                if need_sil_prev:
                    sc.wait_ge(sil.h, need_sil_prev)
                sc.activation(g_acc[:, t * 1024:(t + 1) * 1024],
                              ps[:, b2slot:b2slot + 1024], AF.Copy).then_inc(b2s.h, 1)
            b2s.add(1)
        b2_all = b2s.v

        for e in range(8):
            gidx = st * 8 + e
            yac_at_e = yac.v
            for q in range(4):
                jq = gidx * 4 + q
                free_mm = f1_state["w1_free"].get(jq - 2, 0)

                @block.sync
                def _(sy, e=e, q=q, free_mm=free_mm):
                    if free_mm:
                        sy.wait_ge(mm.h, free_mm)
                    sy.wait_ge(wcast1.h, wcast1.v)
                    w1p.issue(sy.dma_start(
                        out=w1sl[:, (q % 2) * 8192:(q % 2 + 1) * 8192],
                        in_=c.m1w1_b[e][:, q * 8192:(q + 1) * 8192]))
                w1_slot = w1p.slot(w1p.issued - 1)

                for fi in range(8):
                    fs = q * 8 + fi
                    wb1 = (q % 2) * 8192 + fi * 1024
                    hbase = (fs % 2) * 1024
                    for wd in range(2):
                        hslot = hbase + wd * 512
                        need_hev = hev.v - 1 if (gidx * 32 + fs) >= 2 else 0
                        need_yac_all = yac_at_e if (fs == 0 and wd == 0) else 0
                        need_b2_h = b2_all if (e == 0 and fs <= 1) else 0

                        @block.tensor
                        def _(te, wb1=wb1, wd=wd, hslot=hslot, xb=xb, w1_slot=w1_slot,
                              xn_slot=xn_slot, need_hev=need_hev,
                              need_yac_all=need_yac_all, fi=fi, need_b2_h=need_b2_h):
                            if fi == 0 and wd == 0:
                                te.wait_ge(w1_slot[0].h, w1_slot[1])
                                te.wait_ge(xn_slot[0].h, xn_slot[1])
                            if need_b2_h:
                                te.wait_ge(b2s.h, need_b2_h)
                            if need_yac_all:
                                te.wait_ge(yac.h, need_yac_all)
                            if wd == 0 and need_hev:
                                te.wait_ge(hev.h, need_hev)
                            for dc in range(8):
                                ins = te.matmul(
                                    ps[:, hslot:hslot + 512],
                                    w1sl[:, wb1 + dc * 128:wb1 + (dc + 1) * 128],
                                    xnst[:, xb + dc * 1024 + wd * 512:xb + dc * 1024 + wd * 512 + 512],
                                    start=(dc == 0), stop=(dc == 7))
                                if dc == 7:
                                    ins.then_inc(mm.h, 1)
                        mm.add(1)
                    need_mm_h = mm.v

                    @block.scalar
                    def _(sc, e=e, fs=fs, hbase=hbase, need_mm_h=need_mm_h):
                        sc.wait_ge(mm.h, need_mm_h)
                        sc.activation(h_bf[:, fs * 1024:(fs + 1) * 1024],
                                      ps[:, hbase:hbase + 1024], AF.Gelu,
                                      bias=b1c[:, fs * 8 + e:fs * 8 + e + 1]).then_inc(hev.h, 1)
                    hev.add(1)
                f1_state["w1_free"][jq] = mm.v
            hev_all = hev.v

            for wd in range(2):
                yac_at_wd = yac.v
                for q in range(4):
                    jq2 = (st * 16 + e * 2 + wd) * 4 + q
                    free_mm2 = f1_state["w2_free"].get(jq2 - 2, 0)

                    @block.gpsimd
                    def _(g, e=e, q=q, free_mm2=free_mm2):
                        if free_mm2:
                            g.wait_ge(mm.h, free_mm2)
                        g.wait_ge(wcast1.h, wcast1.v)
                        w2p.issue(g.dma_start(
                            out=w2sl[:, (q % 2) * 8192:(q % 2 + 1) * 8192],
                            in_=c.m1w2_b[e][:, q * 8192:(q + 1) * 8192]))
                    w2_slot = w2p.slot(w2p.issued - 1)

                    for fi in range(8):
                        fs = q * 8 + fi
                        wb2 = (q % 2) * 8192 + fi * 1024

                        @block.tensor
                        def _(te, fs=fs, fi=fi, wd=wd, wb2=wb2, w2_slot=w2_slot,
                              hev_all=hev_all, e=e, b2_all=b2_all, yac_at_wd=yac_at_wd):
                            if fi == 0:
                                te.wait_ge(w2_slot[0].h, w2_slot[1])
                                te.wait_ge(hev.h, hev_all)
                                te.wait_ge(b2s.h, b2_all)
                            if fs == 0 and yac_at_wd:
                                te.wait_ge(yac.h, yac_at_wd)
                            for tl in range(4):
                                for dh in range(2):
                                    ins = te.matmul(
                                        ps[:, tl * 1024 + dh * 512:tl * 1024 + dh * 512 + 512],
                                        h_bf[:, fs * 1024 + wd * 512 + tl * 128:
                                             fs * 1024 + wd * 512 + (tl + 1) * 128],
                                        w2sl[:, wb2 + dh * 512:wb2 + dh * 512 + 512],
                                        start=(fs == 0), stop=(fs == 31))
                                    if tl == 3 and dh == 1:
                                        ins.then_inc(mm.h, 1)
                        mm.add(1)
                        f1_state["w2_free"][jq2] = mm.v
                need_mm_y = mm.v

                for tl in range(4):
                    tile = wd * 4 + tl

                    @block.vector
                    def _(v, st=st, e=e, tl=tl, tile=tile, need_mm_y=need_mm_y,
                          b2_all=b2_all):
                        v.wait_ge(mm.h, need_mm_y)
                        v.wait_ge(b2s.h, b2_all)
                        v.wait_ge(yac.h, yac.v)
                        v.scalar_tensor_tensor(
                            g_acc[:, tile * 1024:(tile + 1) * 1024],
                            ps[:, tl * 1024:(tl + 1) * 1024],
                            w_st[:, tile * 8 + e:tile * 8 + e + 1],
                            g_acc[:, tile * 1024:(tile + 1) * 1024],
                            ALU.mult, ALU.add).then_inc(yac.h, 1)
                    yac.add(1)
            expert_mm_done[gidx] = mm.v
        st_consume_mm[st] = mm.v

        last_yac = yac.v
        for t in range(8):
            sg_free = sgp.slot(sgp.issued - 1) if sgp.issued >= 1 else None

            @block.scalar
            def _(sc, t=t, last_yac=last_yac, sg_free=sg_free):
                sc.wait_ge(yac.h, last_yac)
                if sg_free is not None:
                    sc.wait_ge(sg_free[0].h, sg_free[1])
                sc.activation(sg_t[:],
                              g_acc[:, t * 1024:(t + 1) * 1024], AF.Silu).then_inc(sil.h, 1)
            sil.add(1)
            need_sil = sil.v

            @block.sync
            def _(sy, st=st, t=t, need_sil=need_sil):
                sy.wait_ge(sil.h, need_sil)
                sgp.issue(sy.dma_start(
                    out=c.sg1_d[st * 1024 + t * 128:st * 1024 + (t + 1) * 128, :],
                    in_=sg_t[:]))
        prev_sil_all = sil.v
        prev_yac_all = yac.v

    @block.sync
    def _(sy):
        sgp.wait_all(sy)


f1_state = {"w1_free": {}, "w2_free": {}, "y_free": {}}


def emit_t1(nc, block, c, sems, stack):
    mm = sems["t_mm"]; ev = sems["t_ev"]; vev = sems["t_vev"]
    wup = sems["t_wup"]; x2s = sems["t_x2"]; p0st = sems["p0_st"]
    vch = sems["ch"]["v"]
    xnp = pool(sems, nc, "t_xn", 2)
    x1p = pool(sems, nc, "t_x1", 2)
    sgp = pool(sems, nc, "t_sg", 2)
    outp = pool(sems, nc, "t_out", 1)
    thp = pool(sems, nc, "t_th", 1)
    wcp = pool(sems, nc, "t_wc", 1)

    thk_sb = stack.enter_context(nc.sbuf_tensor("t_thk", [128, NCH * D], BF16))
    thq_sb = stack.enter_context(nc.sbuf_tensor("t_thq", [128, NCH * D], BF16))
    thvn_sb = stack.enter_context(nc.sbuf_tensor("t_thvn", [128, NCH * D], BF16))
    wcur2 = stack.enter_context(nc.sbuf_tensor("t_wcur", [128, 2 * NCH * D], F32))
    wcurb2 = stack.enter_context(nc.sbuf_tensor("t_wcurb", [128, 2 * NCH * D], BF16))
    bcurf2 = stack.enter_context(nc.sbuf_tensor("t_bcurf", [1, 2 * D], F32))
    bcurb2 = stack.enter_context(nc.sbuf_tensor("t_bcurb", [1, 2 * D], BF16))
    xn1c = stack.enter_context(nc.sbuf_tensor("t_xn1c", [128, 2 * D], BF16))
    x1c = stack.enter_context(nc.sbuf_tensor("t_x1c", [128, 2 * D], F32))
    sgc = stack.enter_context(nc.sbuf_tensor("t_sgc", [128, 2 * D], F32))
    ktm = stack.enter_context(nc.sbuf_tensor("t_ktm", [128, 2 * D], BF16))
    qtm = stack.enter_context(nc.sbuf_tensor("t_qtm", [128, 2 * D], BF16))
    kfm = stack.enter_context(nc.sbuf_tensor("t_kfm", [128, 2 * D], BF16))
    qfm = stack.enter_context(nc.sbuf_tensor("t_qfm", [128, 2 * D], BF16))
    ghat = stack.enter_context(nc.sbuf_tensor("t_ghat", [128, D], BF16))
    gpb = stack.enter_context(nc.sbuf_tensor("t_gp", [128, D], BF16))
    br_sb = stack.enter_context(nc.sbuf_tensor("t_br", [128, 7 * 128], BF16))
    bc_sb = stack.enter_context(nc.sbuf_tensor("t_bc", [128, 7 * 128], BF16))
    mk_bf = stack.enter_context(nc.sbuf_tensor("t_mk", [128, 128], BF16))
    tmp128 = stack.enter_context(nc.sbuf_tensor("t_tmp128", [128, 128], F32))
    x2t = stack.enter_context(nc.sbuf_tensor("t_x2t", [128, D], F32))
    ccl2 = stack.enter_context(nc.sbuf_tensor("t_ccl2", [128, 1], F32))

    ps = stack.enter_context(nc.psum_tensor("t_ps", [128, 3584], F32))
    psb = stack.enter_context(nc.psum_tensor("t_psb", [128, 1024], BF16))

    st8 = {"A_mm": {}, "F_vev": {}, "A_done": {}}
    tcns = S(sems["_stack"].enter_context(nc.semaphore("t_cns")))

    @block.vector
    def _(v):
        v.memset(ccl2[:, 0:1], LAM2).then_inc(tcns.h, 1)
    tcns.add(1)

    @block.scalar
    def _(sc):
        sc.wait_ge(tcns.h, tcns.v)
        sc.wait_ge(p0st.h, p0st.v)
        thp.issue(sc.dma_start(out=thk_sb[:], in_=c.thk_t))
        thp.issue(sc.dma_start(out=thq_sb[:], in_=c.thq_t))
        thp.issue(sc.dma_start(out=thvn_sb[:], in_=c.thv_tn))

    def emit_front(gch):
        ch, s = divmod(gch, 2)[0], gch % 2
        ch = gch // 2
        slot = gch % 2
        gt = s * SLEN + ch * 128
        xc = slot * D
        ld_free_mm = st8["A_mm"].get(gch - 2, 0)

        @block.sync
        def _(sy, gch=gch, slot=slot, gt=gt, ld_free_mm=ld_free_mm):
            if ld_free_mm:
                sy.wait_ge(mm.h, ld_free_mm)
            xnp.issue(sy.dma_start(
                out=xn1c[:, slot * D:(slot + 1) * D].rearrange("p (c t) -> p c t", t=128),
                in_=c.xn1_d.rearrange("p (c t) -> p c t", c=NCH)[:, :, gt:gt + 128]))
            if st8.get("gate_done", {}).get(gch - 2, 0):
                sy.wait_ge(x2s.h, st8["gate_done"][gch - 2])
            x1p.issue(sy.dma_start(out=x1c[:, slot * D:(slot + 1) * D],
                                   in_=c.x1_d[gt:gt + 128, :]))
            sgp.issue(sy.dma_start(out=sgc[:, slot * D:(slot + 1) * D],
                                   in_=c.sg1_d[gt:gt + 128, :]))
        xn_slot = xnp.slot(xnp.issued - 1)
        st8.setdefault("xn_slot", {})[gch] = xn_slot
        st8.setdefault("x1_slot", {})[gch] = x1p.slot(x1p.issued - 1)
        st8.setdefault("sg_slot", {})[gch] = sgp.slot(sgp.issued - 1)
        prev_q_ev = max(st8["F_vev"].get(gch - 1, 0), st8.get("u45_vev", 0))

        @block.tensor
        def _(te, xc=xc, xn_slot=xn_slot, prev_q_ev=prev_q_ev, ld_free_mm=ld_free_mm):
            thp.wait_all(te)
            te.wait_ge(xn_slot[0].h, xn_slot[1])
            if prev_q_ev:
                te.wait_ge(vev.h, prev_q_ev)
            for half in range(2):
                for dc in range(8):
                    ins = te.matmul(
                        ps[:, 2048 + half * 512:2048 + half * 512 + 512],
                        xn1c[:, xc + dc * 128:xc + (dc + 1) * 128],
                        thk_sb[:, dc * D + half * 512:dc * D + half * 512 + 512],
                        start=(dc == 0), stop=(dc == 7))
                    if half == 1 and dc == 7:
                        ins.then_inc(mm.h, 1)
        mm.add(1)
        need_mm_k = mm.v

        @block.vector
        def _(v, slot=slot, need_mm_k=need_mm_k):
            v.wait_ge(mm.h, need_mm_k)
            v.tensor_copy(ktm[:, slot * D:(slot + 1) * D], ps[:, 2048:3072]).then_inc(vev.h, 1)
        vev.add(1)
        k_ev = vev.v

        @block.tensor
        def _(te, slot=slot, k_ev=k_ev):
            te.wait_ge(vev.h, k_ev)
            for dc in range(8):
                ins = te.transpose(psb[:, dc * 128:(dc + 1) * 128],
                                   ktm[:, slot * D + dc * 128:slot * D + (dc + 1) * 128],
                                   c.g.id_bf[:, :])
                if dc == 7:
                    ins.then_inc(mm.h, 1)
        mm.add(1)
        need_mm_ktr = mm.v

        @block.vector
        def _(v, slot=slot, need_mm_ktr=need_mm_ktr):
            v.wait_ge(mm.h, need_mm_ktr)
            v.tensor_copy(kfm[:, slot * D:(slot + 1) * D], psb[:]).then_inc(vev.h, 1)
        vev.add(1)
        kfm_ev = vev.v

        @block.tensor
        def _(te, xc=xc, k_ev=k_ev):
            te.wait_ge(vev.h, k_ev)
            for half in range(2):
                for dc in range(8):
                    ins = te.matmul(
                        ps[:, 2048 + half * 512:2048 + half * 512 + 512],
                        xn1c[:, xc + dc * 128:xc + (dc + 1) * 128],
                        thq_sb[:, dc * D + half * 512:dc * D + half * 512 + 512],
                        start=(dc == 0), stop=(dc == 7))
                    if half == 1 and dc == 7:
                        ins.then_inc(mm.h, 1)
        mm.add(1)
        need_mm_q = mm.v

        @block.vector
        def _(v, slot=slot, need_mm_q=need_mm_q):
            v.wait_ge(mm.h, need_mm_q)
            v.tensor_copy(qtm[:, slot * D:(slot + 1) * D], ps[:, 2048:3072]).then_inc(vev.h, 1)
        vev.add(1)
        q_ev = vev.v

        @block.tensor
        def _(te, slot=slot, q_ev=q_ev, kfm_ev=kfm_ev):
            te.wait_ge(vev.h, max(q_ev, kfm_ev))
            for dc in range(8):
                ins = te.transpose(psb[:, dc * 128:(dc + 1) * 128],
                                   qtm[:, slot * D + dc * 128:slot * D + (dc + 1) * 128],
                                   c.g.id_bf[:, :])
                if dc == 7:
                    ins.then_inc(mm.h, 1)
        mm.add(1)
        need_mm_qtr = mm.v

        @block.vector
        def _(v, slot=slot, need_mm_qtr=need_mm_qtr):
            v.wait_ge(mm.h, need_mm_qtr)
            v.tensor_copy(qfm[:, slot * D:(slot + 1) * D], psb[:]).then_inc(vev.h, 1)
        vev.add(1)
        st8["F_vev"][gch] = vev.v

    def emit_main(gch, next_front):
        s = gch % 2
        ch = gch // 2
        slot = gch % 2
        gt = s * SLEN + ch * 128
        xc = slot * D
        kf = kfm[:, slot * D:(slot + 1) * D]
        qf = qfm[:, slot * D:(slot + 1) * D]
        kt = ktm[:, slot * D:(slot + 1) * D]
        wcur = wcur2[:, s * NCH * D:(s + 1) * NCH * D]
        wcur_bf = wcurb2[:, s * NCH * D:(s + 1) * NCH * D]
        bcur_f = bcurf2[:, s * D:(s + 1) * D]
        bcur_b = bcurb2[:, s * D:(s + 1) * D]
        fv = st8["F_vev"][gch]
        prev_bc = st8.get("bcur_vev", 0)

        @block.tensor
        def _(te, kf=kf, qf=qf, fv=fv, prev_bc=prev_bc):
            te.wait_ge(vev.h, fv)
            if prev_bc:
                te.wait_ge(vev.h, prev_bc)
            for ac in range(8):
                ins = te.matmul(ps[:, 3072:3200], kf[:, ac * 128:(ac + 1) * 128],
                                kf[:, ac * 128:(ac + 1) * 128],
                                start=(ac == 0), stop=(ac == 7))
            for ac in range(8):
                ins = te.matmul(ps[:, 3200:3328], kf[:, ac * 128:(ac + 1) * 128],
                                qf[:, ac * 128:(ac + 1) * 128],
                                start=(ac == 0), stop=(ac == 7))
                if ac == 7:
                    ins.then_inc(mm.h, 1)
        mm.add(1)
        need_mm_s = mm.v

        @block.vector
        def _(v, need_mm_s=need_mm_s, vev_snap=vev.v):
            v.wait_ge(mm.h, need_mm_s)
            v.wait_ge(vev.h, vev_snap)
            vch.inc(v.tensor_scalar(tmp128[:], ps[:, 3072:3200], -LAM2, -LAM2,
                                    ALU.mult, ALU.add))
            vch.wait(v)
            vch.inc(v.tensor_tensor(br_sb[:, 0:128], tmp128[:], c.g.mask_sl[:], ALU.mult))
            vch.wait(v)
            vch.inc(v.tensor_tensor(bc_sb[:, 0:128], tmp128[:], c.g.mask_su[:], ALU.mult))
            vch.wait(v)
            vch.inc(v.tensor_scalar(tmp128[:], ps[:, 3200:3328], 1.0, None, ALU.add))
            vch.wait(v)
            v.tensor_tensor(mk_bf[:], tmp128[:], c.g.mask_iun[:],
                            ALU.mult).then_inc(vev.h, 1)
        vev.add(1)
        masks_done = vev.v

        need_wup = st8.get(("wup_done", s), 0)
        need_ev_prev = ev.v

        @block.tensor
        def _(te, xc=xc, kf=kf, need_wup=need_wup, need_ev_prev=need_ev_prev):
            te.wait_ge(wup.h, need_wup)
            if need_ev_prev:
                te.wait_ge(ev.h, need_ev_prev)
            for eh in range(2):
                first = True
                for ac in range(8):
                    te.matmul(ps[:, eh * 512:eh * 512 + 512],
                              kf[:, ac * 128:(ac + 1) * 128],
                              wcur_bf[:, ac * D + eh * 512:ac * D + eh * 512 + 512],
                              start=first, stop=False)
                    first = False
                te.matmul(ps[:, eh * 512:eh * 512 + 512], c.g.ones_row[0:1, :],
                          bcur_b[0:1, eh * 512:eh * 512 + 512], start=False, stop=False)
                for dc in range(8):
                    ins = te.matmul(
                        ps[:, eh * 512:eh * 512 + 512],
                        xn1c[:, xc + dc * 128:xc + (dc + 1) * 128],
                        thvn_sb[:, dc * D + eh * 512:dc * D + eh * 512 + 512],
                        start=False, stop=(dc == 7))
                    if eh == 1 and dc == 7:
                        ins.then_inc(mm.h, 1)
        mm.add(1)

        prev_u_ev = st8.get("u_vev", 0)

        @block.tensor
        def _(te, qf=qf, prev_u_ev=prev_u_ev):
            if prev_u_ev:
                te.wait_ge(vev.h, prev_u_ev)
            for eh in range(2):
                first = True
                for ac in range(8):
                    te.matmul(ps[:, 1024 + eh * 512:1024 + eh * 512 + 512],
                              qf[:, ac * 128:(ac + 1) * 128],
                              wcur_bf[:, ac * D + eh * 512:ac * D + eh * 512 + 512],
                              start=first, stop=False)
                    first = False
                ins = te.matmul(ps[:, 1024 + eh * 512:1024 + eh * 512 + 512],
                                c.g.ones_row[0:1, :],
                                bcur_b[0:1, eh * 512:eh * 512 + 512],
                                start=False, stop=True)
                if eh == 1:
                    ins.then_inc(mm.h, 1)
        mm.add(1)
        psc_done = mm.v

        app_done = psc_done
        b_done = [masks_done] + [0] * 6
        for L in range(7):
            @block.scalar
            def _(sc, L=L, app_done=app_done):
                sc.wait_ge(mm.h, app_done)
                sc.activation(ghat[:], ps[:, 0:1024], AF.Copy).then_inc(ev.h, 1)
            ev.add(1)
            need_ev_c = ev.v

            @block.tensor
            def _(te, L=L, need_ev_c=need_ev_c, need_b=b_done[L]):
                te.wait_ge(ev.h, need_ev_c)
                te.wait_ge(vev.h, need_b)
                for eh in range(2):
                    ins = te.matmul(ps[:, eh * 512:eh * 512 + 512],
                                    bc_sb[:, L * 128:(L + 1) * 128],
                                    ghat[:, eh * 512:eh * 512 + 512],
                                    start=False, stop=True, skip_group_check=True)
                    if eh == 1:
                        ins.then_inc(mm.h, 1)
            mm.add(1)
            app_done = mm.v

            if L < 6:
                @block.tensor
                def _(te, L=L, need_b=b_done[L]):
                    te.wait_ge(vev.h, need_b)
                    te.matmul(ps[:, 3328:3456], bc_sb[:, L * 128:(L + 1) * 128],
                              br_sb[:, L * 128:(L + 1) * 128], start=True, stop=True)
                    te.matmul(ps[:, 3456:3584], br_sb[:, L * 128:(L + 1) * 128],
                              bc_sb[:, L * 128:(L + 1) * 128],
                              start=True, stop=True).then_inc(mm.h, 1)
                mm.add(1)
                need_mm_sq = mm.v

                @block.vector
                def _(v, L=L, need_mm_sq=need_mm_sq):
                    v.wait_ge(mm.h, need_mm_sq)
                    v.tensor_copy(br_sb[:, (L + 1) * 128:(L + 2) * 128], ps[:, 3328:3456])
                    v.tensor_copy(bc_sb[:, (L + 1) * 128:(L + 2) * 128],
                                  ps[:, 3456:3584]).then_inc(vev.h, 1)
                vev.add(1)
                b_done[L + 1] = vev.v
        need_mm_chain = mm.v


        @block.scalar
        def _(sc, need_mm_chain=need_mm_chain):
            sc.wait_ge(mm.h, need_mm_chain)
            sc.activation(gpb[:], ps[:, 0:1024], AF.Copy,
                          scale=ccl2[:, 0:1]).then_inc(ev.h, 1)
        ev.add(1)
        gp_done = ev.v

        @block.tensor
        def _(te, gp_done=gp_done, masks_done=masks_done):
            te.wait_ge(ev.h, gp_done)
            for eh in range(2):
                ins = te.matmul(ps[:, 1024 + eh * 512:1024 + eh * 512 + 512],
                                mk_bf[:], gpb[:, eh * 512:eh * 512 + 512],
                                start=False, stop=True, skip_group_check=True)
                if eh == 1:
                    ins.then_inc(mm.h, 1)
        mm.add(1)
        need_mm_out = mm.v

        x1_slot = st8["x1_slot"][gch]
        sg_slot = st8["sg_slot"][gch]
        o_free = outp.slot(outp.issued - 1) if outp.issued >= 1 else None

        @block.vector
        def _(v, slot=slot, need_mm_out=need_mm_out, x1_slot=x1_slot,
              sg_slot=sg_slot, o_free=o_free):
            v.wait_ge(mm.h, need_mm_out)
            v.wait_ge(x1_slot[0].h, x1_slot[1])
            v.wait_ge(sg_slot[0].h, sg_slot[1])
            if o_free is not None:
                v.wait_ge(o_free[0].h, o_free[1])
            vch.wait(v)
            vch.inc(v.tensor_tensor(x2t[:], ps[:, 1024:2048],
                                    sgc[:, slot * D:(slot + 1) * D], ALU.mult))
            vch.wait(v)
            v.tensor_tensor(x2t[:], x2t[:],
                            x1c[:, slot * D:(slot + 1) * D], ALU.add).then_inc(x2s.h, 1)
        x2s.add(1)
        need_x2 = x2s.v

        @block.sync
        def _(sy, s=s, ch=ch, need_x2=need_x2):
            sy.wait_ge(x2s.h, need_x2)
            outp.issue(sy.dma_start(out=c.out[s, ch * 128:(ch + 1) * 128, :],
                                    in_=x2t[:]))

        f45 = st8["F_vev"].get(gch + 1, st8["F_vev"][gch])
        u_evac = []
        for ac in range(8):
            uslot = 1024 if ac % 2 == 0 else 2048
            need_vev_u = u_evac[ac - 2] if ac >= 2 else (f45 if ac == 1 else 0)

            @block.tensor
            def _(te, ac=ac, uslot=uslot, gp_done=gp_done, need_vev_u=need_vev_u,
                  need_x2=need_x2):
                if ac == 0:
                    te.wait_ge(x2s.h, need_x2)
                    te.wait_ge(ev.h, gp_done)
                if need_vev_u:
                    te.wait_ge(vev.h, need_vev_u)
                for half in range(2):
                    ins = te.matmul(ps[:, uslot + half * 512:uslot + half * 512 + 512],
                                    kt[:, ac * 128:(ac + 1) * 128],
                                    gpb[:, half * 512:half * 512 + 512],
                                    start=True, stop=True)
                    if half == 1:
                        ins.then_inc(mm.h, 1)
            mm.add(1)
            need_mm_u = mm.v

            @block.vector
            def _(v, ac=ac, uslot=uslot, need_mm_u=need_mm_u):
                v.wait_ge(mm.h, need_mm_u)
                v.tensor_tensor(wcur[:, ac * D:(ac + 1) * D],
                                wcur[:, ac * D:(ac + 1) * D],
                                ps[:, uslot:uslot + 1024], ALU.subtract).then_inc(vev.h, 1)
            vev.add(1)
            u_evac.append(vev.v)
            need_vev_w = vev.v

            @block.scalar
            def _(sc, ac=ac, need_vev_w=need_vev_w):
                sc.wait_ge(vev.h, need_vev_w)
                sc.activation(wcur_bf[:, ac * D:(ac + 1) * D],
                              wcur[:, ac * D:(ac + 1) * D], AF.Copy).then_inc(wup.h, 1)
            wup.add(1)
        st8["u_vev"] = vev.v
        st8["u45_vev"] = u_evac[-1]

        bank6_free_e = ev.v

        for half in range(2):
            @block.tensor
            def _(te, half=half, gp_done=gp_done, masks_done=masks_done,
                  bank6_free_e=bank6_free_e, vev_snap=vev.v):
                te.wait_ge(ev.h, bank6_free_e)
                te.wait_ge(vev.h, vev_snap if half == 1 else masks_done)
                te.matmul(ps[0:1, 3072:3584], c.g.ones_col[:, :],
                          gpb[:, half * 512:half * 512 + 512],
                          start=True, stop=True).then_inc(mm.h, 1)
            mm.add(1)
            need_mm_bc = mm.v

            @block.vector
            def _(v, half=half, need_mm_bc=need_mm_bc):
                v.wait_ge(mm.h, need_mm_bc)
                v.tensor_tensor(bcur_f[0:1, half * 512:half * 512 + 512],
                                bcur_f[0:1, half * 512:half * 512 + 512],
                                ps[0:1, 3072:3584], ALU.subtract).then_inc(vev.h, 1)
            vev.add(1)
        st8["bcur_vev"] = vev.v
        need_vev_bc = vev.v

        @block.scalar
        def _(sc, need_vev_bc=need_vev_bc, bcur_f=bcur_f, bcur_b=bcur_b):
            sc.wait_ge(vev.h, need_vev_bc)
            sc.activation(bcur_b[:], bcur_f[:], AF.Copy).then_inc(wup.h, 1)
        wup.add(1)

        st8[("wup_done", s)] = wup.v
        st8["A_mm"][gch] = mm.v
        st8.setdefault("gate_done", {})[gch] = x2s.v

        if next_front is not None:
            emit_front(next_front)

    for s in range(SEQ):
        @block.sync
        def _(sy, s=s):
            wcp.wait_all(sy)
            wcp.issue(sy.dma_start(out=wcur2[:, s * NCH * D:(s + 1) * NCH * D],
                                   in_=c.w0_t))
            wcp.issue(sy.dma_start(out=bcurf2[:, s * D:(s + 1) * D], in_=c.b0))
        w_slot = wcp.slot(wcp.issued - 1)

        @block.scalar
        def _(sc, s=s, w_slot=w_slot):
            sc.wait_ge(w_slot[0].h, w_slot[1])
            for dc in range(8):
                sc.activation(wcurb2[:, s * NCH * D + dc * D:s * NCH * D + (dc + 1) * D],
                              wcur2[:, s * NCH * D + dc * D:s * NCH * D + (dc + 1) * D],
                              AF.Copy)
            sc.activation(bcurb2[:, s * D:(s + 1) * D],
                          bcurf2[:, s * D:(s + 1) * D], AF.Copy).then_inc(wup.h, 1)
        wup.add(1)
        st8[("wup_done", s)] = wup.v

    emit_front(0)
    emit_front(1)
    for gch in range(2 * NT_SEQ):
        nxt = gch + 2 if gch + 2 < 2 * NT_SEQ else None
        emit_main(gch, nxt)

    @block.sync
    def _(sy):
        outp.wait_all(sy)


t1_state = {"xn_done": {}, "gate_done": {}, "tm_free": {}}


def build_program(phases=ALL_PHASES, dbg=()):
    nc = bass.Bass("TRN2", target_bir_lowering=False)
    c = declare_io(nc, dbg)
    with contextlib.ExitStack() as stack:
        sems = {"_stack": stack}
        for name in [
            "ph", "wcast0", "wcast1",
            "p0_ld", "p0_cast", "p0_tr", "p0_ev", "p0_st",
            "a_wld", "a_xld", "a_sq", "a_col", "a_rstd", "a_xn", "a_tr",
            "a_xnf", "a_xnst", "a_mm", "a_kev", "a_vev", "a_kvev", "a_kvst",
            "f_mm", "f_gps", "f_gs2", "f_wtv", "f_b2", "f_hev",
            "f_yac", "f_sil",
            "b_mm", "b_qev", "b_zev", "b_aev", "b_x1",
            "t_mm", "t_ev", "t_vev", "t_wup", "t_x2",
        ]:
            sems[name] = S(stack.enter_context(nc.semaphore(name)))

        sems["ch"] = {
            "v": Chain(stack.enter_context(nc.semaphore("chv"))),
            "s": Chain(stack.enter_context(nc.semaphore("chs"))),
            "p": Chain(stack.enter_context(nc.semaphore("chp"))),
            "t": Chain(stack.enter_context(nc.semaphore("cht"))),
        }
        c.g = Ctx()
        c.g.id_bf = stack.enter_context(nc.sbuf_tensor("g_id_bf", [128, 128], BF16))
        c.g.id_f32 = stack.enter_context(nc.sbuf_tensor("g_id_f32", [128, 128], F32))
        c.g.ones_row = stack.enter_context(nc.sbuf_tensor("g_ones_row", [1, 128], BF16))
        c.g.ones_col = stack.enter_context(nc.sbuf_tensor("g_ones_col", [128, 1], BF16))
        c.g.mask_sl = stack.enter_context(nc.sbuf_tensor("g_mask_sl", [128, 128], F32))
        c.g.mask_su = stack.enter_context(nc.sbuf_tensor("g_mask_su", [128, 128], F32))
        c.g.mask_iun = stack.enter_context(nc.sbuf_tensor("g_mask_iun", [128, 128], F32))
        c.g.ksum_bf = stack.enter_context(nc.sbuf_tensor("g_ksum", [128, 16], BF16))

        block = stack.enter_context(nc.Block())

        cld = S(stack.enter_context(nc.semaphore("cld")))

        @block.sync
        def _(s):
            for dst, src in [(c.g.id_bf, c.id_bf), (c.g.id_f32, c.id_f32),
                             (c.g.ones_row, c.ones_row), (c.g.ones_col, c.ones_col),
                             (c.g.mask_sl, c.mask_sl), (c.g.mask_su, c.mask_su),
                             (c.g.mask_iun, c.mask_iun)]:
                s.dma_start(out=dst[:], in_=src).then_inc(cld.h, 16)
                cld.add(16)
        for name in ("sync", "tensor", "vector", "scalar", "gpsimd"):
            sec = getattr(block, name)

            @sec
            def _(eng):
                eng.wait_ge(cld.h, cld.v)

        for ph_name in phases:
            with contextlib.ExitStack() as pst:
                EMITTERS[ph_name](nc, block, c, sems, pst)
                barrier(nc, block, sems["ph"])
    return nc, c


def make_in_maps(inputs):
    consts = host_consts()
    x_full = np.ascontiguousarray(inputs["L0_attn_x"])
    base = {
        "w_qkv": inputs["L0_attn_W_qkv"], "w_out": inputs["L0_attn_W_out"],
        "b_out": np.ascontiguousarray(inputs["L0_attn_b_out"]).reshape(1, D),
        "gate_w": inputs["L0_ffn_gate_w"],
        "m0w1": inputs["L0_ffn_w1"], "m0b1": inputs["L0_ffn_b1"],
        "m0w2": inputs["L0_ffn_w2"], "m0b2": inputs["L0_ffn_b2"],
        "thk": inputs["L1_attn_theta_K"], "thv": inputs["L1_attn_theta_V"],
        "thq": inputs["L1_attn_theta_Q"], "w0": inputs["L1_attn_W0"],
        "b0": np.ascontiguousarray(inputs["L1_attn_b0"]).reshape(1, D),
        "wg": inputs["L1_ffn_Wg"],
        "m1w1": inputs["L1_ffn_W1"], "m1b1": inputs["L1_ffn_b1"],
        "m1w2": inputs["L1_ffn_W2"], "m1b2": inputs["L1_ffn_b2"],
    }
    base.update(consts)
    in_maps = []
    for cid in range(NCORE):
        m = dict(base)
        m["x"] = x_full[cid * SEQ:(cid + 1) * SEQ]
        in_maps.append(m)
    return in_maps


def kernel(**inputs):
    nc, _ = build_program()
    in_maps = make_in_maps(inputs)
    res = run_bass_kernel_spmd(nc, in_maps, core_ids=list(range(NCORE)))
    outs = [np.asarray(res.results[cid]["out"], dtype=np.float32) for cid in range(NCORE)]
    return np.concatenate(outs, axis=0)


EMITTERS = {"p0": emit_p0, "a0": emit_a0, "f0": emit_f0, "b0": emit_b0,
            "f1": emit_f1, "t1": emit_t1}
